# version 08
# baseline (speedup 1.0000x reference)
import os
import numpy as np
import concourse.bass as bass
import concourse.mybir as mybir
from concourse.bass_utils import run_bass_kernel_spmd

F32 = mybir.dt.float32
BF16 = mybir.dt.bfloat16
FP8 = mybir.dt.float8e4
I32 = mybir.dt.int32
AF = mybir.ActivationFunctionType
ALU = mybir.AluOpType
AX = mybir.AxisListType
ts = bass.ts

B, L, D, H = 4, 8192, 1024, 4
DV = 256
OWN = 4096
C = 64
NCH = L // C
GRP = 512
NG = L // GRP
OG = 256
NOG = OWN // OG
XHALO = 128
MAGIC_HI = float(0x5F3759DF >> 9)
NH = 4

DEBUG = bool(int(os.environ.get("KBG_DEBUG", "0")))
NG_LIM = int(os.environ.get("KBG_NG", str(NG)))
NOG_LIM = int(os.environ.get("KBG_NOG", str(NOG)))


def build_program():
    nc = bass.Bass("TRN2", target_bir_lowering=False)

    d = {}
    def din(name, shape, dt=F32):
        d[name] = nc.dram_tensor(name, shape, dt, kind="ExternalInput").ap()
        return d[name]

    x_seq = din("x_seq", [L, D])
    x_own = din("x_own", [OWN + XHALO, D])
    wqT = din("wqT", [D, D]); wkT = din("wkT", [D, D]); wbT = din("wbT", [D, 4])
    wvT = din("wvT", [D, D])
    w1T = din("w1T", [1056, 2048]); w2T = din("w2T", [2048, 16])
    woT = din("woT", [D, D]); wfT = din("wfT", [D, D])
    qcw = din("qcw", [128, 32]); kcw = din("kcw", [128, 32]); vcw = din("vcw", [128, 32])
    firs = din("firs", [128, 24]); firl = din("firl", [128, 504])
    gb1 = din("gb1", [128, 16]); bb = din("bb", [16, 1])
    ltemp = din("ltemp", [1, 4]); fraw = din("fraw", [1, 16])
    onw4 = din("onw4", [128, 8]); fbias = din("fbias", [D])

    out_d = nc.dram_tensor("out", [OWN, D], F32, kind="ExternalOutput").ap()
    dbg = {}
    if DEBUG:
        for nm, shp in [("xfm", [128, 4096]), ("qpre", [128, 516]),
                        ("qfm", [128, 4096]), ("kn0", [C, 1024]),
                        ("uw0", [C, 2048]), ("o0", [C, 1024]),
                        ("pths", [128, 4096]), ("probs", [128, 32]),
                        ("stat", [128, 64]), ("x1", [128, D]), ("amax", [128, 4])]:
            dbg[nm] = nc.dram_tensor("dbg_" + nm, shp, F32, kind="ExternalOutput").ap()

    x1_spill = nc.dram_tensor("x1_spill", [OWN, D], F32, kind="Internal").ap()
    o_loc = nc.dram_tensor("o_loc", [L, D], BF16, kind="Internal").ap()
    o_stage = nc.dram_tensor("o_stage", [OWN, D], BF16, kind="Internal").ap()
    v_loc = nc.dram_tensor("v_loc", [D, 64 + L], BF16, kind="Internal").ap()
    v_stage = nc.dram_tensor("v_stage", [D, 64 + OWN], BF16, kind="Internal").ap()
    loc_sc = nc.dram_tensor("loc_sc", [1, 32], F32, kind="Internal").ap()

    from contextlib import ExitStack
    with ExitStack() as ctx:
        def sb(name, shape, dt):
            return ctx.enter_context(nc.sbuf_tensor(name, shape, dt))
        def pst(name, shape, dt=F32):
            return ctx.enter_context(nc.psum_tensor(name, shape, dt))
        def sem(name):
            return ctx.enter_context(nc.semaphore(name))

        wbig_h = sb("wbig", [128, 24576], BF16)
        wbig = wbig_h[:, :]
        wq_sb = wbig[:, 0:8192]
        wk_sb = wbig[:, 8192:16384]
        wv_sb = wbig[:, 16384:24576]
        w1_sb = wbig[:, 0:18432]
        fir_a = wbig[:, 18432:19456].bitcast(F32)
        fir_b = wbig[:, 19456:20480].bitcast(F32)
        fir_o = wbig[:, 20480:20992]
        stat_tm = wbig[:, 20992:21248].bitcast(F32)
        stat_fm = wbig[:, 21248:21760]
        hmid = wbig[:, 21760:22784]
        logit_t = wbig[:, 22784:22912].bitcast(F32)
        prob_t = wbig[:, 22912:23040].bitcast(F32)
        sm_a = wbig[:, 23040:23168].bitcast(F32)
        sm_b = wbig[:, 23168:23360].bitcast(F32)
        xs_fm8 = wbig[:, 23360:23872].bitcast(FP8)

        fmw_h = sb("fmw", [128, 18944], BF16)
        fmw = fmw_h[:, :]
        qfm = fmw[:, 0:4096]
        kfm = fmw[:, 4096:8192]
        vfm = fmw[:, 8192:12288]
        wstage = fmw[:, 0:4096].bitcast(F32)
        wo_sb = fmw[:, 0:8192]
        vow = fmw[:, 8192:10752]
        pth_s = fmw[:, 10752:12800]
        pth_v = fmw[:, 12800:14848]
        pth_l = fmw[:, 14848:16896]
        pth_d = fmw[:, 16896:18944]
        wf8_sb = fmw[:, 4096:8192].bitcast(FP8)

        diag_h = sb("diag_u", [128, 12288], BF16)
        diag_u = diag_h[:, :]
        qdiag = diag_u[:, 0:4096]
        kdiag = diag_u[:, 4096:8192]
        vdiag = diag_u[:, 8192:12288]
        comb_a = diag_u[:, 0:1024]
        comb_b = diag_u[:, 1024:2048]
        orms = diag_u[:, 2048:3072]
        ofm = diag_u[:, 3072:4096]
        fbias_sb = diag_u[:, 4096:5120]
        ffn_t = comb_a
        xs_b = orms
        xs_fmb = ofm

        wstage2 = fmw[:, 16896:18944].bitcast(F32)
        logit_f = fir_a[0:16, :]

        cw_q = sb("cw_q", [128, 32], F32)
        cw_k = sb("cw_k", [128, 32], F32)
        cw_v = sb("cw_v", [128, 32], F32)
        firs_sb = sb("firs_sb", [128, 24], F32)
        firl_sb = sb("firl_sb", [128, 504], F32)
        ident4_b = sb("ident4_b", [C, 4 * C], BF16)

        xfm = sb("xfm", [128, 4096], BF16)
        xt_f = sb("xt_f", [128, D], F32)
        xt_b = sb("xt_b", [128, D], BF16)

        pre_q = sb("pre_q", [128, GRP + NH], BF16)
        pre_k = sb("pre_k", [128, GRP + NH], BF16)
        pre_v = sb("pre_v", [128, GRP + NH], BF16)
        halo_q = sb("halo_q", [128, 32], BF16)
        halo_k = sb("halo_k", [128, 32], BF16)
        halo_v = sb("halo_v", [128, 32], BF16)

        qsq = sb("qsq", [128, 512], BF16)
        kn_tm = sb("kn_tm", [C, 4 * DV], BF16)
        u_tm = sb("u_tm", [C, 4 * DV], BF16)
        ui_tm = sb("ui_tm", [C, 4 * DV], BF16)
        w_fm = sb("w_fm", [128, 2 * C], BF16)
        attnT = sb("attnT", [C, C], BF16)
        Lt = sb("Lt", [C, 4 * C], BF16)
        LtT = sb("LtT", [C, 4 * C], BF16)
        Xp = sb("Xp", [C, 4 * C], BF16)
        XpT = sb("XpT", [C, 4 * C], BF16)
        PT = sb("PT", [C, 4 * C], BF16)
        PT2 = sb("PT2", [C, 4 * C], BF16)
        S_sb = sb("S_sb", [128, 4 * 512], BF16)
        bvbk = sb("bvbk", [C, 4 * 512], BF16)
        o_tile = sb("o_tile", [C, D], BF16)
        k_tm = sb("k_tm", [C, D], BF16)
        v_tm = sb("v_tm", [C, D], BF16)
        beta_tm = sb("beta_tm", [C, 8], F32)
        beta_g = sb("beta_g", [4, GRP], F32)
        rs_sc = sb("rs_sc", [128, 16], F32)
        rs_a = sb("rs_a", [128, 16], F32)
        rs_b = sb("rs_b", [128, 16], F32)
        rs_i = sb("rs_i", [128, 16], I32)

        ident_b = sb("ident_b", [128, 128], BF16)
        ident_f = sb("ident_f", [128, 128], F32)
        ones_b = sb("ones_b", [128, 128], BF16)
        mask_low = sb("mask_low", [C, C], BF16)
        mask_upc = sb("mask_upc", [C, C], F32)
        w2_sb = sb("w2_sb", [128, 16 * 16], BF16)
        wb_sb = sb("wb_sb", [128, 32], BF16)
        gb1_sb = sb("gb1_sb", [128, 16], F32)
        bb_sb = sb("bb_sb", [16, 1], F32)
        itemp_sb = sb("itemp_sb", [128, 32], F32)
        floor_sb = sb("floor_sb", [128, 32], F32)
        onw_sb = sb("onw_sb", [128, 8], F32)
        ws2_bc = sb("ws2_bc", [128, 1], F32)
        sc_bc = sb("sc_bc", [128, 2], F32)
        sA = sb("sA", [128, 128], F32)
        sB = sb("sB", [128, 128], F32)
        tstage = sb("tstage", [128, 2 * GRP], F32)
        y_t = sb("y_t", [128, D], F32)
        x1_t = sb("x1_t", [128, D], F32)
        out_t = x1_t
        ksq = y_t[0:C, 0:D]
        ln_sc = sb("ln_sc", [128, 16], F32)
        amax_run = sb("amax_run", [128, 4], F32)
        zero_t = sb("zero_t", [128, 64], BF16)

        pA = pst("pA", [128, 512])
        pB = pst("pB", [128, 512])
        pC_ = pst("pC", [128, 512])
        pD_ = pst("pD", [128, 512])
        pS = [pst("pS%d" % i, [128, 512]) for i in range(4)]
        pT_f = pB[:, 384:512]
        pT_b = pB[:, 320:384].bitcast(BF16)

        s_l0 = sem("s_l0")
        s_l1 = sem("s_l1")
        s_d0 = sem("s_d0")
        s_d1 = sem("s_d1")
        s_t = sem("s_t")
        s_v = sem("s_v")
        s_a = sem("s_a")
        s_g = sem("s_g")
        s_gd = sem("s_gd")
        s_st = sem("s_st")
        s_dbg = sem("s_dbg")

        state = {}

        def I(name, fn, *incs):
            for s_, amt in incs:
                state["cnt"][s_.name] = state["cnt"].get(s_.name, 0) + amt
            if state["cur"] == name:
                ins = fn(state["eng"])
                for s_, amt in incs:
                    ins.then_inc(s_, amt)

        def W(name, s_):
            v = state["cnt"].get(s_.name, 0)
            if state["cur"] == name and v > 0:
                state["eng"].wait_ge(s_, v)

        def DR(name):
            if state["cur"] == name:
                state["eng"].drain()

        def emit(pass_name, eng):
            state["cur"], state["eng"], state["cnt"] = pass_name, eng, {}
            state["rrL"], state["rrD"] = 0, 0
            sy = "sync"; te = "tensor"; ve = "vector"; sc = "scalar"; gp = "gpsimd"

            def ld(dst, src, q=sy):
                s_ = (s_l0, s_l1)[state["rrL"]]
                state["rrL"] ^= 1
                W(q, s_)
                I(q, lambda e: e.dma_start(out=dst, in_=src), (s_, 16))

            def dmat(dst, src):
                s_ = (s_d0, s_d1)[state["rrD"]]
                state["rrD"] ^= 1
                W(sc, s_)
                I(sc, lambda e: e.dma_start_transpose(out=dst, in_=src), (s_, 16))

            def WL(name):
                W(name, s_l0)
                W(name, s_l1)

            def WD(name):
                W(name, s_d0)
                W(name, s_d1)

            def vop_(fn, *incs):
                I(ve, fn, *(incs or ((s_v, 1),)))

            def newton4(xcols, ycols, n=4):
                x0, x1_ = xcols
                y0, y1_ = ycols
                vop_(lambda e: e.tensor_scalar(rs_i[:, 0:n], rs_a[:, x0:x1_].bitcast(I32),
                                               10, None, ALU.logical_shift_right))
                DR(ve)
                vop_(lambda e: e.tensor_copy(rs_sc[:, 0:n], rs_i[:, 0:n]))
                DR(ve)
                vop_(lambda e: e.tensor_scalar(rs_sc[:, 4:4 + n], rs_sc[:, 0:n], MAGIC_HI,
                                               -1.0, ALU.subtract, ALU.mult))
                DR(ve)
                vop_(lambda e: e.tensor_copy(rs_i[:, 4:4 + n], rs_sc[:, 4:4 + n]))
                DR(ve)
                vop_(lambda e: e.tensor_scalar(rs_i[:, 8:8 + n], rs_i[:, 4:4 + n], 9, None,
                                               ALU.logical_shift_left))
                DR(ve)
                vop_(lambda e: e.tensor_copy(rs_b[:, y0:y1_], rs_i[:, 8:8 + n].bitcast(F32)))
                DR(ve)
                for it in range(3):
                    vop_(lambda e: e.tensor_tensor(rs_sc[:, 0:n], rs_b[:, y0:y1_],
                                                   rs_b[:, y0:y1_], ALU.mult))
                    DR(ve)
                    vop_(lambda e: e.tensor_tensor(rs_sc[:, 4:4 + n], rs_sc[:, 0:n],
                                                   rs_a[:, x0:x1_], ALU.mult))
                    DR(ve)
                    vop_(lambda e: e.tensor_scalar(rs_sc[:, 8:8 + n], rs_sc[:, 4:4 + n],
                                                   -0.5, 1.5, ALU.mult, ALU.add))
                    DR(ve)
                    vop_(lambda e: e.tensor_tensor(rs_sc[:, 12:12 + n], rs_b[:, y0:y1_],
                                                   rs_sc[:, 8:8 + n], ALU.mult))
                    DR(ve)
                    vop_(lambda e: e.tensor_copy(rs_b[:, y0:y1_], rs_sc[:, 12:12 + n]))
                    DR(ve)

            I(gp, lambda e: e.memset(ident_b[:], 0.0), (s_g, 1))
            DR(gp)
            I(gp, lambda e: e.affine_select(out=ident_b[:], in_=ident_b[:],
                                            compare_op=ALU.not_equal, fill=1.0, base=0,
                                            pattern=[[-1, 128]], channel_multiplier=1), (s_g, 1))
            I(gp, lambda e: e.memset(ident_f[:], 0.0), (s_g, 1))
            DR(gp)
            I(gp, lambda e: e.affine_select(out=ident_f[:], in_=ident_f[:],
                                            compare_op=ALU.not_equal, fill=1.0, base=0,
                                            pattern=[[-1, 128]], channel_multiplier=1), (s_g, 1))
            I(gp, lambda e: e.memset(ones_b[:], 1.0), (s_g, 1))
            I(gp, lambda e: e.memset(mask_low[:], 1.0), (s_g, 1))
            DR(gp)
            I(gp, lambda e: e.affine_select(out=mask_low[:], in_=mask_low[:],
                                            compare_op=ALU.is_gt, fill=0.0, base=0,
                                            pattern=[[-1, C]], channel_multiplier=1), (s_g, 1))
            I(gp, lambda e: e.memset(mask_upc[:], 1.0), (s_g, 1))
            DR(gp)
            I(gp, lambda e: e.affine_select(out=mask_upc[:], in_=mask_upc[:],
                                            compare_op=ALU.is_ge, fill=0.0, base=0,
                                            pattern=[[1, C]], channel_multiplier=-1), (s_g, 1))

            def load_w(dst, src, nkk, ncols, kk0=0):
                for kk in range(nkk):
                    ld(wstage[:, 0:ncols], src[kk * 128:(kk + 1) * 128, :])
                    WL(ve)
                    vop_(lambda e, kk=kk: e.tensor_copy(
                        dst[:, (kk0 + kk) * ncols:(kk0 + kk + 1) * ncols], wstage[:, 0:ncols]))
                    W(sy, s_v)

            load_w(wq_sb, wqT, 8, 1024)
            load_w(wk_sb, wkT, 8, 1024)
            load_w(wv_sb, wvT, 8, 1024)
            load_w(wb_sb, wbT, 8, 4)
            load_w(w2_sb, w2T, 16, 16)

            for kk in range(8):
                ld(wstage[:, 0:1024], wfT[kk * 128:(kk + 1) * 128, :])
                WL(ve)
                if kk == 0:
                    vop_(lambda e: e.tensor_reduce(sA[:, 0:1], wstage[:, 0:1024], AX.X,
                                                   ALU.max, apply_absolute_value=True))
                else:
                    vop_(lambda e: e.tensor_reduce(sA[:, 1:2], wstage[:, 0:1024], AX.X,
                                                   ALU.max, apply_absolute_value=True))
                    DR(ve)
                    vop_(lambda e: e.tensor_tensor(sA[:, 2:3], sA[:, 0:1], sA[:, 1:2], ALU.max))
                    DR(ve)
                    vop_(lambda e: e.tensor_copy(sA[:, 0:1], sA[:, 2:3]))
                    DR(ve)
                W(sy, s_v)
            W(te, s_v)
            W(te, s_g)
            I(te, lambda e: e.transpose(pT_f[0:1, 0:128], sA[:, 0:1], ident_f[:]), (s_t, 1))
            W(ve, s_t)
            vop_(lambda e: e.tensor_reduce(sA[0:1, 3:4], pT_f[0:1, 0:128], AX.X, ALU.max))
            DR(ve)
            vop_(lambda e: e.tensor_scalar(sA[0:1, 4:5], sA[0:1, 3:4], 1e-12, None, ALU.max))
            DR(ve)
            vop_(lambda e: e.reciprocal(sA[0:1, 5:6], sA[0:1, 4:5]))
            DR(ve)
            vop_(lambda e: e.tensor_scalar(sA[0:1, 6:7], sA[0:1, 5:6], 448.0, 448.0, ALU.mult, ALU.min))
            DR(ve)
            vop_(lambda e: e.reciprocal(sA[0:1, 7:8], sA[0:1, 6:7]))
            DR(ve)
            vop_(lambda e: e.tensor_scalar(sA[0:1, 8:9], sA[0:1, 6:7], 0.5, None, ALU.mult))
            DR(ve)
            W(sy, s_v)
            ld(loc_sc[0:1, 0:9], sA[0:1, 0:9])
            WL(sy)
            ld(ws2_bc[:], loc_sc[0:1, 8:9].broadcast_to((128, 1)))

            ld(gb1_sb[:, 0:16], gb1[:])
            ld(bb_sb[:, 0:1], bb[:])
            ld(onw_sb[:, 0:8], onw4[:])
            ld(sA[0:1, 16:20], ltemp[:])
            ld(sA[0:1, 20:36], fraw[:])

            ld(cw_q[:, 0:32], qcw[:])
            ld(cw_k[:, 0:32], kcw[:])
            ld(cw_v[:, 0:32], vcw[:])
            ld(firs_sb[:, 0:24], firs[:])
            ld(firl_sb[:, 0:504], firl[:])

            WL(sc)
            I(sc, lambda e: e.activation(sB[0:1, 0:4], sA[0:1, 16:20], AF.Abs), (s_a, 1))
            DR(sc)
            I(sc, lambda e: e.activation(sB[0:1, 4:8], sB[0:1, 0:4], AF.Exp, scale=-1.0), (s_a, 1))
            DR(sc)
            I(sc, lambda e: e.activation(sB[0:1, 8:12], sB[0:1, 4:8], AF.Ln, bias=1.0), (s_a, 1))
            DR(sc)
            W(ve, s_a)
            vop_(lambda e: e.tensor_scalar(sB[0:1, 12:16], sA[0:1, 16:20], 0.0, None, ALU.max))
            DR(ve)
            vop_(lambda e: e.tensor_tensor(sB[0:1, 16:20], sB[0:1, 8:12], sB[0:1, 12:16], ALU.add))
            DR(ve)
            vop_(lambda e: e.tensor_scalar(sB[0:1, 20:24], sB[0:1, 16:20], 1e-4, None, ALU.add))
            DR(ve)
            vop_(lambda e: e.reciprocal(sB[0:1, 24:28], sB[0:1, 20:24]))
            DR(ve)
            W(sc, s_v)
            I(sc, lambda e: e.activation(sB[0:1, 32:48], sA[0:1, 20:36], AF.Exp, scale=-1.0), (s_a, 1))
            DR(sc)
            W(ve, s_a)
            vop_(lambda e: e.tensor_scalar(sB[0:1, 48:64], sB[0:1, 32:48], 1.0, None, ALU.add))
            DR(ve)
            vop_(lambda e: e.reciprocal(sB[0:1, 64:80], sB[0:1, 48:64]))
            DR(ve)
            vop_(lambda e: e.tensor_scalar(sB[0:1, 80:96], sB[0:1, 64:80], 0.05, None, ALU.mult))
            DR(ve)
            for hh in range(4):
                for pp_ in range(4):
                    vop_(lambda e, hh=hh, pp_=pp_: e.tensor_copy(
                        sB[0:1, 96 + hh * 4 + pp_: 97 + hh * 4 + pp_], sB[0:1, 24 + hh:25 + hh]))
            DR(ve)
            W(sy, s_v)
            ld(loc_sc[0:1, 0:16], sB[0:1, 96:112])
            WL(sy)
            ld(itemp_sb[:, 0:16], loc_sc[0:1, 0:16].broadcast_to((128, 16)))
            ld(loc_sc[0:1, 16:32], sB[0:1, 80:96])
            WL(sy)
            ld(floor_sb[:, 0:16], loc_sc[0:1, 16:32].broadcast_to((128, 16)))
            WL(ve)
            vop_(lambda e: e.tensor_copy(itemp_sb[:, 16:32], itemp_sb[:, 0:16]))
            vop_(lambda e: e.tensor_copy(floor_sb[:, 16:32], floor_sb[:, 0:16]))

            def build_diags(dst, wsrc):
                W(ve, s_g)
                for rb in range(8):
                    for t in range(4):
                        cc0 = (rb * 4 + t) * 128
                        vop_(lambda e, cc0=cc0, rb=rb, t=t, wsrc=wsrc, dst=dst: e.tensor_scalar(
                            dst[:, cc0:cc0 + 128], ones_b[:, 0:128],
                            wsrc[:, rb * 4 + t: rb * 4 + t + 1],
                            None, ALU.mult))
                W(gp, s_v)
                I(gp, lambda e, dst=dst: e.affine_select(
                    out=dst[:, 0:32 * 128], in_=dst[:, 0:32 * 128],
                    compare_op=ALU.is_equal, fill=0.0, base=0,
                    pattern=[[0, 32], [-1, 128]], channel_multiplier=1), (s_g, 1))

            WL(ve)
            build_diags(qdiag, cw_q)
            build_diags(kdiag, cw_k)
            build_diags(vdiag, cw_v)

            for hh in range(4):
                vop_(lambda e, hh=hh: e.tensor_copy(
                    ident4_b[:, hh * C:(hh + 1) * C], ident_b[0:C, 0:C]))

            vop_(lambda e: e.memset(halo_q[:], 0.0))
            vop_(lambda e: e.memset(halo_k[:], 0.0))
            vop_(lambda e: e.memset(halo_v[:], 0.0))
            vop_(lambda e: e.memset(amax_run[:], 0.0))
            vop_(lambda e: e.memset(zero_t[:], 0.0))
            W(sy, s_v)
            for rb in range(8):
                ld(v_loc[rb * 128:(rb + 1) * 128, 0:64], zero_t[:])

            W(sy, s_g)
            W(sy, s_v)
            W(sy, s_a)
            W(te, s_g)
            W(ve, s_g)
            W(sc, s_g)
            W(gp, s_g)

            for g in range(NG_LIM):
                t0 = g * GRP
                WL(ve)
                for ti in range(4):
                    W(sy, s_v)
                    ld(xt_f[:], x_seq[t0 + ti * 128: t0 + (ti + 1) * 128, :])
                    WL(ve)
                    WD(ve)
                    vop_(lambda e: e.tensor_copy(xt_b[:, 0:D], xt_f[:]))
                    W(sc, s_v)
                    W(sc, s_t)
                    for kk in range(8):
                        dmat(xfm[:, kk * GRP + ti * 128: kk * GRP + (ti + 1) * 128],
                             xt_b[:, kk * 128:(kk + 1) * 128])
                WD(te)

                def proj_conv(pre, halo, w_sb, diag, dfm):
                    for rb in range(8):
                        pp = pA if (rb % 2 == 0) else pB
                        for kk in range(8):
                            I(te, lambda e, pp=pp, kk=kk, rb=rb, w_sb=w_sb: e.matmul(
                                pp[:, 0:GRP],
                                w_sb[:, kk * 1024 + rb * 128: kk * 1024 + rb * 128 + 128],
                                xfm[:, kk * GRP:(kk + 1) * GRP],
                                start=(kk == 0), stop=(kk == 7)), (s_t, 1))
                        W(ve, s_t)
                        DR(ve)
                        vop_(lambda e, pre=pre, halo=halo, rb=rb: e.tensor_copy(
                            pre[:, 0:NH], halo[:, rb * 4:(rb + 1) * 4]))
                        DR(ve)
                        vop_(lambda e, pre=pre, pp=pp: e.tensor_copy(
                            pre[:, NH:NH + GRP], pp[:, 0:GRP]))
                        DR(ve)
                        vop_(lambda e, pre=pre, halo=halo, rb=rb: e.tensor_copy(
                            halo[:, rb * 4:(rb + 1) * 4], pre[:, GRP:GRP + NH]))
                        W(te, s_v)
                        ppc = pC_ if (rb % 2 == 0) else pD_
                        for t in range(4):
                            I(te, lambda e, ppc=ppc, diag=diag, rb=rb, t=t, pre=pre:
                              e.matmul(ppc[:, 0:GRP],
                                       diag[:, (rb * 4 + t) * 128:(rb * 4 + t) * 128 + 128],
                                       pre[:, 1 + t: 1 + t + GRP],
                                       start=(t == 0), stop=(t == 3)), (s_t, 1))
                        W(sc, s_t)
                        I(sc, lambda e, ppc=ppc: e.activation(tstage[:, 0:GRP], ppc[:, 0:GRP],
                                                              AF.Tanh, scale=0.5), (s_a, 1))
                        W(ve, s_a)
                        vop_(lambda e: e.tensor_scalar(tstage[:, GRP:2 * GRP], tstage[:, 0:GRP],
                                                       0.5, 0.5, ALU.mult, ALU.add))
                        DR(ve)
                        W(ve, s_t)
                        vop_(lambda e, dfm=dfm, rb=rb, ppc=ppc: e.tensor_tensor(
                            dfm[:, rb * GRP:(rb + 1) * GRP], tstage[:, GRP:2 * GRP],
                            ppc[:, 0:GRP], ALU.mult))
                        W(te, s_v)
                        W(sc, s_v)

                proj_conv(pre_q, halo_q, wq_sb, qdiag, qfm)
                proj_conv(pre_k, halo_k, wk_sb, kdiag, kfm)
                proj_conv(pre_v, halo_v, wv_sb, vdiag, vfm)

                W(sy, s_v)
                for rb in range(8):
                    ld(v_loc[rb * 128:(rb + 1) * 128, 64 + t0: 64 + t0 + GRP],
                       vfm[:, rb * GRP:(rb + 1) * GRP])

                for kk in range(8):
                    I(te, lambda e, kk=kk: e.matmul(
                        pC_[0:4, 0:GRP], wb_sb[:, kk * 4:kk * 4 + 4],
                        xfm[:, kk * GRP:(kk + 1) * GRP],
                        start=(kk == 0), stop=(kk == 7)), (s_t, 1))
                W(sc, s_t)
                I(sc, lambda e: e.activation(tstage[0:4, 0:GRP], pC_[0:4, 0:GRP],
                                             AF.Tanh, scale=0.5), (s_a, 1))
                W(ve, s_a)
                W(ve, s_t)
                vop_(lambda e: e.tensor_scalar(beta_g[0:4, 0:GRP],
                                               tstage[0:4, 0:GRP], 0.5, 0.5,
                                               ALU.mult, ALU.add))
                W(te, s_v)

                if DEBUG and g == 0:
                    W(gp, s_v)
                    W(gp, s_a)
                    WD(gp)
                    W(gp, s_gd)
                    I(gp, lambda e: e.dma_start(out=dbg["xfm"][:], in_=xfm[:, 0:4096]), (s_gd, 16))
                    W(gp, s_gd)
                    I(gp, lambda e: e.dma_start(out=dbg["qpre"][:], in_=pre_q[:, 0:GRP + NH]), (s_gd, 16))
                    W(gp, s_gd)
                    I(gp, lambda e: e.dma_start(out=dbg["qfm"][:], in_=qfm[:, 0:4096]), (s_gd, 16))
                    W(sy, s_dbg)
                    W(ve, s_dbg)
                    W(sc, s_dbg)
                    W(te, s_dbg)
                    W(gp, s_dbg)
                    W(sy, s_gd)
                    W(ve, s_gd)
                    W(sc, s_gd)
                    W(te, s_gd)
                    W(gp, s_gd)

                for jj in range(GRP // C):
                    j = g * (GRP // C) + jj
                    cc = jj * C
                    W(te, s_v)
                    I(te, lambda e, cc=cc: e.transpose(
                        pT_f[0:C, 0:4], beta_g[0:4, cc:cc + C],
                        ident_f[0:4, 0:4]), (s_t, 1))
                    W(ve, s_t)
                    vop_(lambda e: e.tensor_copy(beta_tm[:, 0:4], pT_f[0:C, 0:4]))
                    DR(ve)
                    vop_(lambda e: e.tensor_scalar(beta_tm[:, 4:8], beta_tm[:, 0:4],
                                                   -1.0, None, ALU.mult))
                    for rbk in range(8):
                        W(te, s_v)
                        I(te, lambda e, rbk=rbk, cc=cc: e.transpose(
                            pT_b[0:C, 0:128], kfm[:, rbk * GRP + cc: rbk * GRP + cc + C],
                            ident_b[:]), (s_t, 1))
                        W(ve, s_t)
                        vop_(lambda e, rbk=rbk: e.tensor_copy(
                            k_tm[:, rbk * 128:(rbk + 1) * 128], pT_b[0:C, 0:128]))
                        W(te, s_v)
                        I(te, lambda e, rbk=rbk, cc=cc: e.transpose(
                            pT_b[0:C, 0:128], vfm[:, rbk * GRP + cc: rbk * GRP + cc + C],
                            ident_b[:]), (s_t, 1))
                        W(ve, s_t)
                        vop_(lambda e, rbk=rbk: e.tensor_copy(
                            v_tm[:, rbk * 128:(rbk + 1) * 128], pT_b[0:C, 0:128]))
                    DR(ve)
                    for rbk in range(8):
                        vop_(lambda e, rbk=rbk, cc=cc: e.tensor_tensor(
                            qsq[:, rbk * C:(rbk + 1) * C],
                            qfm[:, rbk * GRP + cc: rbk * GRP + cc + C],
                            qfm[:, rbk * GRP + cc: rbk * GRP + cc + C], ALU.mult))
                    W(te, s_v)
                    for hh in range(4):
                        for sub in range(2):
                            I(te, lambda e, hh=hh, sub=sub: e.matmul(
                                pC_[0:C, hh:hh + 1],
                                qsq[:, (hh * 2 + sub) * C:(hh * 2 + sub + 1) * C],
                                ones_b[:, 0:1], start=(sub == 0), stop=(sub == 1),
                                skip_group_check=True), (s_t, 1))
                    W(ve, s_t)
                    vop_(lambda e: e.tensor_copy(rs_sc[0:C, 0:4], pC_[0:C, 0:4]))
                    W(te, s_v)
                    vop_(lambda e: e.tensor_tensor(ksq[:, 0:D], k_tm[:, 0:D],
                                                   k_tm[:, 0:D], ALU.mult))
                    DR(ve)
                    vop_(lambda e: e.tensor_reduce(
                        rs_sc[0:C, 4:8], ksq[:, 0:D].rearrange("p (h dv) -> p h dv", h=4),
                        AX.X, ALU.add))
                    DR(ve)
                    vop_(lambda e: e.tensor_scalar(rs_a[0:C, 0:8], rs_sc[0:C, 0:8], 1e-6, None, ALU.add))
                    DR(ve)
                    vop_(lambda e: e.tensor_scalar(rs_i[0:C, 0:8], rs_a[0:C, 0:8].bitcast(I32),
                                                   10, None, ALU.logical_shift_right))
                    DR(ve)
                    vop_(lambda e: e.tensor_copy(rs_b[0:C, 0:8], rs_i[0:C, 0:8]))
                    DR(ve)
                    vop_(lambda e: e.tensor_scalar(rs_sc[0:C, 8:16], rs_b[0:C, 0:8], MAGIC_HI, -1.0,
                                                   ALU.subtract, ALU.mult))
                    DR(ve)
                    vop_(lambda e: e.tensor_copy(rs_i[0:C, 8:16], rs_sc[0:C, 8:16]))
                    DR(ve)
                    vop_(lambda e: e.tensor_scalar(rs_i[0:C, 0:8], rs_i[0:C, 8:16], 9, None,
                                                   ALU.logical_shift_left))
                    DR(ve)
                    vop_(lambda e: e.tensor_copy(rs_b[0:C, 0:8], rs_i[0:C, 0:8].bitcast(F32)))
                    DR(ve)
                    for it in range(3):
                        vop_(lambda e: e.tensor_tensor(rs_sc[0:C, 8:16], rs_b[0:C, 0:8],
                                                       rs_b[0:C, 0:8], ALU.mult))
                        DR(ve)
                        vop_(lambda e: e.tensor_tensor(rs_b[0:C, 8:16], rs_sc[0:C, 8:16],
                                                       rs_a[0:C, 0:8], ALU.mult))
                        DR(ve)
                        vop_(lambda e: e.tensor_scalar(rs_sc[0:C, 8:16], rs_b[0:C, 8:16],
                                                       -0.5, 1.5, ALU.mult, ALU.add))
                        DR(ve)
                        vop_(lambda e: e.tensor_tensor(rs_b[0:C, 8:16], rs_b[0:C, 0:8],
                                                       rs_sc[0:C, 8:16], ALU.mult))
                        DR(ve)
                        vop_(lambda e: e.tensor_copy(rs_b[0:C, 0:8], rs_b[0:C, 8:16]))
                        DR(ve)
                    for hh in range(4):
                        vop_(lambda e, hh=hh: e.tensor_scalar(
                            kn_tm[:, hh * DV:(hh + 1) * DV], k_tm[:, hh * DV:(hh + 1) * DV],
                            rs_b[0:C, 4 + hh:5 + hh], None, ALU.mult))
                    DR(ve)
                    for hh in range(4):
                        vop_(lambda e, hh=hh: e.tensor_scalar(
                            bvbk[:, hh * 512:hh * 512 + DV], v_tm[:, hh * DV:(hh + 1) * DV],
                            beta_tm[:, hh:hh + 1], None, ALU.mult))
                    DR(ve)
                    for hh in range(4):
                        vop_(lambda e, hh=hh: e.tensor_scalar(
                            bvbk[:, hh * 512 + DV:hh * 512 + 512],
                            kn_tm[:, hh * DV:(hh + 1) * DV],
                            beta_tm[:, hh:hh + 1], None, ALU.mult))
                    DR(ve)
                    W(te, s_v)
                    for rbk in range(8):
                        I(te, lambda e, rbk=rbk: e.transpose(
                            pT_b[0:128, 0:C],
                            kn_tm[:, rbk * 128:(rbk + 1) * 128],
                            ident_b[0:C, 0:C]), (s_t, 1))
                        W(ve, s_t)
                        vop_(lambda e, rbk=rbk: e.tensor_copy(
                            qsq[:, rbk * C:(rbk + 1) * C], pT_b[0:128, 0:C]))
                        W(te, s_v)
                    kn_fm = qsq
                    DR(ve)

                    if DEBUG and j == 0:
                        W(gp, s_v)
                        W(gp, s_gd)
                        I(gp, lambda e: e.dma_start(out=dbg["kn0"], in_=kn_tm[:]), (s_gd, 16))
                        W(sy, s_dbg)
                        W(ve, s_dbg)
                        W(sc, s_dbg)
                        W(te, s_dbg)
                        W(gp, s_dbg)
                        W(sy, s_gd)
                        W(ve, s_gd)
                        W(sc, s_gd)
                        W(te, s_gd)
                        W(gp, s_gd)

                    for hh in range(4):
                        for sub in range(2):
                            I(te, lambda e, hh=hh, sub=sub: e.matmul(
                                pC_[0:C, hh * C:(hh + 1) * C],
                                kn_fm[:, (hh * 2 + sub) * C:(hh * 2 + sub + 1) * C],
                                kn_fm[:, (hh * 2 + sub) * C:(hh * 2 + sub + 1) * C],
                                start=(sub == 0), stop=(sub == 1),
                                skip_group_check=True), (s_t, 1))
                    W(ve, s_t)
                    for hh in range(4):
                        vop_(lambda e, hh=hh: e.scalar_tensor_tensor(
                            Lt[:, hh * C:(hh + 1) * C], pC_[0:C, hh * C:(hh + 1) * C],
                            beta_tm[:, 4 + hh:5 + hh], mask_low[:], ALU.mult, ALU.mult))
                    W(te, s_v)
                    DR(ve)
                    for hh in range(4):
                        I(te, lambda e, hh=hh: e.transpose(
                            pT_b[0:C, 0:C], Lt[:, hh * C:(hh + 1) * C], ident_b[0:C, 0:C]), (s_t, 1))
                        W(ve, s_t)
                        vop_(lambda e, hh=hh: e.tensor_copy(LtT[:, hh * C:(hh + 1) * C], pT_b[0:C, 0:C]))
                        W(te, s_v)
                    DR(ve)
                    vop_(lambda e: e.tensor_tensor(
                        PT[:, 0:4 * C], ident4_b[:, 0:4 * C], LtT[:, 0:4 * C], ALU.add))
                    vop_(lambda e: e.tensor_copy(Xp[:, 0:4 * C], Lt[:, 0:4 * C]))
                    vop_(lambda e: e.tensor_copy(XpT[:, 0:4 * C], LtT[:, 0:4 * C]))
                    DR(ve)
                    W(te, s_v)
                    for lev in range(5):
                        for hh in range(4):
                            I(te, lambda e, hh=hh: e.matmul(
                                pC_[0:C, hh * C:(hh + 1) * C], XpT[:, hh * C:(hh + 1) * C],
                                Xp[:, hh * C:(hh + 1) * C], start=True, stop=True), (s_t, 1))
                        W(ve, s_t)
                        vop_(lambda e: e.tensor_copy(Xp[:, 0:4 * C], pC_[0:C, 0:4 * C]))
                        DR(ve)
                        W(te, s_v)
                        for hh in range(4):
                            I(te, lambda e, hh=hh: e.transpose(
                                pT_b[0:C, 0:C], Xp[:, hh * C:(hh + 1) * C], ident_b[0:C, 0:C]), (s_t, 1))
                            W(ve, s_t)
                            vop_(lambda e, hh=hh: e.tensor_copy(XpT[:, hh * C:(hh + 1) * C], pT_b[0:C, 0:C]))
                            W(te, s_v)
                        for hh in range(4):
                            I(te, lambda e, hh=hh: e.matmul(
                                pD_[0:C, hh * C:(hh + 1) * C], Xp[:, hh * C:(hh + 1) * C],
                                PT[:, hh * C:(hh + 1) * C], start=True, stop=True), (s_t, 1))
                        W(ve, s_t)
                        DR(ve)
                        vop_(lambda e: e.tensor_tensor(
                            PT2[:, 0:4 * C], pD_[0:C, 0:4 * C], PT[:, 0:4 * C], ALU.add))
                        DR(ve)
                        vop_(lambda e: e.tensor_copy(PT[:, 0:4 * C], PT2[:, 0:4 * C]))
                        DR(ve)
                        W(te, s_v)

                    for hh in range(4):
                        I(te, lambda e, hh=hh: e.matmul(
                            pC_[0:C, 0:DV], PT[:, hh * C:(hh + 1) * C],
                            bvbk[:, hh * 512:hh * 512 + DV], start=True, stop=True), (s_t, 1))
                        for sub in range(2):
                            I(te, lambda e, hh=hh, sub=sub: e.matmul(
                                pD_[0:128, sub * C:(sub + 1) * C],
                                bvbk[:, hh * 512 + DV + sub * 128: hh * 512 + DV + (sub + 1) * 128],
                                PT[:, hh * C:(hh + 1) * C], start=True, stop=True), (s_t, 1))
                        W(ve, s_t)
                        vop_(lambda e, hh=hh: e.tensor_copy(u_tm[:, hh * DV:(hh + 1) * DV],
                                                            pC_[0:C, 0:DV]))
                        vop_(lambda e: e.tensor_copy(w_fm[:, 0:C], pD_[0:128, 0:C]))
                        vop_(lambda e: e.tensor_copy(w_fm[:, C:2 * C], pD_[0:128, C:2 * C]))
                        W(te, s_v)
                        for sub in range(2):
                            I(te, lambda e, hh=hh, sub=sub, cc=cc: e.matmul(
                                pC_[0:C, 0:C],
                                kn_fm[:, (hh * 2 + sub) * C:(hh * 2 + sub + 1) * C],
                                qfm[:, (hh * 2 + sub) * GRP + cc:(hh * 2 + sub) * GRP + cc + C],
                                start=(sub == 0), stop=(sub == 1)), (s_t, 1))
                        W(ve, s_t)
                        DR(ve)
                        vop_(lambda e: e.tensor_tensor(
                            attnT[:, 0:C], pC_[0:C, 0:C], mask_upc[:], ALU.mult))
                        W(te, s_v)
                        if j > 0:
                            for sub in range(2):
                                I(te, lambda e, hh=hh, sub=sub: e.matmul(
                                    pC_[0:C, 0:DV],
                                    w_fm[:, sub * C:(sub + 1) * C],
                                    S_sb[:, hh * 512 + sub * DV: hh * 512 + (sub + 1) * DV],
                                    start=(sub == 0), stop=(sub == 1)), (s_t, 1))
                        W(ve, s_t)
                        DR(ve)
                        if j == 0:
                            vop_(lambda e, hh=hh: e.tensor_copy(
                                ui_tm[:, hh * DV:(hh + 1) * DV], u_tm[:, hh * DV:(hh + 1) * DV]))
                        else:
                            vop_(lambda e, hh=hh: e.tensor_tensor(
                                ui_tm[:, hh * DV:(hh + 1) * DV], u_tm[:, hh * DV:(hh + 1) * DV],
                                pC_[0:C, 0:DV], ALU.subtract))
                        W(te, s_v)
                        if j == 0:
                            I(te, lambda e, hh=hh: e.matmul(
                                pD_[0:C, 0:DV], attnT[:, 0:C],
                                ui_tm[:, hh * DV:(hh + 1) * DV], start=True, stop=True), (s_t, 1))
                        else:
                            for sub in range(2):
                                I(te, lambda e, hh=hh, sub=sub, cc=cc: e.matmul(
                                    pD_[0:C, 0:DV],
                                    qfm[:, (hh * 2 + sub) * GRP + cc:(hh * 2 + sub) * GRP + cc + C],
                                    S_sb[:, hh * 512 + sub * DV: hh * 512 + (sub + 1) * DV],
                                    start=(sub == 0), stop=False), (s_t, 1))
                            I(te, lambda e, hh=hh: e.matmul(
                                pD_[0:C, 0:DV], attnT[:, 0:C],
                                ui_tm[:, hh * DV:(hh + 1) * DV], start=False, stop=True), (s_t, 1))
                        for sub in range(2):
                            I(te, lambda e, hh=hh, sub=sub, j=j: e.matmul(
                                pS[hh][:, sub * DV:(sub + 1) * DV],
                                kn_tm[:, hh * DV + sub * 128: hh * DV + (sub + 1) * 128],
                                ui_tm[:, hh * DV:(hh + 1) * DV],
                                start=(j == 0), stop=False,
                                skip_group_check=True), (s_t, 1))
                        W(ve, s_t)
                        WL(ve)
                        vop_(lambda e, hh=hh: e.tensor_scalar(
                            o_tile[:, hh * DV:(hh + 1) * DV], pD_[0:C, 0:DV],
                            rs_b[0:C, hh:hh + 1], None, ALU.mult))
                        vop_(lambda e, hh=hh: e.tensor_copy(
                            S_sb[:, hh * 512:hh * 512 + 512], pS[hh][:, 0:512]))
                        W(te, s_v)

                    if DEBUG and j == 0:
                        W(gp, s_v)
                        W(gp, s_gd)
                        I(gp, lambda e: e.dma_start(
                            out=dbg["uw0"][:, 0:1024], in_=u_tm[:]), (s_gd, 16))
                        W(gp, s_gd)
                        I(gp, lambda e: e.dma_start(
                            out=dbg["uw0"][:, 1024:2048], in_=ui_tm[:]), (s_gd, 16))
                        W(gp, s_gd)
                        I(gp, lambda e: e.dma_start(out=dbg["o0"][:],
                                                    in_=o_tile[:]), (s_gd, 16))
                        W(sy, s_dbg)
                        W(ve, s_dbg)
                        W(sc, s_dbg)
                        W(te, s_dbg)
                        W(gp, s_dbg)
                        W(sy, s_gd)
                        W(ve, s_gd)
                        W(sc, s_gd)
                        W(te, s_gd)
                        W(gp, s_gd)
                    W(sy, s_v)
                    ld(o_loc[j * C:(j + 1) * C, :], o_tile[:])

            me2 = nc.partition_id() % 2
            WL(sy)
            if state["cur"] == sy:
                e = state["eng"]
                e.dma_start(out=o_stage[:, :], in_=o_loc[ts(me2, OWN), :]).then_inc(s_l0, 16)
            state["cnt"][s_l0.name] = state["cnt"].get(s_l0.name, 0) + 16
            for rb in range(8):
                s_dyn = (s_l0, s_l1)[rb % 2]
                W(sy, s_dyn)
                if state["cur"] == sy:
                    e = state["eng"]
                    e.dma_start(out=v_stage[rb * 128:(rb + 1) * 128, :],
                                in_=v_loc[rb * 128:(rb + 1) * 128, bass.ds(me2 * OWN, OWN + 64)]
                                ).then_inc(s_dyn, 16)
                state["cnt"][s_dyn.name] = state["cnt"].get(s_dyn.name, 0) + 16
            WL(sy)

            W(sy, s_t)
            W(sy, s_a)
            W(sy, s_v)
            W(ve, s_t)
            WD(ve)
            load_w(w1_sb, w1T, 8, 2048)
            ld(wstage2[0:32, 0:1024], w1T[1024:1056, 0:1024])
            WL(ve)
            vop_(lambda e: e.tensor_copy(w1_sb[0:32, 8 * 2048:8 * 2048 + 1024], wstage2[0:32, 0:1024]))
            W(sy, s_v)
            ld(wstage2[0:32, 0:1024], w1T[1024:1056, 1024:2048])
            WL(ve)
            vop_(lambda e: e.tensor_copy(w1_sb[0:32, 8 * 2048 + 1024:9 * 2048], wstage2[0:32, 0:1024]))
            W(sy, s_v)
            for kk in range(8):
                ld(wstage2[0:128, 0:1024], woT[kk * 128:(kk + 1) * 128, :])
                WL(ve)
                vop_(lambda e, kk=kk: e.tensor_scalar(
                    wo_sb[:, kk * 1024:(kk + 1) * 1024], wstage2[0:128, 0:1024],
                    onw_sb[:, kk:kk + 1], None, ALU.mult))
                W(sy, s_v)
            W(te, s_v)
            WL(te)
            W(sc, s_v)
            WL(sc)
            WL(gp)

            def xfm_own(r0, n_ti):
                for ti_ in range(n_ti):
                    W(sy, s_v)
                    ld(xt_f[:], x_own[r0 + ti_ * 128: r0 + (ti_ + 1) * 128, :])
                    WL(ve)
                    WD(ve)
                    vop_(lambda e: e.tensor_copy(xt_b[:, 0:D], xt_f[:]))
                    W(sc, s_v)
                    W(sc, s_t)
                    for kk in range(8):
                        dmat(xfm[:, kk * GRP + ti_ * 128: kk * GRP + (ti_ + 1) * 128],
                             xt_b[:, kk * 128:(kk + 1) * 128])
                WD(te)

            for og in range(NOG_LIM):
                ot0 = og * OG
                xfm_own(XHALO + ot0, 2)
                W(sy, s_v)
                for rb in range(8):
                    ld(vow[:, rb * 320:(rb + 1) * 320],
                       v_stage[rb * 128:(rb + 1) * 128, ot0: ot0 + 320])
                for ti in range(2):
                    ld(pth_d[:, ti * D:(ti + 1) * D],
                       o_stage[ot0 + ti * 128: ot0 + (ti + 1) * 128, :])
                WL(ve)
                WD(ve)
                WL(te)
                WD(te)
                WL(sc)

                W(ve, s_t)
                DR(ve)
                for rb in range(8):
                    vb = rb * 320
                    vop_(lambda e, vb=vb, rb=rb: e.tensor_scalar(
                        fir_a[:, 0:OG], vow[:, vb + 2: vb + 2 + OG],
                        firl_sb[:, rb * 63: rb * 63 + 1], None, ALU.mult))
                    DR(ve)
                    cur, oth = fir_b, fir_a
                    for t in range(1, 63):
                        vop_(lambda e, cur=cur, oth=oth, vb=vb, rb=rb, t=t:
                             e.scalar_tensor_tensor(cur[:, 0:OG], vow[:, vb + 2 + t: vb + 2 + t + OG],
                                                    firl_sb[:, rb * 63 + t: rb * 63 + t + 1],
                                                    oth[:, 0:OG], ALU.mult, ALU.add))
                        DR(ve)
                        cur, oth = oth, cur
                    WD(ve)
                    DR(ve)
                    vop_(lambda e, oth=oth: e.tensor_copy(fir_o[:, 0:OG], oth[:, 0:OG]))
                    DR(ve)
                    W(sc, s_v)
                    for ti in range(2):
                        dmat(pth_l[:, ti * D + rb * 128: ti * D + (rb + 1) * 128],
                             fir_o[:, ti * 128:(ti + 1) * 128])
                    vop_(lambda e, vb=vb, rb=rb: e.tensor_scalar(
                        fir_a[:, 0:OG], vow[:, vb + 62: vb + 62 + OG],
                        firs_sb[:, rb * 3: rb * 3 + 1], None, ALU.mult))
                    DR(ve)
                    vop_(lambda e, vb=vb, rb=rb: e.scalar_tensor_tensor(
                        fir_b[:, 0:OG], vow[:, vb + 63: vb + 63 + OG],
                        firs_sb[:, rb * 3 + 1: rb * 3 + 2], fir_a[:, 0:OG], ALU.mult, ALU.add))
                    DR(ve)
                    vop_(lambda e, vb=vb, rb=rb: e.scalar_tensor_tensor(
                        fir_a[:, 0:OG], vow[:, vb + 64: vb + 64 + OG],
                        firs_sb[:, rb * 3 + 2: rb * 3 + 3], fir_b[:, 0:OG], ALU.mult, ALU.add))
                    DR(ve)
                    WD(ve)
                    DR(ve)
                    vop_(lambda e: e.tensor_copy(fir_o[:, 0:OG], fir_a[:, 0:OG]))
                    DR(ve)
                    W(sc, s_v)
                    for ti in range(2):
                        dmat(pth_s[:, ti * D + rb * 128: ti * D + (rb + 1) * 128],
                             fir_o[:, ti * 128:(ti + 1) * 128])
                        dmat(pth_v[:, ti * D + rb * 128: ti * D + (rb + 1) * 128],
                             vow[:, vb + 64 + ti * 128: vb + 64 + (ti + 1) * 128])

                WD(ve)
                for ti in range(2):
                    for pi, pth in enumerate([pth_s, pth_l, pth_d, pth_v]):
                        vop_(lambda e, pth=pth, ti=ti, pi=pi: e.tensor_reduce(
                            stat_tm[:, ti * 32 + 8 * pi: ti * 32 + 8 * pi + 8:2],
                            pth[:, ti * D:(ti + 1) * D].rearrange("p (h dv) -> p h dv", h=4),
                            AX.X, ALU.add))
                        W(sc, s_v)
                        I(sc, lambda e, pth=pth, ti=ti: e.activation(
                            tstage[:, 0:D], pth[:, ti * D:(ti + 1) * D], AF.Square), (s_a, 1))
                        W(ve, s_a)
                        vop_(lambda e, ti=ti, pi=pi: e.tensor_reduce(
                            stat_tm[:, ti * 32 + 8 * pi + 1: ti * 32 + 8 * pi + 8:2],
                            tstage[:, 0:D].rearrange("p (h dv) -> p h dv", h=4),
                            AX.X, ALU.add))
                        DR(ve)
                    vop_(lambda e, ti=ti: e.tensor_scalar(sm_a[:, 0:32], stat_tm[:, ti * 32: ti * 32 + 32],
                                                          1.0 / DV, None, ALU.mult))
                    DR(ve)
                    vop_(lambda e: e.tensor_tensor(sm_b[:, 0:32], sm_a[:, 0:32], sm_a[:, 0:32], ALU.mult))
                    DR(ve)
                    vop_(lambda e, ti=ti: e.tensor_tensor(
                        stat_tm[:, ti * 32 + 1: ti * 32 + 32: 2],
                        sm_a[:, 1:32:2], sm_b[:, 0:32:2], ALU.subtract))
                    vop_(lambda e, ti=ti: e.tensor_copy(
                        stat_tm[:, ti * 32: ti * 32 + 32: 2], sm_a[:, 0:32:2]))
                    DR(ve)
                if DEBUG and og == 0:
                    W(sy, s_v)
                    W(gp, s_v)
                    WD(gp)
                    W(sy, s_dbg)
                    I(sy, lambda e: e.dma_start(out=dbg["stat"], in_=stat_tm[:, 0:64]), (s_dbg, 16))
                    W(gp, s_gd)
                    I(gp, lambda e: e.dma_start(out=dbg["pths"][:, 0:D], in_=pth_s[:, 0:D]), (s_gd, 16))
                    W(gp, s_gd)
                    I(gp, lambda e: e.dma_start(out=dbg["pths"][:, D:2 * D], in_=pth_l[:, 0:D]), (s_gd, 16))
                    W(gp, s_gd)
                    I(gp, lambda e: e.dma_start(out=dbg["pths"][:, 2 * D:3 * D], in_=pth_d[:, 0:D]), (s_gd, 16))
                    W(gp, s_gd)
                    I(gp, lambda e: e.dma_start(out=dbg["pths"][:, 3 * D:4 * D], in_=pth_v[:, 0:D]), (s_gd, 16))
                    W(sy, s_dbg)
                    W(ve, s_dbg)
                    W(sc, s_dbg)
                    W(te, s_dbg)
                    W(gp, s_dbg)
                    W(sy, s_gd)
                    W(ve, s_gd)
                    W(sc, s_gd)
                    W(te, s_gd)
                    W(gp, s_gd)

                W(te, s_v)
                for ti in range(2):
                    I(te, lambda e, ti=ti: e.transpose(
                        pT_f[0:32, 0:128], stat_tm[:, ti * 32:(ti + 1) * 32], ident_f[:]), (s_t, 1))
                    W(ve, s_t)
                    vop_(lambda e, ti=ti: e.tensor_copy(
                        stat_fm[0:32, ti * 128:(ti + 1) * 128], pT_f[0:32, 0:128]))
                    W(te, s_v)

                for m in range(16):
                    pp = pA if (m % 2 == 0) else pB
                    for kk in range(8):
                        I(te, lambda e, pp=pp, m=m, kk=kk: e.matmul(
                            pp[:, 0:OG], w1_sb[:, kk * 2048 + m * 128: kk * 2048 + m * 128 + 128],
                            xfm[:, kk * GRP: kk * GRP + OG],
                            start=(kk == 0), stop=False), (s_t, 1))
                    I(te, lambda e, pp=pp, m=m: e.matmul(
                        pp[:, 0:OG], w1_sb[0:32, 8 * 2048 + m * 128: 8 * 2048 + m * 128 + 128],
                        stat_fm[0:32, 0:OG], start=False, stop=True), (s_t, 1))
                    W(sc, s_t)
                    I(sc, lambda e, pp=pp, m=m: e.activation(
                        hmid[:, (m % 2) * GRP:(m % 2) * GRP + OG], pp[:, 0:OG],
                        AF.Gelu, bias=gb1_sb[:, m:m + 1]), (s_a, 1))
                    W(te, s_a)
                    I(te, lambda e, m=m: e.matmul(
                        pC_[0:16, 0:OG], w2_sb[:, m * 16:(m + 1) * 16],
                        hmid[:, (m % 2) * GRP:(m % 2) * GRP + OG],
                        start=(m == 0), stop=(m == 15)), (s_t, 1))
                W(ve, s_t)
                vop_(lambda e: e.tensor_scalar(logit_f[:, 0:OG], pC_[0:16, 0:OG],
                                               bb_sb[0:16, 0:1], None, ALU.add))
                W(te, s_v)
                for ti in range(2):
                    I(te, lambda e, ti=ti: e.transpose(
                        pT_f[0:128, 0:16], logit_f[:, ti * 128:(ti + 1) * 128],
                        ident_f[0:16, 0:16]), (s_t, 1))
                    W(ve, s_t)
                    vop_(lambda e, ti=ti: e.tensor_copy(logit_t[:, ti * 16:(ti + 1) * 16],
                                                        pT_f[0:128, 0:16]))
                    W(te, s_v)
                DR(ve)
                vop_(lambda e: e.tensor_tensor(sm_a[:, 0:32], logit_t[:, 0:32],
                                               itemp_sb[:, 0:32], ALU.mult))
                DR(ve)
                vop_(lambda e: e.tensor_reduce(sm_b[:, 0:8],
                                               sm_a[:, 0:32].rearrange("p (t q) -> p t q", q=4),
                                               AX.X, ALU.max))
                DR(ve)
                for th in range(8):
                    vop_(lambda e, th=th: e.tensor_scalar(
                        prob_t[:, th * 4:(th + 1) * 4], sm_a[:, th * 4:(th + 1) * 4],
                        sm_b[:, th:th + 1], None, ALU.subtract))
                W(sc, s_v)
                I(sc, lambda e: e.activation(sm_a[:, 32:64], prob_t[:, 0:32], AF.Tanh, scale=0.5), (s_a, 1))
                W(ve, s_a)
                vop_(lambda e: e.tensor_scalar(sm_b[:, 32:64], sm_a[:, 32:64], -1.0, 1.0,
                                               ALU.mult, ALU.add))
                DR(ve)
                vop_(lambda e: e.reciprocal(prob_t[:, 0:32], sm_b[:, 32:64]))
                DR(ve)
                vop_(lambda e: e.tensor_scalar(sm_b[:, 32:64], sm_a[:, 32:64], 1.0, None, ALU.add))
                DR(ve)
                vop_(lambda e: e.tensor_tensor(sm_a[:, 0:32], sm_b[:, 32:64], prob_t[:, 0:32], ALU.mult))
                DR(ve)
                vop_(lambda e: e.tensor_reduce(sm_b[:, 0:8],
                                               sm_a[:, 0:32].rearrange("p (t q) -> p t q", q=4),
                                               AX.X, ALU.add))
                DR(ve)
                vop_(lambda e: e.reciprocal(sm_b[:, 8:16], sm_b[:, 0:8]))
                DR(ve)
                for th in range(8):
                    vop_(lambda e, th=th: e.tensor_scalar(
                        prob_t[:, th * 4:(th + 1) * 4], sm_a[:, th * 4:(th + 1) * 4],
                        sm_b[:, 8 + th:9 + th], None, ALU.mult))
                DR(ve)
                vop_(lambda e: e.tensor_tensor(sm_a[:, 0:32], prob_t[:, 0:32],
                                               floor_sb[:, 0:32], ALU.max))
                DR(ve)
                vop_(lambda e: e.tensor_reduce(sm_b[:, 0:8],
                                               sm_a[:, 0:32].rearrange("p (t q) -> p t q", q=4),
                                               AX.X, ALU.add))
                DR(ve)
                vop_(lambda e: e.reciprocal(sm_b[:, 8:16], sm_b[:, 0:8]))
                DR(ve)
                for th in range(8):
                    vop_(lambda e, th=th: e.tensor_scalar(
                        prob_t[:, th * 4:(th + 1) * 4], sm_a[:, th * 4:(th + 1) * 4],
                        sm_b[:, 8 + th:9 + th], None, ALU.mult))
                DR(ve)
                if DEBUG and og == 0:
                    W(sy, s_v)
                    W(sy, s_dbg)
                    I(sy, lambda e: e.dma_start(out=dbg["probs"], in_=prob_t[:, 0:32]), (s_dbg, 16))
                    W(sy, s_dbg)
                    W(ve, s_dbg)
                    W(sc, s_dbg)
                    W(te, s_dbg)
                    W(gp, s_dbg)
                    W(sy, s_gd)
                    W(ve, s_gd)
                    W(sc, s_gd)
                    W(te, s_gd)
                    W(gp, s_gd)

                for ti in range(2):
                    tcol = ti * D
                    for hh in range(4):
                        vop_(lambda e, hh=hh, tcol=tcol, ti=ti: e.tensor_scalar(
                            comb_a[:, hh * DV:(hh + 1) * DV],
                            pth_s[:, tcol + hh * DV: tcol + (hh + 1) * DV],
                            prob_t[:, ti * 16 + hh * 4: ti * 16 + hh * 4 + 1], None, ALU.mult))
                    DR(ve)
                    for hh in range(4):
                        vop_(lambda e, hh=hh, tcol=tcol, ti=ti: e.scalar_tensor_tensor(
                            comb_b[:, hh * DV:(hh + 1) * DV],
                            pth_l[:, tcol + hh * DV: tcol + (hh + 1) * DV],
                            prob_t[:, ti * 16 + hh * 4 + 1: ti * 16 + hh * 4 + 2],
                            comb_a[:, hh * DV:(hh + 1) * DV], ALU.mult, ALU.add))
                    DR(ve)
                    for hh in range(4):
                        vop_(lambda e, hh=hh, tcol=tcol, ti=ti: e.scalar_tensor_tensor(
                            comb_a[:, hh * DV:(hh + 1) * DV],
                            pth_d[:, tcol + hh * DV: tcol + (hh + 1) * DV],
                            prob_t[:, ti * 16 + hh * 4 + 2: ti * 16 + hh * 4 + 3],
                            comb_b[:, hh * DV:(hh + 1) * DV], ALU.mult, ALU.add))
                    DR(ve)
                    for hh in range(4):
                        vop_(lambda e, hh=hh, tcol=tcol, ti=ti: e.scalar_tensor_tensor(
                            comb_b[:, hh * DV:(hh + 1) * DV],
                            pth_v[:, tcol + hh * DV: tcol + (hh + 1) * DV],
                            prob_t[:, ti * 16 + hh * 4 + 3: ti * 16 + hh * 4 + 4],
                            comb_a[:, hh * DV:(hh + 1) * DV], ALU.mult, ALU.add))
                    DR(ve)
                    vop_(lambda e: e.tensor_tensor(tstage[:, 0:D], comb_b[:, 0:D], comb_b[:, 0:D], ALU.mult))
                    DR(ve)
                    vop_(lambda e: e.tensor_reduce(ln_sc[:, 0:4],
                                                   tstage[:, 0:D].rearrange("p (h dv) -> p h dv", h=4),
                                                   AX.X, ALU.add))
                    DR(ve)
                    vop_(lambda e: e.tensor_scalar(rs_a[:, 0:4], ln_sc[:, 0:4], 1.0 / DV, 1e-5,
                                                   ALU.mult, ALU.add))
                    DR(ve)

                    newton4((0, 4), (0, 4))
                    WD(ve)
                    for hh in range(4):
                        vop_(lambda e, hh=hh: e.tensor_scalar(
                            orms[:, hh * DV:(hh + 1) * DV], comb_b[:, hh * DV:(hh + 1) * DV],
                            rs_b[:, hh:hh + 1], None, ALU.mult))
                    W(sc, s_v)
                    W(sc, s_t)
                    for kk in range(8):
                        dmat(ofm[:, kk * 128:(kk + 1) * 128], orms[:, kk * 128:(kk + 1) * 128])
                    WD(te)
                    for half in range(2):
                        pp = pA if half == 0 else pB
                        for kk in range(8):
                            I(te, lambda e, pp=pp, half=half, kk=kk: e.matmul(
                                pp[:, 0:512], ofm[:, kk * 128:(kk + 1) * 128],
                                wo_sb[:, kk * 1024 + half * 512: kk * 1024 + half * 512 + 512],
                                start=(kk == 0), stop=(kk == 7)), (s_t, 1))
                    W(sy, s_v)
                    ld(xt_f[:], x_own[XHALO + ot0 + ti * 128: XHALO + ot0 + (ti + 1) * 128, :])
                    W(ve, s_t)
                    WL(ve)
                    vop_(lambda e: e.tensor_tensor(y_t[:, 0:512], pA[:, 0:512], xt_f[:, 0:512], ALU.add))
                    vop_(lambda e: e.tensor_tensor(y_t[:, 512:1024], pB[:, 0:512],
                                                   xt_f[:, 512:1024], ALU.add))
                    W(te, s_v)
                    W(ve, s_a)
                    DR(ve)
                    vop_(lambda e: e.tensor_reduce(ln_sc[:, 0:1], y_t[:, 0:D], AX.X, ALU.add))
                    vop_(lambda e: e.tensor_tensor(tstage[:, 0:GRP], y_t[:, 0:GRP], y_t[:, 0:GRP], ALU.mult))
                    vop_(lambda e: e.tensor_tensor(tstage[:, GRP:2 * GRP], y_t[:, GRP:D],
                                                   y_t[:, GRP:D], ALU.mult))
                    DR(ve)
                    vop_(lambda e: e.tensor_reduce(ln_sc[:, 1:2], tstage[:, 0:2 * GRP], AX.X, ALU.add))
                    DR(ve)
                    vop_(lambda e: e.tensor_scalar(ln_sc[:, 2:3], ln_sc[:, 0:1], 1.0 / D, None, ALU.mult))
                    DR(ve)
                    vop_(lambda e: e.tensor_tensor(ln_sc[:, 3:4], ln_sc[:, 2:3], ln_sc[:, 2:3], ALU.mult))
                    vop_(lambda e: e.tensor_scalar(ln_sc[:, 4:5], ln_sc[:, 1:2], 1.0 / D, None, ALU.mult))
                    DR(ve)
                    vop_(lambda e: e.tensor_tensor(ln_sc[:, 5:6], ln_sc[:, 4:5], ln_sc[:, 3:4], ALU.subtract))
                    DR(ve)
                    vop_(lambda e: e.tensor_scalar(rs_a[:, 0:1], ln_sc[:, 5:6], 1.0, 1e-5, ALU.mult, ALU.add))
                    DR(ve)
                    newton4((0, 1), (0, 1), n=1)
                    vop_(lambda e: e.tensor_scalar(ln_sc[:, 10:11], ln_sc[:, 2:3], -1.0, None, ALU.mult))
                    DR(ve)
                    WL(ve)
                    vop_(lambda e: e.tensor_scalar(x1_t[:, 0:D], y_t[:, 0:D],
                                                   ln_sc[:, 10:11], rs_b[:, 0:1], ALU.add, ALU.mult))
                    DR(ve)
                    vop_(lambda e: e.tensor_reduce(ln_sc[:, 12:13], x1_t[:, 0:D], AX.X, ALU.max,
                                                   apply_absolute_value=True))
                    DR(ve)
                    vop_(lambda e: e.tensor_tensor(ln_sc[:, 13:14], amax_run[:, 0:1],
                                                   ln_sc[:, 12:13], ALU.max))
                    DR(ve)
                    vop_(lambda e: e.tensor_copy(amax_run[:, 0:1], ln_sc[:, 13:14]))
                    W(sy, s_v)
                    ld(x1_spill[ot0 + ti * 128: ot0 + (ti + 1) * 128, :], x1_t[:, 0:D])
                    if DEBUG and og == 0 and ti == 0:
                        W(sy, s_dbg)
                        I(sy, lambda e: e.dma_start(out=dbg["x1"], in_=x1_t[:]), (s_dbg, 16))
                        W(sy, s_dbg)
                        W(ve, s_dbg)
                        W(sc, s_dbg)
                        W(te, s_dbg)
                        W(gp, s_dbg)
                        W(sy, s_gd)
                        W(ve, s_gd)
                        W(sc, s_gd)
                        W(te, s_gd)
                        W(gp, s_gd)

            W(te, s_v)
            I(te, lambda e: e.transpose(pT_f[0:1, 0:128], amax_run[:, 0:1], ident_f[:]), (s_t, 1))
            W(ve, s_t)
            vop_(lambda e: e.tensor_reduce(sA[0:1, 40:41], pT_f[0:1, 0:128], AX.X, ALU.max))
            DR(ve)
            vop_(lambda e: e.tensor_scalar(sA[0:1, 43:44], sA[0:1, 40:41], 1e-12, None, ALU.max))
            DR(ve)
            vop_(lambda e: e.reciprocal(sA[0:1, 44:45], sA[0:1, 43:44]))
            DR(ve)
            vop_(lambda e: e.tensor_scalar(sA[0:1, 45:46], sA[0:1, 44:45], 448.0, 448.0,
                                           ALU.mult, ALU.min))
            DR(ve)
            vop_(lambda e: e.reciprocal(sA[0:1, 46:47], sA[0:1, 45:46]))
            DR(ve)
            vop_(lambda e: e.tensor_scalar(sA[0:1, 47:48], sA[0:1, 45:46], 0.5, None, ALU.mult))
            DR(ve)
            vop_(lambda e: e.tensor_tensor(sA[0:1, 48:49], sA[0:1, 46:47], sA[0:1, 7:8], ALU.mult))
            DR(ve)
            vop_(lambda e: e.tensor_scalar(sA[0:1, 49:50], sA[0:1, 48:49], 4.0, None, ALU.mult))
            DR(ve)
            W(sy, s_v)
            ld(loc_sc[0:1, 4:5], sA[0:1, 47:48])
            WL(sy)
            ld(loc_sc[0:1, 6:7], sA[0:1, 49:50])
            WL(sy)
            ld(sc_bc[:, 0:1], loc_sc[0:1, 4:5].broadcast_to((128, 1)))
            ld(sc_bc[:, 1:2], loc_sc[0:1, 6:7].broadcast_to((128, 1)))
            WL(ve)
            if DEBUG:
                W(sy, s_v)
                W(sy, s_dbg)
                I(sy, lambda e: e.dma_start(out=dbg["amax"], in_=amax_run[:]), (s_dbg, 16))

            W(gp, s_v)
            WD(gp)
            W(gp, s_gd)
            I(gp, lambda e: e.dma_start(out=fbias_sb[:],
                                        in_=fbias[None, :].broadcast_to((128, 1024))), (s_gd, 16))
            W(sy, s_t)
            W(sy, s_v)
            for kk in range(8):
                ld(wstage2[:, 0:1024], wfT[kk * 128:(kk + 1) * 128, :])
                WL(ve)
                W(ve, s_t)
                vop_(lambda e, kk=kk: e.tensor_scalar(
                    wf8_sb[:, kk * 1024:(kk + 1) * 1024], wstage2[:, 0:1024],
                    ws2_bc[:, 0:1], None, ALU.mult))
                W(sy, s_v)
            for tt in range(OWN // 128):
                r0 = tt * 128
                W(sy, s_v)
                ld(xt_f[:], x1_spill[r0:r0 + 128, :])
                WL(ve)
                WD(ve)
                vop_(lambda e: e.tensor_scalar(xs_b[:, 0:D], xt_f[:], sc_bc[:, 0:1], None, ALU.mult))
                W(sc, s_v)
                W(sc, s_t)
                for kk in range(8):
                    dmat(xs_fmb[:, kk * 128:(kk + 1) * 128], xs_b[:, kk * 128:(kk + 1) * 128])
                WD(ve)
                vop_(lambda e: e.tensor_copy(xs_fm8[:], xs_fmb[:]))
                W(te, s_v)
                for half in range(2):
                    pp = pA if half == 0 else pB
                    for kk in range(8):
                        I(te, lambda e, pp=pp, half=half, kk=kk: e.matmul(
                            pp[:, 0:512], xs_fm8[:, kk * 128:(kk + 1) * 128],
                            wf8_sb[:, kk * 1024 + half * 512: kk * 1024 + half * 512 + 512],
                            start=(kk == 0), stop=(kk == 7)), (s_t, 1))
                W(ve, s_t)
                W(ve, s_gd)
                vop_(lambda e: e.scalar_tensor_tensor(ffn_t[:, 0:512], pA[:, 0:512],
                                                      sc_bc[:, 1:2], fbias_sb[:, 0:512],
                                                      ALU.mult, ALU.add))
                vop_(lambda e: e.scalar_tensor_tensor(ffn_t[:, 512:1024], pB[:, 0:512],
                                                      sc_bc[:, 1:2], fbias_sb[:, 512:1024],
                                                      ALU.mult, ALU.add))
                W(te, s_v)
                DR(ve)
                vop_(lambda e: e.tensor_tensor(y_t[:, 0:D], xt_f[:], ffn_t[:, 0:D], ALU.add))
                DR(ve)
                vop_(lambda e: e.tensor_reduce(ln_sc[:, 0:1], y_t[:, 0:D], AX.X, ALU.add))
                vop_(lambda e: e.tensor_tensor(tstage[:, 0:GRP], y_t[:, 0:GRP], y_t[:, 0:GRP], ALU.mult))
                vop_(lambda e: e.tensor_tensor(tstage[:, GRP:2 * GRP], y_t[:, GRP:D],
                                               y_t[:, GRP:D], ALU.mult))
                DR(ve)
                vop_(lambda e: e.tensor_reduce(ln_sc[:, 1:2], tstage[:, 0:2 * GRP], AX.X, ALU.add))
                DR(ve)
                vop_(lambda e: e.tensor_scalar(ln_sc[:, 2:3], ln_sc[:, 0:1], 1.0 / D, None, ALU.mult))
                DR(ve)
                vop_(lambda e: e.tensor_tensor(ln_sc[:, 3:4], ln_sc[:, 2:3], ln_sc[:, 2:3], ALU.mult))
                vop_(lambda e: e.tensor_scalar(ln_sc[:, 4:5], ln_sc[:, 1:2], 1.0 / D, None, ALU.mult))
                DR(ve)
                vop_(lambda e: e.tensor_tensor(ln_sc[:, 5:6], ln_sc[:, 4:5], ln_sc[:, 3:4], ALU.subtract))
                DR(ve)
                vop_(lambda e: e.tensor_scalar(rs_a[:, 0:1], ln_sc[:, 5:6], 1.0, 1e-5, ALU.mult, ALU.add))
                DR(ve)
                newton4((0, 1), (0, 1), n=1)
                vop_(lambda e: e.tensor_scalar(ln_sc[:, 10:11], ln_sc[:, 2:3], -1.0, None, ALU.mult))
                DR(ve)
                W(ve, s_st)
                vop_(lambda e: e.tensor_scalar(out_t[:, 0:D], y_t[:, 0:D],
                                               ln_sc[:, 10:11], rs_b[:, 0:1], ALU.add, ALU.mult))
                W(sy, s_v)
                W(sy, s_st)
                I(sy, lambda e, r0=r0: e.dma_start(out=out_d[r0:r0 + 128, :], in_=out_t[:, 0:D]),
                  (s_st, 16))

            W(sy, s_st)
            WL(sy)
            WD(sy)
            W(sy, s_gd)
            W(sy, s_dbg)

        block = ctx.enter_context(nc.Block())

        @block.sync
        def _(sync):
            emit("sync", sync)

        @block.tensor
        def _(tensor):
            emit("tensor", tensor)

        @block.vector
        def _(vector):
            emit("vector", vector)

        @block.scalar
        def _(scalar):
            emit("scalar", scalar)

        @block.gpsimd
        def _(gpsimd):
            emit("gpsimd", gpsimd)

    return nc, out_d


_cache = {}


def kernel(**inputs):
    x = np.ascontiguousarray(inputs["L0_attn_hidden_states"], dtype=np.float32)
    Wq = np.asarray(inputs["L0_attn_Wq"], np.float32)
    Wk = np.asarray(inputs["L0_attn_Wk"], np.float32)
    Wv = np.asarray(inputs["L0_attn_Wv"], np.float32)
    Wb = np.asarray(inputs["L0_attn_Wb"], np.float32)
    qconv = np.asarray(inputs["L0_attn_qconv_w"], np.float32)
    kconv = np.asarray(inputs["L0_attn_kconv_w"], np.float32)
    vconv = np.asarray(inputs["L0_attn_vconv_w"], np.float32)
    fir_s = np.asarray(inputs["L0_attn_fir_short_w"], np.float32).reshape(D, 3)
    fir_l = np.asarray(inputs["L0_attn_fir_long_w"], np.float32).reshape(D, 63)
    g_w1 = np.asarray(inputs["L0_attn_gate_w1"], np.float32)
    g_b1 = np.asarray(inputs["L0_attn_gate_b1"], np.float32)
    g_w2 = np.asarray(inputs["L0_attn_gate_w2"], np.float32)
    log_temp = np.asarray(inputs["L0_attn_log_temp"], np.float32)
    base_bias = np.asarray(inputs["L0_attn_base_bias"], np.float32)
    floor_raw = np.asarray(inputs["L0_attn_floor_raw"], np.float32)
    onorm_w = np.asarray(inputs["L0_attn_onorm_w"], np.float32)
    Wo = np.asarray(inputs["L0_attn_Wo"], np.float32)
    Wf = np.asarray(inputs["L0_ffn_weight"], np.float32)
    fb = np.asarray(inputs["L0_ffn_bias"], np.float32)

    if "nc" not in _cache:
        _cache["nc"] = build_program()
    nc, out_d = _cache["nc"]

    def rbmajor(a):
        rb = a.shape[0] // 128
        return np.ascontiguousarray(a.reshape(rb, 128, -1).transpose(1, 0, 2).reshape(128, -1))

    in_maps = []
    for c in range(8):
        b, me = c // 2, c % 2
        xb_seq = x[b]
        xo = np.zeros((OWN + XHALO, D), np.float32)
        lo = me * OWN - XHALO
        src0 = max(lo, 0)
        xo[src0 - lo:] = xb_seq[src0: me * OWN + OWN]
        in_maps.append({
            "x_seq": xb_seq,
            "x_own": xo,
            "wqT": np.ascontiguousarray(Wq.T),
            "wkT": np.ascontiguousarray(Wk.T),
            "wbT": np.ascontiguousarray(Wb.T),
            "wvT": np.ascontiguousarray(Wv.T),
            "w1T": np.ascontiguousarray(g_w1.T),
            "w2T": np.ascontiguousarray(g_w2.T),
            "woT": np.ascontiguousarray(Wo.T),
            "wfT": np.ascontiguousarray(Wf.T),
            "qcw": rbmajor(qconv),
            "kcw": rbmajor(kconv),
            "vcw": rbmajor(vconv),
            "firs": rbmajor(fir_s),
            "firl": rbmajor(fir_l),
            "gb1": rbmajor(g_b1.reshape(2048, 1)),
            "bb": np.ascontiguousarray(base_bias.reshape(16, 1)),
            "ltemp": log_temp.reshape(1, 4),
            "fraw": floor_raw.reshape(1, 16),
            "onw4": rbmajor(np.tile(onorm_w, H).reshape(D, 1)),
            "fbias": fb,
        })

    res = run_bass_kernel_spmd(nc, in_maps, core_ids=list(range(8)))
    out = np.empty((B, L, D), np.float32)
    for c in range(8):
        b, me = c // 2, c % 2
        out[b, me * OWN:(me + 1) * OWN] = res.results[c]["out"]
    return out


# version 09
# speedup vs baseline: 1.0808x; 1.0808x over previous
import os
import numpy as np
import concourse.bass as bass
import concourse.mybir as mybir
from concourse.bass_utils import run_bass_kernel_spmd

F32 = mybir.dt.float32
BF16 = mybir.dt.bfloat16
FP8 = mybir.dt.float8e4
I32 = mybir.dt.int32
AF = mybir.ActivationFunctionType
ALU = mybir.AluOpType
AX = mybir.AxisListType
ts = bass.ts

B, L, D, H = 4, 8192, 1024, 4
DV = 256
OWN = 4096
C = 64
NCH = L // C
GRP = 512
NG = L // GRP
OG = 256
NOG = OWN // OG
XHALO = 128
MAGIC_HI = float(0x5F3759DF >> 9)
NH = 4

DEBUG = bool(int(os.environ.get("KBG_DEBUG", "0")))
NG_LIM = int(os.environ.get("KBG_NG", str(NG)))
NOG_LIM = int(os.environ.get("KBG_NOG", str(NOG)))


def build_program():
    nc = bass.Bass("TRN2", target_bir_lowering=False)

    d = {}
    def din(name, shape, dt=F32):
        d[name] = nc.dram_tensor(name, shape, dt, kind="ExternalInput").ap()
        return d[name]

    x_seq = din("x_seq", [L, D])
    x_own = din("x_own", [OWN + XHALO, D])
    wqT = din("wqT", [D, D]); wkT = din("wkT", [D, D]); wbT = din("wbT", [D, 4])
    wvT = din("wvT", [D, D])
    w1T = din("w1T", [1056, 2048]); w2T = din("w2T", [2048, 16])
    woT = din("woT", [D, D]); wfT = din("wfT", [D, D])
    qcw = din("qcw", [128, 32]); kcw = din("kcw", [128, 32]); vcw = din("vcw", [128, 32])
    firs = din("firs", [128, 24]); firl = din("firl", [128, 504])
    gb1 = din("gb1", [128, 16]); bb = din("bb", [16, 1])
    ltemp = din("ltemp", [1, 4]); fraw = din("fraw", [1, 16])
    onw4 = din("onw4", [128, 8]); fbias = din("fbias", [D])

    out_d = nc.dram_tensor("out", [OWN, D], F32, kind="ExternalOutput").ap()
    dbg = {}
    if DEBUG:
        for nm, shp in [("xfm", [128, 4096]), ("qpre", [128, 516]),
                        ("qfm", [128, 4096]), ("kn0", [C, 1024]),
                        ("uw0", [C, 2048]), ("o0", [C, 1024]),
                        ("pths", [128, 4096]), ("probs", [128, 32]),
                        ("stat", [128, 64]), ("x1", [128, D]), ("amax", [128, 4])]:
            dbg[nm] = nc.dram_tensor("dbg_" + nm, shp, F32, kind="ExternalOutput").ap()

    x1_spill = nc.dram_tensor("x1_spill", [OWN, D], F32, kind="Internal").ap()
    o_loc = nc.dram_tensor("o_loc", [L, D], BF16, kind="Internal").ap()
    o_stage = nc.dram_tensor("o_stage", [OWN, D], BF16, kind="Internal").ap()
    v_loc = nc.dram_tensor("v_loc", [D, 64 + L], BF16, kind="Internal").ap()
    v_stage = nc.dram_tensor("v_stage", [D, 64 + OWN], BF16, kind="Internal").ap()
    loc_sc = nc.dram_tensor("loc_sc", [1, 32], F32, kind="Internal").ap()

    from contextlib import ExitStack
    with ExitStack() as ctx:
        def sb(name, shape, dt):
            return ctx.enter_context(nc.sbuf_tensor(name, shape, dt))
        def pst(name, shape, dt=F32):
            return ctx.enter_context(nc.psum_tensor(name, shape, dt))
        def sem(name):
            return ctx.enter_context(nc.semaphore(name))

        wbig_h = sb("wbig", [128, 24576], BF16)
        wbig = wbig_h[:, :]
        wq_sb = wbig[:, 0:8192]
        wk_sb = wbig[:, 8192:16384]
        wv_sb = wbig[:, 16384:24576]
        w1_sb = wbig[:, 0:18432]
        fir_a = wbig[:, 18432:19456].bitcast(F32)
        fir_b = wbig[:, 19456:20480].bitcast(F32)
        fir_o = wbig[:, 20480:20992]
        stat_tm = wbig[:, 20992:21248].bitcast(F32)
        stat_fm = wbig[:, 21248:21760]
        hmid = wbig[:, 21760:22784]
        logit_t = wbig[:, 22784:22912].bitcast(F32)
        prob_t = wbig[:, 22912:23040].bitcast(F32)
        sm_a = wbig[:, 23040:23168].bitcast(F32)
        sm_b = wbig[:, 23168:23360].bitcast(F32)
        xs_fm8 = wbig[:, 23360:23872].bitcast(FP8)

        fmw_h = sb("fmw", [128, 18944], BF16)
        fmw = fmw_h[:, :]
        qfm = fmw[:, 0:4096]
        kfm = fmw[:, 4096:8192]
        vfm = fmw[:, 8192:12288]
        wstage = fmw[:, 0:4096].bitcast(F32)
        wo_sb = fmw[:, 0:8192]
        vow = fmw[:, 8192:10752]
        pth_s = fmw[:, 10752:12800]
        pth_v = fmw[:, 12800:14848]
        pth_l = fmw[:, 14848:16896]
        pth_d = fmw[:, 16896:18944]
        wf8_sb = fmw[:, 4096:8192].bitcast(FP8)

        diag_h = sb("diag_u", [128, 12288], BF16)
        diag_u = diag_h[:, :]
        qdiag = diag_u[:, 0:4096]
        kdiag = diag_u[:, 4096:8192]
        vdiag = diag_u[:, 8192:12288]
        comb_a = diag_u[:, 0:1024]
        comb_b = diag_u[:, 1024:2048]
        orms = diag_u[:, 2048:3072]
        ofm = diag_u[:, 3072:4096]
        fbias_sb = diag_u[:, 4096:5120]
        ffn_t = comb_a
        xs_b = orms
        xs_fmb = ofm

        wstage2 = fmw[:, 16896:18944].bitcast(F32)
        logit_f = fir_a[0:16, :]

        cw_q = sb("cw_q", [128, 32], F32)
        cw_k = sb("cw_k", [128, 32], F32)
        cw_v = sb("cw_v", [128, 32], F32)
        firs_sb = sb("firs_sb", [128, 24], F32)
        firl_sb = sb("firl_sb", [128, 504], F32)
        ident4_b = sb("ident4_b", [C, 4 * C], BF16)

        sq_g = fmw[:, 12288:16384]
        kn_fmb = sb("kn_fmb", [128, 512], BF16)
        rsg_a = sb("rsg_a", [C, 64], F32)
        rsg_b = sb("rsg_b", [C, 64], F32)
        rsg_i = sb("rsg_i", [C, 64], I32)
        rsg_s = sb("rsg_s", [C, 64], F32)
        rs_all = sb("rs_all", [C, 64], F32)

        xfm = sb("xfm", [128, 4096], BF16)
        xt_f = sb("xt_f", [128, D], F32)
        xt_b = sb("xt_b", [128, D], BF16)

        pre_q = sb("pre_q", [128, GRP + NH], BF16)
        pre_k = sb("pre_k", [128, GRP + NH], BF16)
        pre_v = sb("pre_v", [128, GRP + NH], BF16)
        halo_q = sb("halo_q", [128, 32], BF16)
        halo_k = sb("halo_k", [128, 32], BF16)
        halo_v = sb("halo_v", [128, 32], BF16)

        qsq = sb("qsq", [128, 512], BF16)
        kn_tm = sb("kn_tm", [C, 4 * DV], BF16)
        u_tm = sb("u_tm", [C, 4 * DV], BF16)
        ui_tm = sb("ui_tm", [C, 4 * DV], BF16)
        w_fm = sb("w_fm", [128, 2 * C], BF16)
        attnT = sb("attnT", [C, C], BF16)
        Lt = sb("Lt", [C, 4 * C], BF16)
        LtT = sb("LtT", [C, 4 * C], BF16)
        Xp = sb("Xp", [C, 4 * C], BF16)
        XpT = sb("XpT", [C, 4 * C], BF16)
        PT = sb("PT", [C, 4 * C], BF16)
        PT2 = sb("PT2", [C, 4 * C], BF16)
        S_sb = sb("S_sb", [128, 4 * 512], BF16)
        bvbk = sb("bvbk", [C, 4 * 512], BF16)
        o_tile = sb("o_tile", [C, D], BF16)
        k_tm = sb("k_tm", [C, D], BF16)
        v_tm = sb("v_tm", [C, D], BF16)
        beta_tm = sb("beta_tm", [C, 8], F32)
        beta_g = sb("beta_g", [4, GRP], F32)
        rs_sc = sb("rs_sc", [128, 16], F32)
        rs_a = sb("rs_a", [128, 16], F32)
        rs_b = sb("rs_b", [128, 16], F32)
        rs_i = sb("rs_i", [128, 16], I32)

        ident_b = sb("ident_b", [128, 128], BF16)
        ident_f = sb("ident_f", [128, 128], F32)
        ones_b = sb("ones_b", [128, 128], BF16)
        mask_low = sb("mask_low", [C, C], BF16)
        mask_upc = sb("mask_upc", [C, C], F32)
        w2_sb = sb("w2_sb", [128, 16 * 16], BF16)
        wb_sb = sb("wb_sb", [128, 32], BF16)
        gb1_sb = sb("gb1_sb", [128, 16], F32)
        bb_sb = sb("bb_sb", [16, 1], F32)
        itemp_sb = sb("itemp_sb", [128, 32], F32)
        floor_sb = sb("floor_sb", [128, 32], F32)
        onw_sb = sb("onw_sb", [128, 8], F32)
        ws2_bc = sb("ws2_bc", [128, 1], F32)
        sc_bc = sb("sc_bc", [128, 2], F32)
        sA = sb("sA", [128, 128], F32)
        sB = sb("sB", [128, 128], F32)
        tstage = sb("tstage", [128, 2 * GRP], F32)
        y_t = sb("y_t", [128, D], F32)
        x1_t = sb("x1_t", [128, D], F32)
        out_t = x1_t
        ksq = y_t[0:C, 0:D]
        ln_sc = sb("ln_sc", [128, 16], F32)
        amax_run = sb("amax_run", [128, 4], F32)
        zero_t = sb("zero_t", [128, 64], BF16)

        pA = pst("pA", [128, 512])
        pB = pst("pB", [128, 512])
        pC_ = pst("pC", [128, 512])
        pD_ = pst("pD", [128, 512])
        pS = [pst("pS%d" % i, [128, 512]) for i in range(4)]
        pT_f = pB[:, 384:512]
        pT_b = pB[:, 320:384].bitcast(BF16)

        s_l0 = sem("s_l0")
        s_l1 = sem("s_l1")
        s_d0 = sem("s_d0")
        s_d1 = sem("s_d1")
        s_t = sem("s_t")
        s_v = sem("s_v")
        s_a = sem("s_a")
        s_g = sem("s_g")
        s_gd = sem("s_gd")
        s_st = sem("s_st")
        s_dbg = sem("s_dbg")

        state = {}

        def I(name, fn, *incs):
            for s_, amt in incs:
                state["cnt"][s_.name] = state["cnt"].get(s_.name, 0) + amt
            if state["cur"] == name:
                ins = fn(state["eng"])
                for s_, amt in incs:
                    ins.then_inc(s_, amt)

        def W(name, s_):
            v = state["cnt"].get(s_.name, 0)
            if state["cur"] == name and v > 0:
                state["eng"].wait_ge(s_, v)

        def DR(name):
            if state["cur"] == name:
                state["eng"].drain()

        def emit(pass_name, eng):
            state["cur"], state["eng"], state["cnt"] = pass_name, eng, {}
            state["rrL"], state["rrD"] = 0, 0
            sy = "sync"; te = "tensor"; ve = "vector"; sc = "scalar"; gp = "gpsimd"

            def ld(dst, src, q=sy):
                s_ = (s_l0, s_l1)[state["rrL"]]
                state["rrL"] ^= 1
                W(q, s_)
                I(q, lambda e: e.dma_start(out=dst, in_=src), (s_, 16))

            def dmat(dst, src):
                s_ = (s_d0, s_d1)[state["rrD"]]
                state["rrD"] ^= 1
                W(sc, s_)
                I(sc, lambda e: e.dma_start_transpose(out=dst, in_=src), (s_, 16))

            def WV1():
                if state["cur"] == "vector":
                    v = state["cnt"].get(s_v.name, 0)
                    if v > 1:
                        state["eng"].wait_ge(s_v, v - 1)

            def WL(name):
                W(name, s_l0)
                W(name, s_l1)

            def WD(name):
                W(name, s_d0)
                W(name, s_d1)

            def vop_(fn, *incs):
                I(ve, fn, *(incs or ((s_v, 1),)))

            def newton4(xcols, ycols, n=4):
                x0, x1_ = xcols
                y0, y1_ = ycols
                vop_(lambda e: e.tensor_scalar(rs_i[:, 0:n], rs_a[:, x0:x1_].bitcast(I32),
                                               10, None, ALU.logical_shift_right))
                DR(ve)
                vop_(lambda e: e.tensor_copy(rs_sc[:, 0:n], rs_i[:, 0:n]))
                DR(ve)
                vop_(lambda e: e.tensor_scalar(rs_sc[:, 4:4 + n], rs_sc[:, 0:n], MAGIC_HI,
                                               -1.0, ALU.subtract, ALU.mult))
                DR(ve)
                vop_(lambda e: e.tensor_copy(rs_i[:, 4:4 + n], rs_sc[:, 4:4 + n]))
                DR(ve)
                vop_(lambda e: e.tensor_scalar(rs_i[:, 8:8 + n], rs_i[:, 4:4 + n], 9, None,
                                               ALU.logical_shift_left))
                DR(ve)
                vop_(lambda e: e.tensor_copy(rs_b[:, y0:y1_], rs_i[:, 8:8 + n].bitcast(F32)))
                DR(ve)
                for it in range(3):
                    vop_(lambda e: e.tensor_tensor(rs_sc[:, 0:n], rs_b[:, y0:y1_],
                                                   rs_b[:, y0:y1_], ALU.mult))
                    DR(ve)
                    vop_(lambda e: e.tensor_tensor(rs_sc[:, 4:4 + n], rs_sc[:, 0:n],
                                                   rs_a[:, x0:x1_], ALU.mult))
                    DR(ve)
                    vop_(lambda e: e.tensor_scalar(rs_sc[:, 8:8 + n], rs_sc[:, 4:4 + n],
                                                   -0.5, 1.5, ALU.mult, ALU.add))
                    DR(ve)
                    vop_(lambda e: e.tensor_tensor(rs_sc[:, 12:12 + n], rs_b[:, y0:y1_],
                                                   rs_sc[:, 8:8 + n], ALU.mult))
                    DR(ve)
                    vop_(lambda e: e.tensor_copy(rs_b[:, y0:y1_], rs_sc[:, 12:12 + n]))
                    DR(ve)

            I(gp, lambda e: e.memset(ident_b[:], 0.0), (s_g, 1))
            DR(gp)
            I(gp, lambda e: e.affine_select(out=ident_b[:], in_=ident_b[:],
                                            compare_op=ALU.not_equal, fill=1.0, base=0,
                                            pattern=[[-1, 128]], channel_multiplier=1), (s_g, 1))
            I(gp, lambda e: e.memset(ident_f[:], 0.0), (s_g, 1))
            DR(gp)
            I(gp, lambda e: e.affine_select(out=ident_f[:], in_=ident_f[:],
                                            compare_op=ALU.not_equal, fill=1.0, base=0,
                                            pattern=[[-1, 128]], channel_multiplier=1), (s_g, 1))
            I(gp, lambda e: e.memset(ones_b[:], 1.0), (s_g, 1))
            I(gp, lambda e: e.memset(mask_low[:], 1.0), (s_g, 1))
            DR(gp)
            I(gp, lambda e: e.affine_select(out=mask_low[:], in_=mask_low[:],
                                            compare_op=ALU.is_gt, fill=0.0, base=0,
                                            pattern=[[-1, C]], channel_multiplier=1), (s_g, 1))
            I(gp, lambda e: e.memset(mask_upc[:], 1.0), (s_g, 1))
            DR(gp)
            I(gp, lambda e: e.affine_select(out=mask_upc[:], in_=mask_upc[:],
                                            compare_op=ALU.is_ge, fill=0.0, base=0,
                                            pattern=[[1, C]], channel_multiplier=-1), (s_g, 1))

            def load_w(dst, src, nkk, ncols, kk0=0):
                for kk in range(nkk):
                    ld(wstage[:, 0:ncols], src[kk * 128:(kk + 1) * 128, :])
                    WL(ve)
                    vop_(lambda e, kk=kk: e.tensor_copy(
                        dst[:, (kk0 + kk) * ncols:(kk0 + kk + 1) * ncols], wstage[:, 0:ncols]))
                    W(sy, s_v)

            load_w(wq_sb, wqT, 8, 1024)
            load_w(wk_sb, wkT, 8, 1024)
            load_w(wv_sb, wvT, 8, 1024)
            load_w(wb_sb, wbT, 8, 4)
            load_w(w2_sb, w2T, 16, 16)

            for kk in range(8):
                ld(wstage[:, 0:1024], wfT[kk * 128:(kk + 1) * 128, :])
                WL(ve)
                if kk == 0:
                    vop_(lambda e: e.tensor_reduce(sA[:, 0:1], wstage[:, 0:1024], AX.X,
                                                   ALU.max, apply_absolute_value=True))
                else:
                    vop_(lambda e: e.tensor_reduce(sA[:, 1:2], wstage[:, 0:1024], AX.X,
                                                   ALU.max, apply_absolute_value=True))
                    DR(ve)
                    vop_(lambda e: e.tensor_tensor(sA[:, 2:3], sA[:, 0:1], sA[:, 1:2], ALU.max))
                    DR(ve)
                    vop_(lambda e: e.tensor_copy(sA[:, 0:1], sA[:, 2:3]))
                    DR(ve)
                W(sy, s_v)
            W(te, s_v)
            W(te, s_g)
            I(te, lambda e: e.transpose(pT_f[0:1, 0:128], sA[:, 0:1], ident_f[:]), (s_t, 1))
            W(ve, s_t)
            vop_(lambda e: e.tensor_reduce(sA[0:1, 3:4], pT_f[0:1, 0:128], AX.X, ALU.max))
            DR(ve)
            vop_(lambda e: e.tensor_scalar(sA[0:1, 4:5], sA[0:1, 3:4], 1e-12, None, ALU.max))
            DR(ve)
            vop_(lambda e: e.reciprocal(sA[0:1, 5:6], sA[0:1, 4:5]))
            DR(ve)
            vop_(lambda e: e.tensor_scalar(sA[0:1, 6:7], sA[0:1, 5:6], 448.0, 448.0, ALU.mult, ALU.min))
            DR(ve)
            vop_(lambda e: e.reciprocal(sA[0:1, 7:8], sA[0:1, 6:7]))
            DR(ve)
            vop_(lambda e: e.tensor_scalar(sA[0:1, 8:9], sA[0:1, 6:7], 0.5, None, ALU.mult))
            DR(ve)
            W(sy, s_v)
            ld(loc_sc[0:1, 0:9], sA[0:1, 0:9])
            WL(sy)
            ld(ws2_bc[:], loc_sc[0:1, 8:9].broadcast_to((128, 1)))

            ld(gb1_sb[:, 0:16], gb1[:])
            ld(bb_sb[:, 0:1], bb[:])
            ld(onw_sb[:, 0:8], onw4[:])
            ld(sA[0:1, 16:20], ltemp[:])
            ld(sA[0:1, 20:36], fraw[:])

            ld(cw_q[:, 0:32], qcw[:])
            ld(cw_k[:, 0:32], kcw[:])
            ld(cw_v[:, 0:32], vcw[:])
            ld(firs_sb[:, 0:24], firs[:])
            ld(firl_sb[:, 0:504], firl[:])

            WL(sc)
            I(sc, lambda e: e.activation(sB[0:1, 0:4], sA[0:1, 16:20], AF.Abs), (s_a, 1))
            DR(sc)
            I(sc, lambda e: e.activation(sB[0:1, 4:8], sB[0:1, 0:4], AF.Exp, scale=-1.0), (s_a, 1))
            DR(sc)
            I(sc, lambda e: e.activation(sB[0:1, 8:12], sB[0:1, 4:8], AF.Ln, bias=1.0), (s_a, 1))
            DR(sc)
            W(ve, s_a)
            vop_(lambda e: e.tensor_scalar(sB[0:1, 12:16], sA[0:1, 16:20], 0.0, None, ALU.max))
            DR(ve)
            vop_(lambda e: e.tensor_tensor(sB[0:1, 16:20], sB[0:1, 8:12], sB[0:1, 12:16], ALU.add))
            DR(ve)
            vop_(lambda e: e.tensor_scalar(sB[0:1, 20:24], sB[0:1, 16:20], 1e-4, None, ALU.add))
            DR(ve)
            vop_(lambda e: e.reciprocal(sB[0:1, 24:28], sB[0:1, 20:24]))
            DR(ve)
            W(sc, s_v)
            I(sc, lambda e: e.activation(sB[0:1, 32:48], sA[0:1, 20:36], AF.Exp, scale=-1.0), (s_a, 1))
            DR(sc)
            W(ve, s_a)
            vop_(lambda e: e.tensor_scalar(sB[0:1, 48:64], sB[0:1, 32:48], 1.0, None, ALU.add))
            DR(ve)
            vop_(lambda e: e.reciprocal(sB[0:1, 64:80], sB[0:1, 48:64]))
            DR(ve)
            vop_(lambda e: e.tensor_scalar(sB[0:1, 80:96], sB[0:1, 64:80], 0.05, None, ALU.mult))
            DR(ve)
            for hh in range(4):
                for pp_ in range(4):
                    vop_(lambda e, hh=hh, pp_=pp_: e.tensor_copy(
                        sB[0:1, 96 + hh * 4 + pp_: 97 + hh * 4 + pp_], sB[0:1, 24 + hh:25 + hh]))
            DR(ve)
            W(sy, s_v)
            ld(loc_sc[0:1, 0:16], sB[0:1, 96:112])
            WL(sy)
            ld(itemp_sb[:, 0:16], loc_sc[0:1, 0:16].broadcast_to((128, 16)))
            ld(loc_sc[0:1, 16:32], sB[0:1, 80:96])
            WL(sy)
            ld(floor_sb[:, 0:16], loc_sc[0:1, 16:32].broadcast_to((128, 16)))
            WL(ve)
            vop_(lambda e: e.tensor_copy(itemp_sb[:, 16:32], itemp_sb[:, 0:16]))
            vop_(lambda e: e.tensor_copy(floor_sb[:, 16:32], floor_sb[:, 0:16]))

            def build_diags(dst, wsrc):
                W(ve, s_g)
                for rb in range(8):
                    for t in range(4):
                        cc0 = (rb * 4 + t) * 128
                        vop_(lambda e, cc0=cc0, rb=rb, t=t, wsrc=wsrc, dst=dst: e.tensor_scalar(
                            dst[:, cc0:cc0 + 128], ones_b[:, 0:128],
                            wsrc[:, rb * 4 + t: rb * 4 + t + 1],
                            None, ALU.mult))
                W(gp, s_v)
                I(gp, lambda e, dst=dst: e.affine_select(
                    out=dst[:, 0:32 * 128], in_=dst[:, 0:32 * 128],
                    compare_op=ALU.is_equal, fill=0.0, base=0,
                    pattern=[[0, 32], [-1, 128]], channel_multiplier=1), (s_g, 1))

            WL(ve)
            build_diags(qdiag, cw_q)
            build_diags(kdiag, cw_k)
            build_diags(vdiag, cw_v)

            for hh in range(4):
                vop_(lambda e, hh=hh: e.tensor_copy(
                    ident4_b[:, hh * C:(hh + 1) * C], ident_b[0:C, 0:C]))

            vop_(lambda e: e.memset(halo_q[:], 0.0))
            vop_(lambda e: e.memset(halo_k[:], 0.0))
            vop_(lambda e: e.memset(halo_v[:], 0.0))
            vop_(lambda e: e.memset(amax_run[:], 0.0))
            vop_(lambda e: e.memset(zero_t[:], 0.0))
            W(sy, s_v)
            for rb in range(8):
                ld(v_loc[rb * 128:(rb + 1) * 128, 0:64], zero_t[:])

            W(sy, s_g)
            W(sy, s_v)
            W(sy, s_a)
            W(te, s_g)
            W(ve, s_g)
            W(sc, s_g)
            W(gp, s_g)

            for g in range(NG_LIM):
                t0 = g * GRP
                WL(ve)
                for ti in range(4):
                    W(sy, s_v)
                    ld(xt_f[:], x_seq[t0 + ti * 128: t0 + (ti + 1) * 128, :])
                    WL(ve)
                    WD(ve)
                    vop_(lambda e: e.tensor_copy(xt_b[:, 0:D], xt_f[:]))
                    W(sc, s_v)
                    W(sc, s_t)
                    for kk in range(8):
                        dmat(xfm[:, kk * GRP + ti * 128: kk * GRP + (ti + 1) * 128],
                             xt_b[:, kk * 128:(kk + 1) * 128])
                WD(te)

                def proj_conv(pre, halo, w_sb, diag, dfm):
                    for rb in range(8):
                        pp = pA if (rb % 2 == 0) else pB
                        for kk in range(8):
                            I(te, lambda e, pp=pp, kk=kk, rb=rb, w_sb=w_sb: e.matmul(
                                pp[:, 0:GRP],
                                w_sb[:, kk * 1024 + rb * 128: kk * 1024 + rb * 128 + 128],
                                xfm[:, kk * GRP:(kk + 1) * GRP],
                                start=(kk == 0), stop=(kk == 7)), (s_t, 1))
                        W(ve, s_t)
                        DR(ve)
                        vop_(lambda e, pre=pre, halo=halo, rb=rb: e.tensor_copy(
                            pre[:, 0:NH], halo[:, rb * 4:(rb + 1) * 4]))
                        DR(ve)
                        vop_(lambda e, pre=pre, pp=pp: e.tensor_copy(
                            pre[:, NH:NH + GRP], pp[:, 0:GRP]))
                        DR(ve)
                        vop_(lambda e, pre=pre, halo=halo, rb=rb: e.tensor_copy(
                            halo[:, rb * 4:(rb + 1) * 4], pre[:, GRP:GRP + NH]))
                        W(te, s_v)
                        ppc = pC_ if (rb % 2 == 0) else pD_
                        for t in range(4):
                            I(te, lambda e, ppc=ppc, diag=diag, rb=rb, t=t, pre=pre:
                              e.matmul(ppc[:, 0:GRP],
                                       diag[:, (rb * 4 + t) * 128:(rb * 4 + t) * 128 + 128],
                                       pre[:, 1 + t: 1 + t + GRP],
                                       start=(t == 0), stop=(t == 3)), (s_t, 1))
                        W(sc, s_t)
                        I(sc, lambda e, ppc=ppc: e.activation(tstage[:, 0:GRP], ppc[:, 0:GRP],
                                                              AF.Tanh, scale=0.5), (s_a, 1))
                        W(ve, s_a)
                        vop_(lambda e: e.tensor_scalar(tstage[:, GRP:2 * GRP], tstage[:, 0:GRP],
                                                       0.5, 0.5, ALU.mult, ALU.add))
                        DR(ve)
                        W(ve, s_t)
                        vop_(lambda e, dfm=dfm, rb=rb, ppc=ppc: e.tensor_tensor(
                            dfm[:, rb * GRP:(rb + 1) * GRP], tstage[:, GRP:2 * GRP],
                            ppc[:, 0:GRP], ALU.mult))
                        W(te, s_v)
                        W(sc, s_v)

                proj_conv(pre_q, halo_q, wq_sb, qdiag, qfm)
                proj_conv(pre_k, halo_k, wk_sb, kdiag, kfm)
                proj_conv(pre_v, halo_v, wv_sb, vdiag, vfm)

                W(sy, s_v)
                for rb in range(8):
                    ld(v_loc[rb * 128:(rb + 1) * 128, 64 + t0: 64 + t0 + GRP],
                       vfm[:, rb * GRP:(rb + 1) * GRP])

                for kk in range(8):
                    I(te, lambda e, kk=kk: e.matmul(
                        pC_[0:4, 0:GRP], wb_sb[:, kk * 4:kk * 4 + 4],
                        xfm[:, kk * GRP:(kk + 1) * GRP],
                        start=(kk == 0), stop=(kk == 7)), (s_t, 1))
                W(sc, s_t)
                I(sc, lambda e: e.activation(tstage[0:4, 0:GRP], pC_[0:4, 0:GRP],
                                             AF.Tanh, scale=0.5), (s_a, 1))
                W(ve, s_a)
                W(ve, s_t)
                vop_(lambda e: e.tensor_scalar(beta_g[0:4, 0:GRP],
                                               tstage[0:4, 0:GRP], 0.5, 0.5,
                                               ALU.mult, ALU.add))
                W(te, s_v)

                if DEBUG and g == 0:
                    W(gp, s_v)
                    W(gp, s_a)
                    WD(gp)
                    W(gp, s_gd)
                    I(gp, lambda e: e.dma_start(out=dbg["xfm"][:], in_=xfm[:, 0:4096]), (s_gd, 16))
                    W(gp, s_gd)
                    I(gp, lambda e: e.dma_start(out=dbg["qpre"][:], in_=pre_q[:, 0:GRP + NH]), (s_gd, 16))
                    W(gp, s_gd)
                    I(gp, lambda e: e.dma_start(out=dbg["qfm"][:], in_=qfm[:, 0:4096]), (s_gd, 16))
                    W(sy, s_dbg)
                    W(ve, s_dbg)
                    W(sc, s_dbg)
                    W(te, s_dbg)
                    W(gp, s_dbg)
                    W(sy, s_gd)
                    W(ve, s_gd)
                    W(sc, s_gd)
                    W(te, s_gd)
                    W(gp, s_gd)

                W(ve, s_t)
                for blk in range(8):
                    vop_(lambda e, blk=blk: e.tensor_tensor(
                        sq_g[:, blk * GRP:(blk + 1) * GRP], qfm[:, blk * GRP:(blk + 1) * GRP],
                        qfm[:, blk * GRP:(blk + 1) * GRP], ALU.mult))
                W(te, s_v)
                for jj in range(8):
                    for hh in range(4):
                        for sub in range(2):
                            I(te, lambda e, jj=jj, hh=hh, sub=sub: e.matmul(
                                pC_[0:C, jj * 8 + hh: jj * 8 + hh + 1],
                                sq_g[:, (hh * 2 + sub) * GRP + jj * C:(hh * 2 + sub) * GRP + (jj + 1) * C],
                                ones_b[:, 0:1], start=(sub == 0), stop=(sub == 1),
                                skip_group_check=True), (s_t, 1))
                W(ve, s_t)
                for blk in range(8):
                    vop_(lambda e, blk=blk: e.tensor_tensor(
                        sq_g[:, blk * GRP:(blk + 1) * GRP], kfm[:, blk * GRP:(blk + 1) * GRP],
                        kfm[:, blk * GRP:(blk + 1) * GRP], ALU.mult))
                W(te, s_v)
                for jj in range(8):
                    for hh in range(4):
                        for sub in range(2):
                            I(te, lambda e, jj=jj, hh=hh, sub=sub: e.matmul(
                                pC_[0:C, jj * 8 + 4 + hh: jj * 8 + 4 + hh + 1],
                                sq_g[:, (hh * 2 + sub) * GRP + jj * C:(hh * 2 + sub) * GRP + (jj + 1) * C],
                                ones_b[:, 0:1], start=(sub == 0), stop=(sub == 1),
                                skip_group_check=True), (s_t, 1))
                W(ve, s_t)
                vop_(lambda e: e.tensor_scalar(rsg_a[0:C, 0:64], pC_[0:C, 0:64],
                                               1e-6, None, ALU.add))
                DR(ve)
                W(te, s_v)
                vop_(lambda e: e.tensor_scalar(rsg_i[0:C, 0:64], rsg_a[0:C, 0:64].bitcast(I32),
                                               10, None, ALU.logical_shift_right))
                DR(ve)
                vop_(lambda e: e.tensor_copy(rsg_b[0:C, 0:64], rsg_i[0:C, 0:64]))
                DR(ve)
                vop_(lambda e: e.tensor_scalar(rsg_s[0:C, 0:64], rsg_b[0:C, 0:64], MAGIC_HI, -1.0,
                                               ALU.subtract, ALU.mult))
                DR(ve)
                vop_(lambda e: e.tensor_copy(rsg_i[0:C, 0:64], rsg_s[0:C, 0:64]))
                DR(ve)
                vop_(lambda e: e.tensor_scalar(rsg_s[0:C, 0:64].bitcast(I32), rsg_i[0:C, 0:64], 9, None,
                                               ALU.logical_shift_left))
                DR(ve)
                vop_(lambda e: e.tensor_copy(rsg_b[0:C, 0:64], rsg_s[0:C, 0:64]))
                DR(ve)
                for it in range(3):
                    vop_(lambda e: e.tensor_tensor(rsg_s[0:C, 0:64], rsg_b[0:C, 0:64],
                                                   rsg_b[0:C, 0:64], ALU.mult))
                    DR(ve)
                    vop_(lambda e: e.tensor_tensor(rs_all[0:C, 0:64], rsg_s[0:C, 0:64],
                                                   rsg_a[0:C, 0:64], ALU.mult))
                    DR(ve)
                    vop_(lambda e: e.tensor_scalar(rsg_s[0:C, 0:64], rs_all[0:C, 0:64],
                                                   -0.5, 1.5, ALU.mult, ALU.add))
                    DR(ve)
                    vop_(lambda e: e.tensor_tensor(rs_all[0:C, 0:64], rsg_b[0:C, 0:64],
                                                   rsg_s[0:C, 0:64], ALU.mult))
                    DR(ve)
                    vop_(lambda e: e.tensor_copy(rsg_b[0:C, 0:64], rs_all[0:C, 0:64]))
                    DR(ve)
                vop_(lambda e: e.tensor_copy(rs_all[0:C, 0:64], rsg_b[0:C, 0:64]))
                DR(ve)

                for jj in range(GRP // C):
                    j = g * (GRP // C) + jj
                    cc = jj * C
                    W(te, s_v)
                    I(te, lambda e, cc=cc: e.transpose(
                        pT_f[0:C, 0:4], beta_g[0:4, cc:cc + C],
                        ident_f[0:4, 0:4]), (s_t, 1))
                    W(ve, s_t)
                    vop_(lambda e: e.tensor_copy(beta_tm[:, 0:4], pT_f[0:C, 0:4]))
                    DR(ve)
                    vop_(lambda e: e.tensor_scalar(beta_tm[:, 4:8], beta_tm[:, 0:4],
                                                   -1.0, None, ALU.mult))
                    for rbk in range(8):
                        W(te, s_v)
                        I(te, lambda e, rbk=rbk, cc=cc: e.transpose(
                            pT_b[0:C, 0:128], kfm[:, rbk * GRP + cc: rbk * GRP + cc + C],
                            ident_b[:]), (s_t, 1))
                        W(ve, s_t)
                        vop_(lambda e, rbk=rbk: e.tensor_copy(
                            k_tm[:, rbk * 128:(rbk + 1) * 128], pT_b[0:C, 0:128]))
                        W(te, s_v)
                        I(te, lambda e, rbk=rbk, cc=cc: e.transpose(
                            pT_b[0:C, 0:128], vfm[:, rbk * GRP + cc: rbk * GRP + cc + C],
                            ident_b[:]), (s_t, 1))
                        W(ve, s_t)
                        vop_(lambda e, rbk=rbk: e.tensor_copy(
                            v_tm[:, rbk * 128:(rbk + 1) * 128], pT_b[0:C, 0:128]))
                    DR(ve)
                    for hh in range(4):
                        vop_(lambda e, hh=hh, jj=jj: e.tensor_scalar(
                            kn_tm[:, hh * DV:(hh + 1) * DV], k_tm[:, hh * DV:(hh + 1) * DV],
                            rs_all[0:C, jj * 8 + 4 + hh: jj * 8 + 5 + hh], None, ALU.mult))
                    DR(ve)
                    for hh in range(4):
                        vop_(lambda e, hh=hh: e.tensor_scalar(
                            bvbk[:, hh * 512:hh * 512 + DV], v_tm[:, hh * DV:(hh + 1) * DV],
                            beta_tm[:, hh:hh + 1], None, ALU.mult))
                    DR(ve)
                    for hh in range(4):
                        vop_(lambda e, hh=hh: e.tensor_scalar(
                            bvbk[:, hh * 512 + DV:hh * 512 + 512],
                            kn_tm[:, hh * DV:(hh + 1) * DV],
                            beta_tm[:, hh:hh + 1], None, ALU.mult))
                    DR(ve)
                    W(te, s_v)
                    for rbk in range(8):
                        I(te, lambda e, rbk=rbk: e.transpose(
                            pT_b[0:128, 0:C],
                            kn_tm[:, rbk * 128:(rbk + 1) * 128],
                            ident_b[0:C, 0:C]), (s_t, 1))
                        W(ve, s_t)
                        vop_(lambda e, rbk=rbk: e.tensor_copy(
                            kn_fmb[:, rbk * C:(rbk + 1) * C], pT_b[0:128, 0:C]))
                        W(te, s_v)
                    kn_fm = kn_fmb
                    DR(ve)

                    if DEBUG and j == 0:
                        W(gp, s_v)
                        W(gp, s_gd)
                        I(gp, lambda e: e.dma_start(out=dbg["kn0"], in_=kn_tm[:]), (s_gd, 16))
                        W(sy, s_dbg)
                        W(ve, s_dbg)
                        W(sc, s_dbg)
                        W(te, s_dbg)
                        W(gp, s_dbg)
                        W(sy, s_gd)
                        W(ve, s_gd)
                        W(sc, s_gd)
                        W(te, s_gd)
                        W(gp, s_gd)

                    for hh in range(4):
                        for sub in range(2):
                            I(te, lambda e, hh=hh, sub=sub: e.matmul(
                                pC_[0:C, hh * C:(hh + 1) * C],
                                kn_fm[:, (hh * 2 + sub) * C:(hh * 2 + sub + 1) * C],
                                kn_fm[:, (hh * 2 + sub) * C:(hh * 2 + sub + 1) * C],
                                start=(sub == 0), stop=(sub == 1),
                                skip_group_check=True), (s_t, 1))
                    W(ve, s_t)
                    for hh in range(4):
                        vop_(lambda e, hh=hh: e.scalar_tensor_tensor(
                            Lt[:, hh * C:(hh + 1) * C], pC_[0:C, hh * C:(hh + 1) * C],
                            beta_tm[:, 4 + hh:5 + hh], mask_low[:], ALU.mult, ALU.mult))
                    W(te, s_v)
                    DR(ve)
                    for hh in range(4):
                        I(te, lambda e, hh=hh: e.transpose(
                            pT_b[0:C, 0:C], Lt[:, hh * C:(hh + 1) * C], ident_b[0:C, 0:C]), (s_t, 1))
                        W(ve, s_t)
                        vop_(lambda e, hh=hh: e.tensor_copy(LtT[:, hh * C:(hh + 1) * C], pT_b[0:C, 0:C]))
                        W(te, s_v)
                    DR(ve)
                    vop_(lambda e: e.tensor_tensor(
                        PT[:, 0:4 * C], ident4_b[:, 0:4 * C], LtT[:, 0:4 * C], ALU.add))
                    vop_(lambda e: e.tensor_copy(Xp[:, 0:4 * C], Lt[:, 0:4 * C]))
                    vop_(lambda e: e.tensor_copy(XpT[:, 0:4 * C], LtT[:, 0:4 * C]))
                    DR(ve)
                    W(te, s_v)
                    for lev in range(5):
                        for hh in range(4):
                            I(te, lambda e, hh=hh: e.matmul(
                                pC_[0:C, hh * C:(hh + 1) * C], XpT[:, hh * C:(hh + 1) * C],
                                Xp[:, hh * C:(hh + 1) * C], start=True, stop=True), (s_t, 1))
                        W(ve, s_t)
                        vop_(lambda e: e.tensor_copy(Xp[:, 0:4 * C], pC_[0:C, 0:4 * C]))
                        DR(ve)
                        W(te, s_v)
                        for hh in range(4):
                            I(te, lambda e, hh=hh: e.transpose(
                                pT_b[0:C, 0:C], Xp[:, hh * C:(hh + 1) * C], ident_b[0:C, 0:C]), (s_t, 1))
                            W(ve, s_t)
                            vop_(lambda e, hh=hh: e.tensor_copy(XpT[:, hh * C:(hh + 1) * C], pT_b[0:C, 0:C]))
                            W(te, s_v)
                        for hh in range(4):
                            I(te, lambda e, hh=hh: e.matmul(
                                pD_[0:C, hh * C:(hh + 1) * C], Xp[:, hh * C:(hh + 1) * C],
                                PT[:, hh * C:(hh + 1) * C], start=True, stop=True), (s_t, 1))
                        W(ve, s_t)
                        DR(ve)
                        vop_(lambda e: e.tensor_tensor(
                            PT2[:, 0:4 * C], pD_[0:C, 0:4 * C], PT[:, 0:4 * C], ALU.add))
                        DR(ve)
                        vop_(lambda e: e.tensor_copy(PT[:, 0:4 * C], PT2[:, 0:4 * C]))
                        DR(ve)
                        W(te, s_v)

                    for hh in range(4):
                        I(te, lambda e, hh=hh: e.matmul(
                            pC_[0:C, 0:DV], PT[:, hh * C:(hh + 1) * C],
                            bvbk[:, hh * 512:hh * 512 + DV], start=True, stop=True), (s_t, 1))
                        for sub in range(2):
                            I(te, lambda e, hh=hh, sub=sub: e.matmul(
                                pD_[0:128, sub * C:(sub + 1) * C],
                                bvbk[:, hh * 512 + DV + sub * 128: hh * 512 + DV + (sub + 1) * 128],
                                PT[:, hh * C:(hh + 1) * C], start=True, stop=True), (s_t, 1))
                        W(ve, s_t)
                        vop_(lambda e, hh=hh: e.tensor_copy(u_tm[:, hh * DV:(hh + 1) * DV],
                                                            pC_[0:C, 0:DV]))
                        vop_(lambda e: e.tensor_copy(w_fm[:, 0:C], pD_[0:128, 0:C]))
                        vop_(lambda e: e.tensor_copy(w_fm[:, C:2 * C], pD_[0:128, C:2 * C]))
                        W(te, s_v)
                        for sub in range(2):
                            I(te, lambda e, hh=hh, sub=sub, cc=cc: e.matmul(
                                pC_[0:C, 0:C],
                                kn_fm[:, (hh * 2 + sub) * C:(hh * 2 + sub + 1) * C],
                                qfm[:, (hh * 2 + sub) * GRP + cc:(hh * 2 + sub) * GRP + cc + C],
                                start=(sub == 0), stop=(sub == 1)), (s_t, 1))
                        W(ve, s_t)
                        DR(ve)
                        vop_(lambda e: e.tensor_tensor(
                            attnT[:, 0:C], pC_[0:C, 0:C], mask_upc[:], ALU.mult))
                        W(te, s_v)
                        if j > 0:
                            for sub in range(2):
                                I(te, lambda e, hh=hh, sub=sub: e.matmul(
                                    pC_[0:C, 0:DV],
                                    w_fm[:, sub * C:(sub + 1) * C],
                                    S_sb[:, hh * 512 + sub * DV: hh * 512 + (sub + 1) * DV],
                                    start=(sub == 0), stop=(sub == 1)), (s_t, 1))
                        W(ve, s_t)
                        DR(ve)
                        if j == 0:
                            vop_(lambda e, hh=hh: e.tensor_copy(
                                ui_tm[:, hh * DV:(hh + 1) * DV], u_tm[:, hh * DV:(hh + 1) * DV]))
                        else:
                            vop_(lambda e, hh=hh: e.tensor_tensor(
                                ui_tm[:, hh * DV:(hh + 1) * DV], u_tm[:, hh * DV:(hh + 1) * DV],
                                pC_[0:C, 0:DV], ALU.subtract))
                        W(te, s_v)
                        if j == 0:
                            I(te, lambda e, hh=hh: e.matmul(
                                pD_[0:C, 0:DV], attnT[:, 0:C],
                                ui_tm[:, hh * DV:(hh + 1) * DV], start=True, stop=True), (s_t, 1))
                        else:
                            for sub in range(2):
                                I(te, lambda e, hh=hh, sub=sub, cc=cc: e.matmul(
                                    pD_[0:C, 0:DV],
                                    qfm[:, (hh * 2 + sub) * GRP + cc:(hh * 2 + sub) * GRP + cc + C],
                                    S_sb[:, hh * 512 + sub * DV: hh * 512 + (sub + 1) * DV],
                                    start=(sub == 0), stop=False), (s_t, 1))
                            I(te, lambda e, hh=hh: e.matmul(
                                pD_[0:C, 0:DV], attnT[:, 0:C],
                                ui_tm[:, hh * DV:(hh + 1) * DV], start=False, stop=True), (s_t, 1))
                        for sub in range(2):
                            I(te, lambda e, hh=hh, sub=sub, j=j: e.matmul(
                                pS[hh][:, sub * DV:(sub + 1) * DV],
                                kn_tm[:, hh * DV + sub * 128: hh * DV + (sub + 1) * 128],
                                ui_tm[:, hh * DV:(hh + 1) * DV],
                                start=(j == 0), stop=False,
                                skip_group_check=True), (s_t, 1))
                        W(ve, s_t)
                        WL(ve)
                        vop_(lambda e, hh=hh, jj=jj: e.tensor_scalar(
                            o_tile[:, hh * DV:(hh + 1) * DV], pD_[0:C, 0:DV],
                            rs_all[0:C, jj * 8 + hh: jj * 8 + hh + 1], None, ALU.mult))
                        vop_(lambda e, hh=hh: e.tensor_copy(
                            S_sb[:, hh * 512:hh * 512 + 512], pS[hh][:, 0:512]))
                        W(te, s_v)

                    if DEBUG and j == 0:
                        W(gp, s_v)
                        W(gp, s_gd)
                        I(gp, lambda e: e.dma_start(
                            out=dbg["uw0"][:, 0:1024], in_=u_tm[:]), (s_gd, 16))
                        W(gp, s_gd)
                        I(gp, lambda e: e.dma_start(
                            out=dbg["uw0"][:, 1024:2048], in_=ui_tm[:]), (s_gd, 16))
                        W(gp, s_gd)
                        I(gp, lambda e: e.dma_start(out=dbg["o0"][:],
                                                    in_=o_tile[:]), (s_gd, 16))
                        W(sy, s_dbg)
                        W(ve, s_dbg)
                        W(sc, s_dbg)
                        W(te, s_dbg)
                        W(gp, s_dbg)
                        W(sy, s_gd)
                        W(ve, s_gd)
                        W(sc, s_gd)
                        W(te, s_gd)
                        W(gp, s_gd)
                    W(sy, s_v)
                    ld(o_loc[j * C:(j + 1) * C, :], o_tile[:])

            me2 = nc.partition_id() % 2
            WL(sy)
            if state["cur"] == sy:
                e = state["eng"]
                e.dma_start(out=o_stage[:, :], in_=o_loc[ts(me2, OWN), :]).then_inc(s_l0, 16)
            state["cnt"][s_l0.name] = state["cnt"].get(s_l0.name, 0) + 16
            for rb in range(8):
                s_dyn = (s_l0, s_l1)[rb % 2]
                W(sy, s_dyn)
                if state["cur"] == sy:
                    e = state["eng"]
                    e.dma_start(out=v_stage[rb * 128:(rb + 1) * 128, :],
                                in_=v_loc[rb * 128:(rb + 1) * 128, bass.ds(me2 * OWN, OWN + 64)]
                                ).then_inc(s_dyn, 16)
                state["cnt"][s_dyn.name] = state["cnt"].get(s_dyn.name, 0) + 16
            WL(sy)

            W(sy, s_t)
            W(sy, s_a)
            W(sy, s_v)
            W(ve, s_t)
            WD(ve)
            load_w(w1_sb, w1T, 8, 2048)
            ld(wstage2[0:32, 0:1024], w1T[1024:1056, 0:1024])
            WL(ve)
            vop_(lambda e: e.tensor_copy(w1_sb[0:32, 8 * 2048:8 * 2048 + 1024], wstage2[0:32, 0:1024]))
            W(sy, s_v)
            ld(wstage2[0:32, 0:1024], w1T[1024:1056, 1024:2048])
            WL(ve)
            vop_(lambda e: e.tensor_copy(w1_sb[0:32, 8 * 2048 + 1024:9 * 2048], wstage2[0:32, 0:1024]))
            W(sy, s_v)
            for kk in range(8):
                ld(wstage2[0:128, 0:1024], woT[kk * 128:(kk + 1) * 128, :])
                WL(ve)
                vop_(lambda e, kk=kk: e.tensor_scalar(
                    wo_sb[:, kk * 1024:(kk + 1) * 1024], wstage2[0:128, 0:1024],
                    onw_sb[:, kk:kk + 1], None, ALU.mult))
                W(sy, s_v)
            W(te, s_v)
            WL(te)
            W(sc, s_v)
            WL(sc)
            WL(gp)

            def xfm_own(r0, n_ti):
                for ti_ in range(n_ti):
                    W(sy, s_v)
                    ld(xt_f[:], x_own[r0 + ti_ * 128: r0 + (ti_ + 1) * 128, :])
                    WL(ve)
                    WD(ve)
                    vop_(lambda e: e.tensor_copy(xt_b[:, 0:D], xt_f[:]))
                    W(sc, s_v)
                    W(sc, s_t)
                    for kk in range(8):
                        dmat(xfm[:, kk * GRP + ti_ * 128: kk * GRP + (ti_ + 1) * 128],
                             xt_b[:, kk * 128:(kk + 1) * 128])
                WD(te)

            for og in range(NOG_LIM):
                ot0 = og * OG
                xfm_own(XHALO + ot0, 2)
                W(sy, s_v)
                for rb in range(8):
                    ld(vow[:, rb * 320:(rb + 1) * 320],
                       v_stage[rb * 128:(rb + 1) * 128, ot0: ot0 + 320])
                for ti in range(2):
                    ld(pth_d[:, ti * D:(ti + 1) * D],
                       o_stage[ot0 + ti * 128: ot0 + (ti + 1) * 128, :])
                WL(ve)
                WD(ve)
                WL(te)
                WD(te)
                WL(sc)

                W(ve, s_t)
                DR(ve)
                for rb in range(8):
                    vb = rb * 320
                    achain = [fir_a[:, 0:OG], fir_a[:, OG:2 * OG]]
                    bchain = [fir_b[:, 0:OG], fir_b[:, OG:2 * OG]]
                    WD(ve)
                    vop_(lambda e, vb=vb, rb=rb: e.tensor_scalar(
                        achain[0], vow[:, vb + 2: vb + 2 + OG],
                        firl_sb[:, rb * 63: rb * 63 + 1], None, ALU.mult))
                    vop_(lambda e, vb=vb, rb=rb: e.tensor_scalar(
                        bchain[0], vow[:, vb + 3: vb + 3 + OG],
                        firl_sb[:, rb * 63 + 1: rb * 63 + 2], None, ALU.mult))
                    ai, bi = 0, 0
                    for t in range(2, 63):
                        WV1()
                        if t % 2 == 0:
                            vop_(lambda e, vb=vb, rb=rb, t=t, ai=ai:
                                 e.scalar_tensor_tensor(achain[1 - ai], vow[:, vb + 2 + t: vb + 2 + t + OG],
                                                        firl_sb[:, rb * 63 + t: rb * 63 + t + 1],
                                                        achain[ai], ALU.mult, ALU.add))
                            ai = 1 - ai
                        else:
                            vop_(lambda e, vb=vb, rb=rb, t=t, bi=bi:
                                 e.scalar_tensor_tensor(bchain[1 - bi], vow[:, vb + 2 + t: vb + 2 + t + OG],
                                                        firl_sb[:, rb * 63 + t: rb * 63 + t + 1],
                                                        bchain[bi], ALU.mult, ALU.add))
                            bi = 1 - bi
                    DR(ve)
                    vop_(lambda e, ai=ai, bi=bi: e.tensor_tensor(
                        fir_o[:, 0:OG], achain[ai], bchain[bi], ALU.add))
                    DR(ve)
                    W(sc, s_v)
                    for ti in range(2):
                        dmat(pth_l[:, ti * D + rb * 128: ti * D + (rb + 1) * 128],
                             fir_o[:, ti * 128:(ti + 1) * 128])
                    vop_(lambda e, vb=vb, rb=rb: e.tensor_scalar(
                        fir_a[:, 0:OG], vow[:, vb + 62: vb + 62 + OG],
                        firs_sb[:, rb * 3: rb * 3 + 1], None, ALU.mult))
                    DR(ve)
                    vop_(lambda e, vb=vb, rb=rb: e.scalar_tensor_tensor(
                        fir_b[:, 0:OG], vow[:, vb + 63: vb + 63 + OG],
                        firs_sb[:, rb * 3 + 1: rb * 3 + 2], fir_a[:, 0:OG], ALU.mult, ALU.add))
                    DR(ve)
                    vop_(lambda e, vb=vb, rb=rb: e.scalar_tensor_tensor(
                        fir_a[:, 0:OG], vow[:, vb + 64: vb + 64 + OG],
                        firs_sb[:, rb * 3 + 2: rb * 3 + 3], fir_b[:, 0:OG], ALU.mult, ALU.add))
                    DR(ve)
                    WD(ve)
                    DR(ve)
                    vop_(lambda e: e.tensor_copy(fir_o[:, 0:OG], fir_a[:, 0:OG]))
                    DR(ve)
                    W(sc, s_v)
                    for ti in range(2):
                        dmat(pth_s[:, ti * D + rb * 128: ti * D + (rb + 1) * 128],
                             fir_o[:, ti * 128:(ti + 1) * 128])
                        dmat(pth_v[:, ti * D + rb * 128: ti * D + (rb + 1) * 128],
                             vow[:, vb + 64 + ti * 128: vb + 64 + (ti + 1) * 128])

                WD(ve)
                for ti in range(2):
                    for pi, pth in enumerate([pth_s, pth_l, pth_d, pth_v]):
                        vop_(lambda e, pth=pth, ti=ti, pi=pi: e.tensor_reduce(
                            stat_tm[:, ti * 32 + 8 * pi: ti * 32 + 8 * pi + 8:2],
                            pth[:, ti * D:(ti + 1) * D].rearrange("p (h dv) -> p h dv", h=4),
                            AX.X, ALU.add))
                        W(sc, s_v)
                        I(sc, lambda e, pth=pth, ti=ti: e.activation(
                            tstage[:, 0:D], pth[:, ti * D:(ti + 1) * D], AF.Square), (s_a, 1))
                        W(ve, s_a)
                        vop_(lambda e, ti=ti, pi=pi: e.tensor_reduce(
                            stat_tm[:, ti * 32 + 8 * pi + 1: ti * 32 + 8 * pi + 8:2],
                            tstage[:, 0:D].rearrange("p (h dv) -> p h dv", h=4),
                            AX.X, ALU.add))
                        DR(ve)
                    vop_(lambda e, ti=ti: e.tensor_scalar(sm_a[:, 0:32], stat_tm[:, ti * 32: ti * 32 + 32],
                                                          1.0 / DV, None, ALU.mult))
                    DR(ve)
                    vop_(lambda e: e.tensor_tensor(sm_b[:, 0:32], sm_a[:, 0:32], sm_a[:, 0:32], ALU.mult))
                    DR(ve)
                    vop_(lambda e, ti=ti: e.tensor_tensor(
                        stat_tm[:, ti * 32 + 1: ti * 32 + 32: 2],
                        sm_a[:, 1:32:2], sm_b[:, 0:32:2], ALU.subtract))
                    vop_(lambda e, ti=ti: e.tensor_copy(
                        stat_tm[:, ti * 32: ti * 32 + 32: 2], sm_a[:, 0:32:2]))
                    DR(ve)
                if DEBUG and og == 0:
                    W(sy, s_v)
                    W(gp, s_v)
                    WD(gp)
                    W(sy, s_dbg)
                    I(sy, lambda e: e.dma_start(out=dbg["stat"], in_=stat_tm[:, 0:64]), (s_dbg, 16))
                    W(gp, s_gd)
                    I(gp, lambda e: e.dma_start(out=dbg["pths"][:, 0:D], in_=pth_s[:, 0:D]), (s_gd, 16))
                    W(gp, s_gd)
                    I(gp, lambda e: e.dma_start(out=dbg["pths"][:, D:2 * D], in_=pth_l[:, 0:D]), (s_gd, 16))
                    W(gp, s_gd)
                    I(gp, lambda e: e.dma_start(out=dbg["pths"][:, 2 * D:3 * D], in_=pth_d[:, 0:D]), (s_gd, 16))
                    W(gp, s_gd)
                    I(gp, lambda e: e.dma_start(out=dbg["pths"][:, 3 * D:4 * D], in_=pth_v[:, 0:D]), (s_gd, 16))
                    W(sy, s_dbg)
                    W(ve, s_dbg)
                    W(sc, s_dbg)
                    W(te, s_dbg)
                    W(gp, s_dbg)
                    W(sy, s_gd)
                    W(ve, s_gd)
                    W(sc, s_gd)
                    W(te, s_gd)
                    W(gp, s_gd)

                W(te, s_v)
                for ti in range(2):
                    I(te, lambda e, ti=ti: e.transpose(
                        pT_f[0:32, 0:128], stat_tm[:, ti * 32:(ti + 1) * 32], ident_f[:]), (s_t, 1))
                    W(ve, s_t)
                    vop_(lambda e, ti=ti: e.tensor_copy(
                        stat_fm[0:32, ti * 128:(ti + 1) * 128], pT_f[0:32, 0:128]))
                    W(te, s_v)

                for m in range(16):
                    pp = pA if (m % 2 == 0) else pB
                    for kk in range(8):
                        I(te, lambda e, pp=pp, m=m, kk=kk: e.matmul(
                            pp[:, 0:OG], w1_sb[:, kk * 2048 + m * 128: kk * 2048 + m * 128 + 128],
                            xfm[:, kk * GRP: kk * GRP + OG],
                            start=(kk == 0), stop=False), (s_t, 1))
                    I(te, lambda e, pp=pp, m=m: e.matmul(
                        pp[:, 0:OG], w1_sb[0:32, 8 * 2048 + m * 128: 8 * 2048 + m * 128 + 128],
                        stat_fm[0:32, 0:OG], start=False, stop=True), (s_t, 1))
                    W(sc, s_t)
                    I(sc, lambda e, pp=pp, m=m: e.activation(
                        hmid[:, (m % 2) * GRP:(m % 2) * GRP + OG], pp[:, 0:OG],
                        AF.Gelu, bias=gb1_sb[:, m:m + 1]), (s_a, 1))
                    W(te, s_a)
                    I(te, lambda e, m=m: e.matmul(
                        pC_[0:16, 0:OG], w2_sb[:, m * 16:(m + 1) * 16],
                        hmid[:, (m % 2) * GRP:(m % 2) * GRP + OG],
                        start=(m == 0), stop=(m == 15)), (s_t, 1))
                W(ve, s_t)
                vop_(lambda e: e.tensor_scalar(logit_f[:, 0:OG], pC_[0:16, 0:OG],
                                               bb_sb[0:16, 0:1], None, ALU.add))
                W(te, s_v)
                for ti in range(2):
                    I(te, lambda e, ti=ti: e.transpose(
                        pT_f[0:128, 0:16], logit_f[:, ti * 128:(ti + 1) * 128],
                        ident_f[0:16, 0:16]), (s_t, 1))
                    W(ve, s_t)
                    vop_(lambda e, ti=ti: e.tensor_copy(logit_t[:, ti * 16:(ti + 1) * 16],
                                                        pT_f[0:128, 0:16]))
                    W(te, s_v)
                DR(ve)
                vop_(lambda e: e.tensor_tensor(sm_a[:, 0:32], logit_t[:, 0:32],
                                               itemp_sb[:, 0:32], ALU.mult))
                DR(ve)
                vop_(lambda e: e.tensor_reduce(sm_b[:, 0:8],
                                               sm_a[:, 0:32].rearrange("p (t q) -> p t q", q=4),
                                               AX.X, ALU.max))
                DR(ve)
                for th in range(8):
                    vop_(lambda e, th=th: e.tensor_scalar(
                        prob_t[:, th * 4:(th + 1) * 4], sm_a[:, th * 4:(th + 1) * 4],
                        sm_b[:, th:th + 1], None, ALU.subtract))
                W(sc, s_v)
                I(sc, lambda e: e.activation(sm_a[:, 32:64], prob_t[:, 0:32], AF.Tanh, scale=0.5), (s_a, 1))
                W(ve, s_a)
                vop_(lambda e: e.tensor_scalar(sm_b[:, 32:64], sm_a[:, 32:64], -1.0, 1.0,
                                               ALU.mult, ALU.add))
                DR(ve)
                vop_(lambda e: e.reciprocal(prob_t[:, 0:32], sm_b[:, 32:64]))
                DR(ve)
                vop_(lambda e: e.tensor_scalar(sm_b[:, 32:64], sm_a[:, 32:64], 1.0, None, ALU.add))
                DR(ve)
                vop_(lambda e: e.tensor_tensor(sm_a[:, 0:32], sm_b[:, 32:64], prob_t[:, 0:32], ALU.mult))
                DR(ve)
                vop_(lambda e: e.tensor_reduce(sm_b[:, 0:8],
                                               sm_a[:, 0:32].rearrange("p (t q) -> p t q", q=4),
                                               AX.X, ALU.add))
                DR(ve)
                vop_(lambda e: e.reciprocal(sm_b[:, 8:16], sm_b[:, 0:8]))
                DR(ve)
                for th in range(8):
                    vop_(lambda e, th=th: e.tensor_scalar(
                        prob_t[:, th * 4:(th + 1) * 4], sm_a[:, th * 4:(th + 1) * 4],
                        sm_b[:, 8 + th:9 + th], None, ALU.mult))
                DR(ve)
                vop_(lambda e: e.tensor_tensor(sm_a[:, 0:32], prob_t[:, 0:32],
                                               floor_sb[:, 0:32], ALU.max))
                DR(ve)
                vop_(lambda e: e.tensor_reduce(sm_b[:, 0:8],
                                               sm_a[:, 0:32].rearrange("p (t q) -> p t q", q=4),
                                               AX.X, ALU.add))
                DR(ve)
                vop_(lambda e: e.reciprocal(sm_b[:, 8:16], sm_b[:, 0:8]))
                DR(ve)
                for th in range(8):
                    vop_(lambda e, th=th: e.tensor_scalar(
                        prob_t[:, th * 4:(th + 1) * 4], sm_a[:, th * 4:(th + 1) * 4],
                        sm_b[:, 8 + th:9 + th], None, ALU.mult))
                DR(ve)
                if DEBUG and og == 0:
                    W(sy, s_v)
                    W(sy, s_dbg)
                    I(sy, lambda e: e.dma_start(out=dbg["probs"], in_=prob_t[:, 0:32]), (s_dbg, 16))
                    W(sy, s_dbg)
                    W(ve, s_dbg)
                    W(sc, s_dbg)
                    W(te, s_dbg)
                    W(gp, s_dbg)
                    W(sy, s_gd)
                    W(ve, s_gd)
                    W(sc, s_gd)
                    W(te, s_gd)
                    W(gp, s_gd)

                for ti in range(2):
                    tcol = ti * D
                    for hh in range(4):
                        vop_(lambda e, hh=hh, tcol=tcol, ti=ti: e.tensor_scalar(
                            comb_a[:, hh * DV:(hh + 1) * DV],
                            pth_s[:, tcol + hh * DV: tcol + (hh + 1) * DV],
                            prob_t[:, ti * 16 + hh * 4: ti * 16 + hh * 4 + 1], None, ALU.mult))
                    DR(ve)
                    for hh in range(4):
                        vop_(lambda e, hh=hh, tcol=tcol, ti=ti: e.scalar_tensor_tensor(
                            comb_b[:, hh * DV:(hh + 1) * DV],
                            pth_l[:, tcol + hh * DV: tcol + (hh + 1) * DV],
                            prob_t[:, ti * 16 + hh * 4 + 1: ti * 16 + hh * 4 + 2],
                            comb_a[:, hh * DV:(hh + 1) * DV], ALU.mult, ALU.add))
                    DR(ve)
                    for hh in range(4):
                        vop_(lambda e, hh=hh, tcol=tcol, ti=ti: e.scalar_tensor_tensor(
                            comb_a[:, hh * DV:(hh + 1) * DV],
                            pth_d[:, tcol + hh * DV: tcol + (hh + 1) * DV],
                            prob_t[:, ti * 16 + hh * 4 + 2: ti * 16 + hh * 4 + 3],
                            comb_b[:, hh * DV:(hh + 1) * DV], ALU.mult, ALU.add))
                    DR(ve)
                    for hh in range(4):
                        vop_(lambda e, hh=hh, tcol=tcol, ti=ti: e.scalar_tensor_tensor(
                            comb_b[:, hh * DV:(hh + 1) * DV],
                            pth_v[:, tcol + hh * DV: tcol + (hh + 1) * DV],
                            prob_t[:, ti * 16 + hh * 4 + 3: ti * 16 + hh * 4 + 4],
                            comb_a[:, hh * DV:(hh + 1) * DV], ALU.mult, ALU.add))
                    DR(ve)
                    vop_(lambda e: e.tensor_tensor(tstage[:, 0:D], comb_b[:, 0:D], comb_b[:, 0:D], ALU.mult))
                    DR(ve)
                    vop_(lambda e: e.tensor_reduce(ln_sc[:, 0:4],
                                                   tstage[:, 0:D].rearrange("p (h dv) -> p h dv", h=4),
                                                   AX.X, ALU.add))
                    DR(ve)
                    vop_(lambda e: e.tensor_scalar(rs_a[:, 0:4], ln_sc[:, 0:4], 1.0 / DV, 1e-5,
                                                   ALU.mult, ALU.add))
                    DR(ve)

                    newton4((0, 4), (0, 4))
                    WD(ve)
                    for hh in range(4):
                        vop_(lambda e, hh=hh: e.tensor_scalar(
                            orms[:, hh * DV:(hh + 1) * DV], comb_b[:, hh * DV:(hh + 1) * DV],
                            rs_b[:, hh:hh + 1], None, ALU.mult))
                    W(sc, s_v)
                    W(sc, s_t)
                    for kk in range(8):
                        dmat(ofm[:, kk * 128:(kk + 1) * 128], orms[:, kk * 128:(kk + 1) * 128])
                    WD(te)
                    for half in range(2):
                        pp = pA if half == 0 else pB
                        for kk in range(8):
                            I(te, lambda e, pp=pp, half=half, kk=kk: e.matmul(
                                pp[:, 0:512], ofm[:, kk * 128:(kk + 1) * 128],
                                wo_sb[:, kk * 1024 + half * 512: kk * 1024 + half * 512 + 512],
                                start=(kk == 0), stop=(kk == 7)), (s_t, 1))
                    W(sy, s_v)
                    ld(xt_f[:], x_own[XHALO + ot0 + ti * 128: XHALO + ot0 + (ti + 1) * 128, :])
                    W(ve, s_t)
                    WL(ve)
                    vop_(lambda e: e.tensor_tensor(y_t[:, 0:512], pA[:, 0:512], xt_f[:, 0:512], ALU.add))
                    vop_(lambda e: e.tensor_tensor(y_t[:, 512:1024], pB[:, 0:512],
                                                   xt_f[:, 512:1024], ALU.add))
                    W(te, s_v)
                    W(ve, s_a)
                    DR(ve)
                    vop_(lambda e: e.tensor_reduce(ln_sc[:, 0:1], y_t[:, 0:D], AX.X, ALU.add))
                    vop_(lambda e: e.tensor_tensor(tstage[:, 0:GRP], y_t[:, 0:GRP], y_t[:, 0:GRP], ALU.mult))
                    vop_(lambda e: e.tensor_tensor(tstage[:, GRP:2 * GRP], y_t[:, GRP:D],
                                                   y_t[:, GRP:D], ALU.mult))
                    DR(ve)
                    vop_(lambda e: e.tensor_reduce(ln_sc[:, 1:2], tstage[:, 0:2 * GRP], AX.X, ALU.add))
                    DR(ve)
                    vop_(lambda e: e.tensor_scalar(ln_sc[:, 2:3], ln_sc[:, 0:1], 1.0 / D, None, ALU.mult))
                    DR(ve)
                    vop_(lambda e: e.tensor_tensor(ln_sc[:, 3:4], ln_sc[:, 2:3], ln_sc[:, 2:3], ALU.mult))
                    vop_(lambda e: e.tensor_scalar(ln_sc[:, 4:5], ln_sc[:, 1:2], 1.0 / D, None, ALU.mult))
                    DR(ve)
                    vop_(lambda e: e.tensor_tensor(ln_sc[:, 5:6], ln_sc[:, 4:5], ln_sc[:, 3:4], ALU.subtract))
                    DR(ve)
                    vop_(lambda e: e.tensor_scalar(rs_a[:, 0:1], ln_sc[:, 5:6], 1.0, 1e-5, ALU.mult, ALU.add))
                    DR(ve)
                    newton4((0, 1), (0, 1), n=1)
                    vop_(lambda e: e.tensor_scalar(ln_sc[:, 10:11], ln_sc[:, 2:3], -1.0, None, ALU.mult))
                    DR(ve)
                    WL(ve)
                    vop_(lambda e: e.tensor_scalar(x1_t[:, 0:D], y_t[:, 0:D],
                                                   ln_sc[:, 10:11], rs_b[:, 0:1], ALU.add, ALU.mult))
                    DR(ve)
                    vop_(lambda e: e.tensor_reduce(ln_sc[:, 12:13], x1_t[:, 0:D], AX.X, ALU.max,
                                                   apply_absolute_value=True))
                    DR(ve)
                    vop_(lambda e: e.tensor_tensor(ln_sc[:, 13:14], amax_run[:, 0:1],
                                                   ln_sc[:, 12:13], ALU.max))
                    DR(ve)
                    vop_(lambda e: e.tensor_copy(amax_run[:, 0:1], ln_sc[:, 13:14]))
                    W(sy, s_v)
                    ld(x1_spill[ot0 + ti * 128: ot0 + (ti + 1) * 128, :], x1_t[:, 0:D])
                    if DEBUG and og == 0 and ti == 0:
                        W(sy, s_dbg)
                        I(sy, lambda e: e.dma_start(out=dbg["x1"], in_=x1_t[:]), (s_dbg, 16))
                        W(sy, s_dbg)
                        W(ve, s_dbg)
                        W(sc, s_dbg)
                        W(te, s_dbg)
                        W(gp, s_dbg)
                        W(sy, s_gd)
                        W(ve, s_gd)
                        W(sc, s_gd)
                        W(te, s_gd)
                        W(gp, s_gd)

            W(te, s_v)
            I(te, lambda e: e.transpose(pT_f[0:1, 0:128], amax_run[:, 0:1], ident_f[:]), (s_t, 1))
            W(ve, s_t)
            vop_(lambda e: e.tensor_reduce(sA[0:1, 40:41], pT_f[0:1, 0:128], AX.X, ALU.max))
            DR(ve)
            vop_(lambda e: e.tensor_scalar(sA[0:1, 43:44], sA[0:1, 40:41], 1e-12, None, ALU.max))
            DR(ve)
            vop_(lambda e: e.reciprocal(sA[0:1, 44:45], sA[0:1, 43:44]))
            DR(ve)
            vop_(lambda e: e.tensor_scalar(sA[0:1, 45:46], sA[0:1, 44:45], 448.0, 448.0,
                                           ALU.mult, ALU.min))
            DR(ve)
            vop_(lambda e: e.reciprocal(sA[0:1, 46:47], sA[0:1, 45:46]))
            DR(ve)
            vop_(lambda e: e.tensor_scalar(sA[0:1, 47:48], sA[0:1, 45:46], 0.5, None, ALU.mult))
            DR(ve)
            vop_(lambda e: e.tensor_tensor(sA[0:1, 48:49], sA[0:1, 46:47], sA[0:1, 7:8], ALU.mult))
            DR(ve)
            vop_(lambda e: e.tensor_scalar(sA[0:1, 49:50], sA[0:1, 48:49], 4.0, None, ALU.mult))
            DR(ve)
            W(sy, s_v)
            ld(loc_sc[0:1, 4:5], sA[0:1, 47:48])
            WL(sy)
            ld(loc_sc[0:1, 6:7], sA[0:1, 49:50])
            WL(sy)
            ld(sc_bc[:, 0:1], loc_sc[0:1, 4:5].broadcast_to((128, 1)))
            ld(sc_bc[:, 1:2], loc_sc[0:1, 6:7].broadcast_to((128, 1)))
            WL(ve)
            if DEBUG:
                W(sy, s_v)
                W(sy, s_dbg)
                I(sy, lambda e: e.dma_start(out=dbg["amax"], in_=amax_run[:]), (s_dbg, 16))

            W(gp, s_v)
            WD(gp)
            W(gp, s_gd)
            I(gp, lambda e: e.dma_start(out=fbias_sb[:],
                                        in_=fbias[None, :].broadcast_to((128, 1024))), (s_gd, 16))
            W(sy, s_t)
            W(sy, s_v)
            for kk in range(8):
                ld(wstage2[:, 0:1024], wfT[kk * 128:(kk + 1) * 128, :])
                WL(ve)
                W(ve, s_t)
                vop_(lambda e, kk=kk: e.tensor_scalar(
                    wf8_sb[:, kk * 1024:(kk + 1) * 1024], wstage2[:, 0:1024],
                    ws2_bc[:, 0:1], None, ALU.mult))
                W(sy, s_v)
            for tt in range(OWN // 128):
                r0 = tt * 128
                W(sy, s_v)
                ld(xt_f[:], x1_spill[r0:r0 + 128, :])
                WL(ve)
                WD(ve)
                vop_(lambda e: e.tensor_scalar(xs_b[:, 0:D], xt_f[:], sc_bc[:, 0:1], None, ALU.mult))
                W(sc, s_v)
                W(sc, s_t)
                for kk in range(8):
                    dmat(xs_fmb[:, kk * 128:(kk + 1) * 128], xs_b[:, kk * 128:(kk + 1) * 128])
                WD(ve)
                vop_(lambda e: e.tensor_copy(xs_fm8[:], xs_fmb[:]))
                W(te, s_v)
                for half in range(2):
                    pp = pA if half == 0 else pB
                    for kk in range(8):
                        I(te, lambda e, pp=pp, half=half, kk=kk: e.matmul(
                            pp[:, 0:512], xs_fm8[:, kk * 128:(kk + 1) * 128],
                            wf8_sb[:, kk * 1024 + half * 512: kk * 1024 + half * 512 + 512],
                            start=(kk == 0), stop=(kk == 7)), (s_t, 1))
                W(ve, s_t)
                W(ve, s_gd)
                vop_(lambda e: e.scalar_tensor_tensor(ffn_t[:, 0:512], pA[:, 0:512],
                                                      sc_bc[:, 1:2], fbias_sb[:, 0:512],
                                                      ALU.mult, ALU.add))
                vop_(lambda e: e.scalar_tensor_tensor(ffn_t[:, 512:1024], pB[:, 0:512],
                                                      sc_bc[:, 1:2], fbias_sb[:, 512:1024],
                                                      ALU.mult, ALU.add))
                W(te, s_v)
                DR(ve)
                vop_(lambda e: e.tensor_tensor(y_t[:, 0:D], xt_f[:], ffn_t[:, 0:D], ALU.add))
                DR(ve)
                vop_(lambda e: e.tensor_reduce(ln_sc[:, 0:1], y_t[:, 0:D], AX.X, ALU.add))
                vop_(lambda e: e.tensor_tensor(tstage[:, 0:GRP], y_t[:, 0:GRP], y_t[:, 0:GRP], ALU.mult))
                vop_(lambda e: e.tensor_tensor(tstage[:, GRP:2 * GRP], y_t[:, GRP:D],
                                               y_t[:, GRP:D], ALU.mult))
                DR(ve)
                vop_(lambda e: e.tensor_reduce(ln_sc[:, 1:2], tstage[:, 0:2 * GRP], AX.X, ALU.add))
                DR(ve)
                vop_(lambda e: e.tensor_scalar(ln_sc[:, 2:3], ln_sc[:, 0:1], 1.0 / D, None, ALU.mult))
                DR(ve)
                vop_(lambda e: e.tensor_tensor(ln_sc[:, 3:4], ln_sc[:, 2:3], ln_sc[:, 2:3], ALU.mult))
                vop_(lambda e: e.tensor_scalar(ln_sc[:, 4:5], ln_sc[:, 1:2], 1.0 / D, None, ALU.mult))
                DR(ve)
                vop_(lambda e: e.tensor_tensor(ln_sc[:, 5:6], ln_sc[:, 4:5], ln_sc[:, 3:4], ALU.subtract))
                DR(ve)
                vop_(lambda e: e.tensor_scalar(rs_a[:, 0:1], ln_sc[:, 5:6], 1.0, 1e-5, ALU.mult, ALU.add))
                DR(ve)
                newton4((0, 1), (0, 1), n=1)
                vop_(lambda e: e.tensor_scalar(ln_sc[:, 10:11], ln_sc[:, 2:3], -1.0, None, ALU.mult))
                DR(ve)
                W(ve, s_st)
                vop_(lambda e: e.tensor_scalar(out_t[:, 0:D], y_t[:, 0:D],
                                               ln_sc[:, 10:11], rs_b[:, 0:1], ALU.add, ALU.mult))
                W(sy, s_v)
                W(sy, s_st)
                I(sy, lambda e, r0=r0: e.dma_start(out=out_d[r0:r0 + 128, :], in_=out_t[:, 0:D]),
                  (s_st, 16))

            W(sy, s_st)
            WL(sy)
            WD(sy)
            W(sy, s_gd)
            W(sy, s_dbg)

        block = ctx.enter_context(nc.Block())

        @block.sync
        def _(sync):
            emit("sync", sync)

        @block.tensor
        def _(tensor):
            emit("tensor", tensor)

        @block.vector
        def _(vector):
            emit("vector", vector)

        @block.scalar
        def _(scalar):
            emit("scalar", scalar)

        @block.gpsimd
        def _(gpsimd):
            emit("gpsimd", gpsimd)

    return nc, out_d


_cache = {}


def kernel(**inputs):
    x = np.ascontiguousarray(inputs["L0_attn_hidden_states"], dtype=np.float32)
    Wq = np.asarray(inputs["L0_attn_Wq"], np.float32)
    Wk = np.asarray(inputs["L0_attn_Wk"], np.float32)
    Wv = np.asarray(inputs["L0_attn_Wv"], np.float32)
    Wb = np.asarray(inputs["L0_attn_Wb"], np.float32)
    qconv = np.asarray(inputs["L0_attn_qconv_w"], np.float32)
    kconv = np.asarray(inputs["L0_attn_kconv_w"], np.float32)
    vconv = np.asarray(inputs["L0_attn_vconv_w"], np.float32)
    fir_s = np.asarray(inputs["L0_attn_fir_short_w"], np.float32).reshape(D, 3)
    fir_l = np.asarray(inputs["L0_attn_fir_long_w"], np.float32).reshape(D, 63)
    g_w1 = np.asarray(inputs["L0_attn_gate_w1"], np.float32)
    g_b1 = np.asarray(inputs["L0_attn_gate_b1"], np.float32)
    g_w2 = np.asarray(inputs["L0_attn_gate_w2"], np.float32)
    log_temp = np.asarray(inputs["L0_attn_log_temp"], np.float32)
    base_bias = np.asarray(inputs["L0_attn_base_bias"], np.float32)
    floor_raw = np.asarray(inputs["L0_attn_floor_raw"], np.float32)
    onorm_w = np.asarray(inputs["L0_attn_onorm_w"], np.float32)
    Wo = np.asarray(inputs["L0_attn_Wo"], np.float32)
    Wf = np.asarray(inputs["L0_ffn_weight"], np.float32)
    fb = np.asarray(inputs["L0_ffn_bias"], np.float32)

    if "nc" not in _cache:
        _cache["nc"] = build_program()
    nc, out_d = _cache["nc"]

    def rbmajor(a):
        rb = a.shape[0] // 128
        return np.ascontiguousarray(a.reshape(rb, 128, -1).transpose(1, 0, 2).reshape(128, -1))

    in_maps = []
    for c in range(8):
        b, me = c // 2, c % 2
        xb_seq = x[b]
        xo = np.zeros((OWN + XHALO, D), np.float32)
        lo = me * OWN - XHALO
        src0 = max(lo, 0)
        xo[src0 - lo:] = xb_seq[src0: me * OWN + OWN]
        in_maps.append({
            "x_seq": xb_seq,
            "x_own": xo,
            "wqT": np.ascontiguousarray(Wq.T),
            "wkT": np.ascontiguousarray(Wk.T),
            "wbT": np.ascontiguousarray(Wb.T),
            "wvT": np.ascontiguousarray(Wv.T),
            "w1T": np.ascontiguousarray(g_w1.T),
            "w2T": np.ascontiguousarray(g_w2.T),
            "woT": np.ascontiguousarray(Wo.T),
            "wfT": np.ascontiguousarray(Wf.T),
            "qcw": rbmajor(qconv),
            "kcw": rbmajor(kconv),
            "vcw": rbmajor(vconv),
            "firs": rbmajor(fir_s),
            "firl": rbmajor(fir_l),
            "gb1": rbmajor(g_b1.reshape(2048, 1)),
            "bb": np.ascontiguousarray(base_bias.reshape(16, 1)),
            "ltemp": log_temp.reshape(1, 4),
            "fraw": floor_raw.reshape(1, 16),
            "onw4": rbmajor(np.tile(onorm_w, H).reshape(D, 1)),
            "fbias": fb,
        })

    res = run_bass_kernel_spmd(nc, in_maps, core_ids=list(range(8)))
    out = np.empty((B, L, D), np.float32)
    for c in range(8):
        b, me = c // 2, c % 2
        out[b, me * OWN:(me + 1) * OWN] = res.results[c]["out"]
    return out


# version 17 (final)
# speedup vs baseline: 1.2966x; 1.1996x over previous
import os
import numpy as np
import concourse.bass as bass
import concourse.mybir as mybir
from concourse.bass_utils import run_bass_kernel_spmd

F32 = mybir.dt.float32
BF16 = mybir.dt.bfloat16
FP8 = mybir.dt.float8e4
I32 = mybir.dt.int32
AF = mybir.ActivationFunctionType
ALU = mybir.AluOpType
AX = mybir.AxisListType
ts = bass.ts

B, L, D, H = 4, 8192, 1024, 4
DV = 256
OWN = 4096
C = 64
NCH = L // C
GRP = 512
NG = L // GRP
OG = 256
NOG = OWN // OG
XHALO = 128
MAGIC_HI = float(0x5F3759DF >> 9)
NH = 4

DEBUG = bool(int(os.environ.get("KBG_DEBUG", "0")))
NG_LIM = int(os.environ.get("KBG_NG", str(NG)))
NOG_LIM = int(os.environ.get("KBG_NOG", str(NOG)))


def build_program():
    nc = bass.Bass("TRN2", target_bir_lowering=False)

    d = {}
    def din(name, shape, dt=F32):
        d[name] = nc.dram_tensor(name, shape, dt, kind="ExternalInput").ap()
        return d[name]

    x_seq = din("x_seq", [L, D])
    x_own = din("x_own", [OWN + XHALO, D])
    wqT = din("wqT", [D, D]); wkT = din("wkT", [D, D]); wbT = din("wbT", [D, 4])
    wvT = din("wvT", [D, D])
    w1T = din("w1T", [1056, 2048]); w2T = din("w2T", [2048, 16])
    woT = din("woT", [D, D]); wfT = din("wfT", [D, D])
    qcw = din("qcw", [128, 32]); kcw = din("kcw", [128, 32]); vcw = din("vcw", [128, 32])
    firs = din("firs", [128, 24]); firl = din("firl", [128, 504])
    gb1 = din("gb1", [128, 16]); bb = din("bb", [16, 1])
    ltemp = din("ltemp", [1, 4]); fraw = din("fraw", [1, 16])
    onw4 = din("onw4", [128, 8]); fbias = din("fbias", [D])

    out_d = nc.dram_tensor("out", [OWN, D], F32, kind="ExternalOutput").ap()
    dbg = {}
    if DEBUG:
        for nm, shp in [("xfm", [128, 4096]), ("qpre", [128, 516]),
                        ("qfm", [128, 4096]), ("kn0", [C, 1024]),
                        ("uw0", [C, 2048]), ("o0", [C, 1024]),
                        ("pths", [128, 4096]), ("probs", [128, 32]),
                        ("stat", [128, 64]), ("x1", [128, D]), ("amax", [128, 4])]:
            dbg[nm] = nc.dram_tensor("dbg_" + nm, shp, F32, kind="ExternalOutput").ap()

    x1_spill = nc.dram_tensor("x1_spill", [OWN, D], F32, kind="Internal").ap()
    o_loc = nc.dram_tensor("o_loc", [L, D], BF16, kind="Internal").ap()
    o_stage = nc.dram_tensor("o_stage", [OWN, D], BF16, kind="Internal").ap()
    v_loc = nc.dram_tensor("v_loc", [D, 64 + L], BF16, kind="Internal").ap()
    v_stage = nc.dram_tensor("v_stage", [D, 64 + OWN], BF16, kind="Internal").ap()
    loc_sc = nc.dram_tensor("loc_sc", [1, 32], F32, kind="Internal").ap()

    from contextlib import ExitStack
    with ExitStack() as ctx:
        def sb(name, shape, dt):
            return ctx.enter_context(nc.sbuf_tensor(name, shape, dt))
        def pst(name, shape, dt=F32):
            return ctx.enter_context(nc.psum_tensor(name, shape, dt))
        def sem(name):
            return ctx.enter_context(nc.semaphore(name))

        wbig_h = sb("wbig", [128, 24576], BF16)
        wbig = wbig_h[:, :]
        wq_sb = wbig[:, 0:8192]
        wk_sb = wbig[:, 8192:16384]
        wv_sb = wbig[:, 16384:24576]
        w1_sb = wbig[:, 0:18432]
        fir_a = wbig[:, 18432:19456].bitcast(F32)
        fir_b = wbig[:, 19456:20480].bitcast(F32)
        fir_o = wbig[:, 20480:20992]
        stat_tm = wbig[:, 20992:21248].bitcast(F32)
        stat_fm = wbig[:, 21248:21760]
        hmid = wbig[:, 21760:22784]
        logit_t = wbig[:, 22784:22912].bitcast(F32)
        prob_t = wbig[:, 22912:23040].bitcast(F32)
        sm_a = wbig[:, 23040:23168].bitcast(F32)
        sm_b = wbig[:, 23168:23360].bitcast(F32)
        xs_fm8 = wbig[:, 23360:23872].bitcast(FP8)

        fmw_h = sb("fmw", [128, 18944], BF16)
        fmw = fmw_h[:, :]
        qfm = fmw[:, 0:4096]
        kfm = fmw[:, 4096:8192]
        vfm = fmw[:, 8192:12288]
        wstage = fmw[:, 0:4096].bitcast(F32)
        wo_sb = fmw[:, 0:8192]
        vow = fmw[:, 8192:10752]
        pth_s = fmw[:, 10752:12800]
        pth_v = fmw[:, 12800:14848]
        pth_l = fmw[:, 14848:16896]
        pth_d = fmw[:, 16896:18944]
        wf8_sb = fmw[:, 4096:8192].bitcast(FP8)

        diag_h = sb("diag_u", [128, 12288], BF16)
        diag_u = diag_h[:, :]
        qdiag = diag_u[:, 0:4096]
        kdiag = diag_u[:, 4096:8192]
        vdiag = diag_u[:, 8192:12288]
        comb_a = diag_u[:, 0:1024]
        comb_b = diag_u[:, 1024:2048]
        orms = diag_u[:, 2048:3072]
        ofm = diag_u[:, 3072:4096]
        fbias_sb = diag_u[:, 4096:5120]
        ffn_t = comb_a
        xs_b = orms
        xs_fmb = ofm

        wstage2 = fmw[:, 16896:18944].bitcast(F32)
        logit_f = fir_a[0:16, :]

        cw_q = sb("cw_q", [128, 32], F32)
        cw_k = sb("cw_k", [128, 32], F32)
        cw_v = sb("cw_v", [128, 32], F32)
        firs_sb = sb("firs_sb", [128, 24], F32)
        firl_sb = sb("firl_sb", [128, 504], F32)
        ident4_b = sb("ident4_b", [C, 4 * C], BF16)

        sq_g = fmw[:, 12288:16384]
        kn_fmb = sb("kn_fmb", [128, 512], BF16)
        rsg_a = sb("rsg_a", [C, 64], F32)
        rsg_b = sb("rsg_b", [C, 64], F32)
        rsg_i = sb("rsg_i", [C, 64], I32)
        rsg_s = sb("rsg_s", [C, 64], F32)
        rs_all = sb("rs_all", [C, 64], F32)

        xfm = sb("xfm", [128, 4096], BF16)
        xt_f = sb("xt_f", [128, D], F32)
        xt_b = sb("xt_b", [128, D], BF16)

        pre_q = sb("pre_q", [128, GRP + NH], BF16)
        pre_k = sb("pre_k", [128, GRP + NH], BF16)
        pre_v = sb("pre_v", [128, GRP + NH], BF16)
        halo_q = sb("halo_q", [128, 32], BF16)
        halo_k = sb("halo_k", [128, 32], BF16)
        halo_v = sb("halo_v", [128, 32], BF16)

        qsq = sb("qsq", [128, 512], BF16)
        kn_tm = sb("kn_tm", [C, 4 * DV], BF16)
        u_tm = sb("u_tm", [C, 4 * DV], BF16)
        ui_tm = sb("ui_tm", [C, 4 * DV], BF16)
        w_fm = sb("w_fm", [128, 2 * C], BF16)
        attnT = sb("attnT", [C, C], BF16)
        Lt = sb("Lt", [C, 4 * C], BF16)
        LtT = sb("LtT", [C, 4 * C], BF16)
        Xp = sb("Xp", [C, 4 * C], BF16)
        XpT = sb("XpT", [C, 4 * C], BF16)
        PT = sb("PT", [C, 4 * C], BF16)
        PT2 = sb("PT2", [C, 4 * C], BF16)
        S_sb = sb("S_sb", [128, 4 * 512], BF16)
        bvbk = sb("bvbk", [C, 4 * 512], BF16)
        o_tile = sb("o_tile", [C, D], BF16)
        k_tm = sb("k_tm", [C, D], BF16)
        v_tm = sb("v_tm", [C, D], BF16)
        beta_tm = sb("beta_tm", [C, 8], F32)
        beta_g = sb("beta_g", [4, GRP], F32)
        rs_sc = sb("rs_sc", [128, 16], F32)
        rs_a = sb("rs_a", [128, 16], F32)
        rs_b = sb("rs_b", [128, 16], F32)
        rs_i = sb("rs_i", [128, 16], I32)

        ident_b = sb("ident_b", [128, 128], BF16)
        ident_f = sb("ident_f", [128, 128], F32)
        ones_b = sb("ones_b", [128, 128], BF16)
        mask_low = sb("mask_low", [C, C], BF16)
        mask_upc = sb("mask_upc", [C, C], F32)
        w2_sb = sb("w2_sb", [128, 16 * 16], BF16)
        wb_sb = sb("wb_sb", [128, 32], BF16)
        gb1_sb = sb("gb1_sb", [128, 16], F32)
        bb_sb = sb("bb_sb", [16, 1], F32)
        itemp_sb = sb("itemp_sb", [128, 32], F32)
        floor_sb = sb("floor_sb", [128, 32], F32)
        onw_sb = sb("onw_sb", [128, 8], F32)
        ws2_bc = sb("ws2_bc", [128, 1], F32)
        sc_bc = sb("sc_bc", [128, 2], F32)
        sA = sb("sA", [128, 128], F32)
        sB = sb("sB", [128, 128], F32)
        tstage = sb("tstage", [128, 2 * GRP], F32)
        y_t = sb("y_t", [128, D], F32)
        x1_t = sb("x1_t", [128, D], F32)
        out_t = x1_t
        ksq = y_t[0:C, 0:D]
        ln_sc = sb("ln_sc", [128, 16], F32)
        amax_run = sb("amax_run", [128, 4], F32)
        zero_t = sb("zero_t", [128, 64], BF16)

        pA = pst("pA", [128, 512])
        pB = pst("pB", [128, 512])
        pC_ = pst("pC", [128, 512])
        pD_ = pst("pD", [128, 512])
        pS = [pst("pS%d" % i, [128, 512]) for i in range(4)]
        pT_f = pB[:, 384:512]
        pT_b = pB[:, 320:384].bitcast(BF16)
        pT_c = pA[:, 448:512].bitcast(BF16)

        s_l0 = sem("s_l0")
        s_l1 = sem("s_l1")
        s_d0 = sem("s_d0")
        s_d1 = sem("s_d1")
        s_t = sem("s_t")
        s_v = sem("s_v")
        s_a = sem("s_a")
        s_g = sem("s_g")
        s_gd = sem("s_gd")
        s_st = sem("s_st")
        s_dbg = sem("s_dbg")

        state = {}

        def I(name, fn, *incs):
            for s_, amt in incs:
                state["cnt"][s_.name] = state["cnt"].get(s_.name, 0) + amt
            if state["cur"] == name:
                ins = fn(state["eng"])
                for s_, amt in incs:
                    ins.then_inc(s_, amt)

        def W(name, s_):
            v = state["cnt"].get(s_.name, 0)
            if state["cur"] == name and v > 0:
                state["eng"].wait_ge(s_, v)

        def WG(name, s_, lag):
            v = state["cnt"].get(s_.name, 0) - lag
            if state["cur"] == name and v > 0:
                state["eng"].wait_ge(s_, v)

        def DR(name):
            if state["cur"] == name:
                state["eng"].drain()

        def emit(pass_name, eng):
            state["cur"], state["eng"], state["cnt"] = pass_name, eng, {}
            state["rrL"], state["rrD"] = 0, 0
            sy = "sync"; te = "tensor"; ve = "vector"; sc = "scalar"; gp = "gpsimd"

            def ld(dst, src, q=sy):
                s_ = (s_l0, s_l1)[state["rrL"]]
                state["rrL"] ^= 1
                W(q, s_)
                I(q, lambda e: e.dma_start(out=dst, in_=src), (s_, 16))

            def dmat(dst, src):
                s_ = (s_d0, s_d1)[state["rrD"]]
                state["rrD"] ^= 1
                W(sc, s_)
                I(sc, lambda e: e.dma_start_transpose(out=dst, in_=src), (s_, 16))

            def WV1():
                if state["cur"] == "vector":
                    v = state["cnt"].get(s_v.name, 0)
                    if v > 1:
                        state["eng"].wait_ge(s_v, v - 1)

            def WL(name):
                W(name, s_l0)
                W(name, s_l1)

            def WD(name):
                W(name, s_d0)
                W(name, s_d1)

            def vop_(fn, *incs):
                I(ve, fn, *(incs or ((s_v, 1),)))

            def newton4(xcols, ycols, n=4):
                x0, x1_ = xcols
                y0, y1_ = ycols
                vop_(lambda e: e.tensor_scalar(rs_i[:, 0:n], rs_a[:, x0:x1_].bitcast(I32),
                                               10, None, ALU.logical_shift_right))
                DR(ve)
                vop_(lambda e: e.tensor_copy(rs_sc[:, 0:n], rs_i[:, 0:n]))
                DR(ve)
                vop_(lambda e: e.tensor_scalar(rs_sc[:, 4:4 + n], rs_sc[:, 0:n], MAGIC_HI,
                                               -1.0, ALU.subtract, ALU.mult))
                DR(ve)
                vop_(lambda e: e.tensor_copy(rs_i[:, 4:4 + n], rs_sc[:, 4:4 + n]))
                DR(ve)
                vop_(lambda e: e.tensor_scalar(rs_i[:, 8:8 + n], rs_i[:, 4:4 + n], 9, None,
                                               ALU.logical_shift_left))
                DR(ve)
                vop_(lambda e: e.tensor_copy(rs_b[:, y0:y1_], rs_i[:, 8:8 + n].bitcast(F32)))
                DR(ve)
                for it in range(2):
                    vop_(lambda e: e.tensor_tensor(rs_sc[:, 0:n], rs_b[:, y0:y1_],
                                                   rs_b[:, y0:y1_], ALU.mult))
                    DR(ve)
                    vop_(lambda e: e.tensor_tensor(rs_sc[:, 4:4 + n], rs_sc[:, 0:n],
                                                   rs_a[:, x0:x1_], ALU.mult))
                    DR(ve)
                    vop_(lambda e: e.tensor_scalar(rs_sc[:, 8:8 + n], rs_sc[:, 4:4 + n],
                                                   -0.5, 1.5, ALU.mult, ALU.add))
                    DR(ve)
                    vop_(lambda e: e.tensor_tensor(rs_sc[:, 12:12 + n], rs_b[:, y0:y1_],
                                                   rs_sc[:, 8:8 + n], ALU.mult))
                    DR(ve)
                    vop_(lambda e: e.tensor_copy(rs_b[:, y0:y1_], rs_sc[:, 12:12 + n]))
                    DR(ve)

            I(gp, lambda e: e.memset(ident_b[:], 0.0), (s_g, 1))
            DR(gp)
            I(gp, lambda e: e.affine_select(out=ident_b[:], in_=ident_b[:],
                                            compare_op=ALU.not_equal, fill=1.0, base=0,
                                            pattern=[[-1, 128]], channel_multiplier=1), (s_g, 1))
            I(gp, lambda e: e.memset(ident_f[:], 0.0), (s_g, 1))
            DR(gp)
            I(gp, lambda e: e.affine_select(out=ident_f[:], in_=ident_f[:],
                                            compare_op=ALU.not_equal, fill=1.0, base=0,
                                            pattern=[[-1, 128]], channel_multiplier=1), (s_g, 1))
            I(gp, lambda e: e.memset(ones_b[:], 1.0), (s_g, 1))
            I(gp, lambda e: e.memset(mask_low[:], 1.0), (s_g, 1))
            DR(gp)
            I(gp, lambda e: e.affine_select(out=mask_low[:], in_=mask_low[:],
                                            compare_op=ALU.is_gt, fill=0.0, base=0,
                                            pattern=[[-1, C]], channel_multiplier=1), (s_g, 1))
            I(gp, lambda e: e.memset(mask_upc[:], 1.0), (s_g, 1))
            DR(gp)
            I(gp, lambda e: e.affine_select(out=mask_upc[:], in_=mask_upc[:],
                                            compare_op=ALU.is_ge, fill=0.0, base=0,
                                            pattern=[[1, C]], channel_multiplier=-1), (s_g, 1))

            def load_w(dst, src, nkk, ncols, kk0=0):
                for kk in range(nkk):
                    ld(wstage[:, 0:ncols], src[kk * 128:(kk + 1) * 128, :])
                    WL(ve)
                    vop_(lambda e, kk=kk: e.tensor_copy(
                        dst[:, (kk0 + kk) * ncols:(kk0 + kk + 1) * ncols], wstage[:, 0:ncols]))
                    W(sy, s_v)

            load_w(wq_sb, wqT, 8, 1024)
            load_w(wk_sb, wkT, 8, 1024)
            load_w(wv_sb, wvT, 8, 1024)
            load_w(wb_sb, wbT, 8, 4)
            load_w(w2_sb, w2T, 16, 16)

            for kk in range(8):
                ld(wstage[:, 0:1024], wfT[kk * 128:(kk + 1) * 128, :])
                WL(ve)
                if kk == 0:
                    vop_(lambda e: e.tensor_reduce(sA[:, 0:1], wstage[:, 0:1024], AX.X,
                                                   ALU.max, apply_absolute_value=True))
                else:
                    vop_(lambda e: e.tensor_reduce(sA[:, 1:2], wstage[:, 0:1024], AX.X,
                                                   ALU.max, apply_absolute_value=True))
                    DR(ve)
                    vop_(lambda e: e.tensor_tensor(sA[:, 2:3], sA[:, 0:1], sA[:, 1:2], ALU.max))
                    DR(ve)
                    vop_(lambda e: e.tensor_copy(sA[:, 0:1], sA[:, 2:3]))
                    DR(ve)
                W(sy, s_v)
            W(te, s_v)
            W(te, s_g)
            I(te, lambda e: e.transpose(pT_f[0:1, 0:128], sA[:, 0:1], ident_f[:]), (s_t, 1))
            W(ve, s_t)
            vop_(lambda e: e.tensor_reduce(sA[0:1, 3:4], pT_f[0:1, 0:128], AX.X, ALU.max))
            DR(ve)
            vop_(lambda e: e.tensor_scalar(sA[0:1, 4:5], sA[0:1, 3:4], 1e-12, None, ALU.max))
            DR(ve)
            vop_(lambda e: e.reciprocal(sA[0:1, 5:6], sA[0:1, 4:5]))
            DR(ve)
            vop_(lambda e: e.tensor_scalar(sA[0:1, 6:7], sA[0:1, 5:6], 448.0, 448.0, ALU.mult, ALU.min))
            DR(ve)
            vop_(lambda e: e.reciprocal(sA[0:1, 7:8], sA[0:1, 6:7]))
            DR(ve)
            vop_(lambda e: e.tensor_scalar(sA[0:1, 8:9], sA[0:1, 6:7], 0.5, None, ALU.mult))
            DR(ve)
            W(sy, s_v)
            ld(loc_sc[0:1, 0:9], sA[0:1, 0:9])
            WL(sy)
            ld(ws2_bc[:], loc_sc[0:1, 8:9].broadcast_to((128, 1)))

            ld(gb1_sb[:, 0:16], gb1[:])
            ld(bb_sb[:, 0:1], bb[:])
            ld(onw_sb[:, 0:8], onw4[:])
            ld(sA[0:1, 16:20], ltemp[:])
            ld(sA[0:1, 20:36], fraw[:])

            ld(cw_q[:, 0:32], qcw[:])
            ld(cw_k[:, 0:32], kcw[:])
            ld(cw_v[:, 0:32], vcw[:])
            ld(firs_sb[:, 0:24], firs[:])
            ld(firl_sb[:, 0:504], firl[:])

            WL(sc)
            I(sc, lambda e: e.activation(sB[0:1, 0:4], sA[0:1, 16:20], AF.Abs), (s_a, 1))
            DR(sc)
            I(sc, lambda e: e.activation(sB[0:1, 4:8], sB[0:1, 0:4], AF.Exp, scale=-1.0), (s_a, 1))
            DR(sc)
            I(sc, lambda e: e.activation(sB[0:1, 8:12], sB[0:1, 4:8], AF.Ln, bias=1.0), (s_a, 1))
            DR(sc)
            W(ve, s_a)
            vop_(lambda e: e.tensor_scalar(sB[0:1, 12:16], sA[0:1, 16:20], 0.0, None, ALU.max))
            DR(ve)
            vop_(lambda e: e.tensor_tensor(sB[0:1, 16:20], sB[0:1, 8:12], sB[0:1, 12:16], ALU.add))
            DR(ve)
            vop_(lambda e: e.tensor_scalar(sB[0:1, 20:24], sB[0:1, 16:20], 1e-4, None, ALU.add))
            DR(ve)
            vop_(lambda e: e.reciprocal(sB[0:1, 24:28], sB[0:1, 20:24]))
            DR(ve)
            W(sc, s_v)
            I(sc, lambda e: e.activation(sB[0:1, 32:48], sA[0:1, 20:36], AF.Exp, scale=-1.0), (s_a, 1))
            DR(sc)
            W(ve, s_a)
            vop_(lambda e: e.tensor_scalar(sB[0:1, 48:64], sB[0:1, 32:48], 1.0, None, ALU.add))
            DR(ve)
            vop_(lambda e: e.reciprocal(sB[0:1, 64:80], sB[0:1, 48:64]))
            DR(ve)
            vop_(lambda e: e.tensor_scalar(sB[0:1, 80:96], sB[0:1, 64:80], 0.05, None, ALU.mult))
            DR(ve)
            for hh in range(4):
                for pp_ in range(4):
                    vop_(lambda e, hh=hh, pp_=pp_: e.tensor_copy(
                        sB[0:1, 96 + hh * 4 + pp_: 97 + hh * 4 + pp_], sB[0:1, 24 + hh:25 + hh]))
            DR(ve)
            W(sy, s_v)
            ld(loc_sc[0:1, 0:16], sB[0:1, 96:112])
            WL(sy)
            ld(itemp_sb[:, 0:16], loc_sc[0:1, 0:16].broadcast_to((128, 16)))
            ld(loc_sc[0:1, 16:32], sB[0:1, 80:96])
            WL(sy)
            ld(floor_sb[:, 0:16], loc_sc[0:1, 16:32].broadcast_to((128, 16)))
            WL(ve)
            vop_(lambda e: e.tensor_copy(itemp_sb[:, 16:32], itemp_sb[:, 0:16]))
            vop_(lambda e: e.tensor_copy(floor_sb[:, 16:32], floor_sb[:, 0:16]))

            def build_diags(dst, wsrc):
                W(ve, s_g)
                for rb in range(8):
                    for t in range(4):
                        cc0 = (rb * 4 + t) * 128
                        vop_(lambda e, cc0=cc0, rb=rb, t=t, wsrc=wsrc, dst=dst: e.tensor_scalar(
                            dst[:, cc0:cc0 + 128], ones_b[:, 0:128],
                            wsrc[:, rb * 4 + t: rb * 4 + t + 1],
                            None, ALU.mult))
                W(gp, s_v)
                I(gp, lambda e, dst=dst: e.affine_select(
                    out=dst[:, 0:32 * 128], in_=dst[:, 0:32 * 128],
                    compare_op=ALU.is_equal, fill=0.0, base=0,
                    pattern=[[0, 32], [-1, 128]], channel_multiplier=1), (s_g, 1))

            WL(ve)
            build_diags(qdiag, cw_q)
            build_diags(kdiag, cw_k)
            build_diags(vdiag, cw_v)

            for hh in range(4):
                vop_(lambda e, hh=hh: e.tensor_copy(
                    ident4_b[:, hh * C:(hh + 1) * C], ident_b[0:C, 0:C]))

            vop_(lambda e: e.memset(halo_q[:], 0.0))
            vop_(lambda e: e.memset(halo_k[:], 0.0))
            vop_(lambda e: e.memset(halo_v[:], 0.0))
            vop_(lambda e: e.memset(amax_run[:], 0.0))
            vop_(lambda e: e.memset(zero_t[:], 0.0))
            W(sy, s_v)
            for rb in range(8):
                ld(v_loc[rb * 128:(rb + 1) * 128, 0:64], zero_t[:])

            W(sy, s_g)
            W(sy, s_v)
            W(sy, s_a)
            W(te, s_g)
            W(ve, s_g)
            W(sc, s_g)
            W(gp, s_g)

            for g in range(NG_LIM):
                t0 = g * GRP
                WL(ve)
                for ti in range(4):
                    W(sy, s_v)
                    ld(xt_f[:], x_seq[t0 + ti * 128: t0 + (ti + 1) * 128, :])
                    WL(ve)
                    WD(ve)
                    vop_(lambda e: e.tensor_copy(xt_b[:, 0:D], xt_f[:]))
                    W(sc, s_v)
                    W(sc, s_t)
                    for kk in range(8):
                        dmat(xfm[:, kk * GRP + ti * 128: kk * GRP + (ti + 1) * 128],
                             xt_b[:, kk * 128:(kk + 1) * 128])
                WD(te)

                def proj_conv(pre, halo, w_sb, diag, dfm):
                    for rb in range(8):
                        pp = pA if (rb % 2 == 0) else pB
                        for kk in range(8):
                            I(te, lambda e, pp=pp, kk=kk, rb=rb, w_sb=w_sb: e.matmul(
                                pp[:, 0:GRP],
                                w_sb[:, kk * 1024 + rb * 128: kk * 1024 + rb * 128 + 128],
                                xfm[:, kk * GRP:(kk + 1) * GRP],
                                start=(kk == 0), stop=(kk == 7)), (s_t, 1))
                        W(ve, s_t)
                        DR(ve)
                        vop_(lambda e, pre=pre, halo=halo, rb=rb: e.tensor_copy(
                            pre[:, 0:NH], halo[:, rb * 4:(rb + 1) * 4]))
                        DR(ve)
                        vop_(lambda e, pre=pre, pp=pp: e.tensor_copy(
                            pre[:, NH:NH + GRP], pp[:, 0:GRP]))
                        DR(ve)
                        vop_(lambda e, pre=pre, halo=halo, rb=rb: e.tensor_copy(
                            halo[:, rb * 4:(rb + 1) * 4], pre[:, GRP:GRP + NH]))
                        W(te, s_v)
                        ppc = pC_ if (rb % 2 == 0) else pD_
                        for t in range(4):
                            I(te, lambda e, ppc=ppc, diag=diag, rb=rb, t=t, pre=pre:
                              e.matmul(ppc[:, 0:GRP],
                                       diag[:, (rb * 4 + t) * 128:(rb * 4 + t) * 128 + 128],
                                       pre[:, 1 + t: 1 + t + GRP],
                                       start=(t == 0), stop=(t == 3)), (s_t, 1))
                        W(sc, s_t)
                        I(sc, lambda e, ppc=ppc: e.activation(tstage[:, 0:GRP], ppc[:, 0:GRP],
                                                              AF.Tanh, scale=0.5), (s_a, 1))
                        W(ve, s_a)
                        vop_(lambda e: e.tensor_scalar(tstage[:, GRP:2 * GRP], tstage[:, 0:GRP],
                                                       0.5, 0.5, ALU.mult, ALU.add))
                        DR(ve)
                        W(ve, s_t)
                        vop_(lambda e, dfm=dfm, rb=rb, ppc=ppc: e.tensor_tensor(
                            dfm[:, rb * GRP:(rb + 1) * GRP], tstage[:, GRP:2 * GRP],
                            ppc[:, 0:GRP], ALU.mult))
                        W(te, s_v)
                        W(sc, s_v)

                proj_conv(pre_q, halo_q, wq_sb, qdiag, qfm)
                proj_conv(pre_k, halo_k, wk_sb, kdiag, kfm)
                proj_conv(pre_v, halo_v, wv_sb, vdiag, vfm)

                W(sy, s_v)
                for rb in range(8):
                    ld(v_loc[rb * 128:(rb + 1) * 128, 64 + t0: 64 + t0 + GRP],
                       vfm[:, rb * GRP:(rb + 1) * GRP])

                for kk in range(8):
                    I(te, lambda e, kk=kk: e.matmul(
                        pC_[0:4, 0:GRP], wb_sb[:, kk * 4:kk * 4 + 4],
                        xfm[:, kk * GRP:(kk + 1) * GRP],
                        start=(kk == 0), stop=(kk == 7)), (s_t, 1))
                W(sc, s_t)
                I(sc, lambda e: e.activation(tstage[0:4, 0:GRP], pC_[0:4, 0:GRP],
                                             AF.Tanh, scale=0.5), (s_a, 1))
                W(ve, s_a)
                W(ve, s_t)
                vop_(lambda e: e.tensor_scalar(beta_g[0:4, 0:GRP],
                                               tstage[0:4, 0:GRP], 0.5, 0.5,
                                               ALU.mult, ALU.add))
                W(te, s_v)

                if DEBUG and g == 0:
                    W(gp, s_v)
                    W(gp, s_a)
                    WD(gp)
                    W(gp, s_gd)
                    I(gp, lambda e: e.dma_start(out=dbg["xfm"][:], in_=xfm[:, 0:4096]), (s_gd, 16))
                    W(gp, s_gd)
                    I(gp, lambda e: e.dma_start(out=dbg["qpre"][:], in_=pre_q[:, 0:GRP + NH]), (s_gd, 16))
                    W(gp, s_gd)
                    I(gp, lambda e: e.dma_start(out=dbg["qfm"][:], in_=qfm[:, 0:4096]), (s_gd, 16))
                    W(sy, s_dbg)
                    W(ve, s_dbg)
                    W(sc, s_dbg)
                    W(te, s_dbg)
                    W(gp, s_dbg)
                    W(sy, s_gd)
                    W(ve, s_gd)
                    W(sc, s_gd)
                    W(te, s_gd)
                    W(gp, s_gd)

                W(ve, s_t)
                for blk in range(8):
                    vop_(lambda e, blk=blk: e.tensor_tensor(
                        sq_g[:, blk * GRP:(blk + 1) * GRP], qfm[:, blk * GRP:(blk + 1) * GRP],
                        qfm[:, blk * GRP:(blk + 1) * GRP], ALU.mult))
                W(te, s_v)
                for jj in range(8):
                    for hh in range(4):
                        for sub in range(2):
                            I(te, lambda e, jj=jj, hh=hh, sub=sub: e.matmul(
                                pC_[0:C, jj * 8 + hh: jj * 8 + hh + 1],
                                sq_g[:, (hh * 2 + sub) * GRP + jj * C:(hh * 2 + sub) * GRP + (jj + 1) * C],
                                ones_b[:, 0:1], start=(sub == 0), stop=(sub == 1),
                                skip_group_check=True), (s_t, 1))
                W(ve, s_t)
                for blk in range(8):
                    vop_(lambda e, blk=blk: e.tensor_tensor(
                        sq_g[:, blk * GRP:(blk + 1) * GRP], kfm[:, blk * GRP:(blk + 1) * GRP],
                        kfm[:, blk * GRP:(blk + 1) * GRP], ALU.mult))
                W(te, s_v)
                for jj in range(8):
                    for hh in range(4):
                        for sub in range(2):
                            I(te, lambda e, jj=jj, hh=hh, sub=sub: e.matmul(
                                pC_[0:C, jj * 8 + 4 + hh: jj * 8 + 4 + hh + 1],
                                sq_g[:, (hh * 2 + sub) * GRP + jj * C:(hh * 2 + sub) * GRP + (jj + 1) * C],
                                ones_b[:, 0:1], start=(sub == 0), stop=(sub == 1),
                                skip_group_check=True), (s_t, 1))
                W(ve, s_t)
                vop_(lambda e: e.tensor_scalar(rsg_a[0:C, 0:64], pC_[0:C, 0:64],
                                               1e-6, None, ALU.add))
                DR(ve)
                W(te, s_v)
                vop_(lambda e: e.tensor_scalar(rsg_i[0:C, 0:64], rsg_a[0:C, 0:64].bitcast(I32),
                                               10, None, ALU.logical_shift_right))
                DR(ve)
                vop_(lambda e: e.tensor_copy(rsg_b[0:C, 0:64], rsg_i[0:C, 0:64]))
                DR(ve)
                vop_(lambda e: e.tensor_scalar(rsg_s[0:C, 0:64], rsg_b[0:C, 0:64], MAGIC_HI, -1.0,
                                               ALU.subtract, ALU.mult))
                DR(ve)
                vop_(lambda e: e.tensor_copy(rsg_i[0:C, 0:64], rsg_s[0:C, 0:64]))
                DR(ve)
                vop_(lambda e: e.tensor_scalar(rsg_s[0:C, 0:64].bitcast(I32), rsg_i[0:C, 0:64], 9, None,
                                               ALU.logical_shift_left))
                DR(ve)
                vop_(lambda e: e.tensor_copy(rsg_b[0:C, 0:64], rsg_s[0:C, 0:64]))
                DR(ve)
                for it in range(2):
                    vop_(lambda e: e.tensor_tensor(rsg_s[0:C, 0:64], rsg_b[0:C, 0:64],
                                                   rsg_b[0:C, 0:64], ALU.mult))
                    DR(ve)
                    vop_(lambda e: e.tensor_tensor(rs_all[0:C, 0:64], rsg_s[0:C, 0:64],
                                                   rsg_a[0:C, 0:64], ALU.mult))
                    DR(ve)
                    vop_(lambda e: e.tensor_scalar(rsg_s[0:C, 0:64], rs_all[0:C, 0:64],
                                                   -0.5, 1.5, ALU.mult, ALU.add))
                    DR(ve)
                    vop_(lambda e: e.tensor_tensor(rs_all[0:C, 0:64], rsg_b[0:C, 0:64],
                                                   rsg_s[0:C, 0:64], ALU.mult))
                    DR(ve)
                    vop_(lambda e: e.tensor_copy(rsg_b[0:C, 0:64], rs_all[0:C, 0:64]))
                    DR(ve)
                vop_(lambda e: e.tensor_copy(rs_all[0:C, 0:64], rsg_b[0:C, 0:64]))
                DR(ve)

                for jj in range(GRP // C):
                    j = g * (GRP // C) + jj
                    cc = jj * C
                    W(te, s_v)
                    I(te, lambda e, cc=cc: e.transpose(
                        pT_f[0:C, 0:4], beta_g[0:4, cc:cc + C],
                        ident_f[0:4, 0:4]), (s_t, 1))
                    W(ve, s_t)
                    vop_(lambda e: e.tensor_copy(beta_tm[:, 0:4], pT_f[0:C, 0:4]))
                    DR(ve)
                    vop_(lambda e: e.tensor_scalar(beta_tm[:, 4:8], beta_tm[:, 0:4],
                                                   -1.0, None, ALU.mult))
                    for it_ in range(16):
                        srcfm = kfm if it_ < 8 else vfm
                        dstt = k_tm if it_ < 8 else v_tm
                        rbk = it_ % 8
                        sl = pT_b if (it_ % 2 == 0) else pT_c
                        WG(te, s_v, 1)
                        I(te, lambda e, rbk=rbk, cc=cc, srcfm=srcfm, sl=sl: e.transpose(
                            sl[0:C, 0:128], srcfm[:, rbk * GRP + cc: rbk * GRP + cc + C],
                            ident_b[:]), (s_t, 1))
                        W(ve, s_t)
                        vop_(lambda e, rbk=rbk, dstt=dstt, sl=sl: e.tensor_copy(
                            dstt[:, rbk * 128:(rbk + 1) * 128], sl[0:C, 0:128]))
                    DR(ve)
                    for hh in range(4):
                        vop_(lambda e, hh=hh, jj=jj: e.tensor_scalar(
                            kn_tm[:, hh * DV:(hh + 1) * DV], k_tm[:, hh * DV:(hh + 1) * DV],
                            rs_all[0:C, jj * 8 + 4 + hh: jj * 8 + 5 + hh], None, ALU.mult))
                    DR(ve)
                    W(te, s_v)
                    for rbk in range(8):
                        sl = pT_b if (rbk % 2 == 0) else pT_c
                        if rbk >= 2:
                            WG(te, s_v, 1)
                        I(te, lambda e, rbk=rbk, sl=sl: e.transpose(
                            sl[0:128, 0:C],
                            kn_tm[:, rbk * 128:(rbk + 1) * 128],
                            ident_b[0:C, 0:C]), (s_t, 1))
                        W(ve, s_t)
                        vop_(lambda e, rbk=rbk, sl=sl: e.tensor_copy(
                            kn_fmb[:, rbk * C:(rbk + 1) * C], sl[0:128, 0:C]))
                    kn_fm = kn_fmb
                    W(te, s_v)
                    DR(ve)
                    for hh in range(4):
                        vop_(lambda e, hh=hh: e.tensor_scalar(
                            bvbk[:, hh * 512:hh * 512 + DV], v_tm[:, hh * DV:(hh + 1) * DV],
                            beta_tm[:, hh:hh + 1], None, ALU.mult))
                    DR(ve)

                    if DEBUG and j == 0:
                        W(gp, s_v)
                        W(gp, s_gd)
                        I(gp, lambda e: e.dma_start(out=dbg["kn0"], in_=kn_tm[:]), (s_gd, 16))
                        W(sy, s_dbg)
                        W(ve, s_dbg)
                        W(sc, s_dbg)
                        W(te, s_dbg)
                        W(gp, s_dbg)
                        W(sy, s_gd)
                        W(ve, s_gd)
                        W(sc, s_gd)
                        W(te, s_gd)
                        W(gp, s_gd)

                    for hh in range(4):
                        for sub in range(2):
                            I(te, lambda e, hh=hh, sub=sub: e.matmul(
                                pC_[0:C, hh * C:(hh + 1) * C],
                                kn_fm[:, (hh * 2 + sub) * C:(hh * 2 + sub + 1) * C],
                                kn_fm[:, (hh * 2 + sub) * C:(hh * 2 + sub + 1) * C],
                                start=(sub == 0), stop=(sub == 1),
                                skip_group_check=True), (s_t, 1))
                    W(ve, s_t)
                    for hh in range(4):
                        vop_(lambda e, hh=hh: e.scalar_tensor_tensor(
                            Lt[:, hh * C:(hh + 1) * C], pC_[0:C, hh * C:(hh + 1) * C],
                            beta_tm[:, 4 + hh:5 + hh], mask_low[:], ALU.mult, ALU.mult))
                    W(te, s_v)
                    DR(ve)
                    for hh in range(4):
                        sl = pT_b if (hh % 2 == 0) else pT_c
                        if hh >= 2:
                            WG(te, s_v, 1)
                        I(te, lambda e, hh=hh, sl=sl: e.transpose(
                            sl[0:C, 0:C], Lt[:, hh * C:(hh + 1) * C], ident_b[0:C, 0:C]), (s_t, 1))
                        W(ve, s_t)
                        vop_(lambda e, hh=hh, sl=sl: e.tensor_copy(LtT[:, hh * C:(hh + 1) * C], sl[0:C, 0:C]))
                    DR(ve)
                    vop_(lambda e: e.tensor_tensor(
                        PT[:, 0:4 * C], ident4_b[:, 0:4 * C], LtT[:, 0:4 * C], ALU.add))
                    DR(ve)
                    W(te, s_v)
                    for lev in range(5):
                        for hh in range(4):
                            I(te, lambda e, hh=hh: e.matmul(
                                pC_[0:C, hh * C:(hh + 1) * C], LtT[:, hh * C:(hh + 1) * C],
                                Lt[:, hh * C:(hh + 1) * C], start=True, stop=True), (s_t, 1))
                        if lev < 4:
                            for hh in range(4):
                                I(te, lambda e, hh=hh: e.matmul(
                                    pC_[0:C, 256 + hh * C: 256 + (hh + 1) * C],
                                    Lt[:, hh * C:(hh + 1) * C],
                                    LtT[:, hh * C:(hh + 1) * C], start=True, stop=True), (s_t, 1))
                        W(ve, s_t)
                        DR(ve)
                        vop_(lambda e: e.tensor_copy(Lt[:, 0:4 * C], pC_[0:C, 0:4 * C]))
                        if lev < 4:
                            vop_(lambda e: e.tensor_copy(LtT[:, 0:4 * C], pC_[0:C, 256:256 + 4 * C]))
                        DR(ve)
                        W(te, s_v)
                        for hh in range(4):
                            I(te, lambda e, hh=hh: e.matmul(
                                pD_[0:C, hh * C:(hh + 1) * C], Lt[:, hh * C:(hh + 1) * C],
                                PT[:, hh * C:(hh + 1) * C], start=True, stop=True), (s_t, 1))
                        W(ve, s_t)
                        DR(ve)
                        vop_(lambda e: e.tensor_tensor(
                            PT[:, 0:4 * C], pD_[0:C, 0:4 * C], PT[:, 0:4 * C], ALU.add))
                        DR(ve)
                        W(te, s_v)

                    for hh in range(4):
                        if hh > 0:
                            WG(te, s_v, 2)
                        if j > 0:
                            for sub in range(2):
                                I(te, lambda e, hh=hh, sub=sub: e.matmul(
                                    pB[0:C, 0:DV],
                                    kn_fm[:, (hh * 2 + sub) * C:(hh * 2 + sub + 1) * C],
                                    S_sb[:, hh * 512 + sub * DV: hh * 512 + (sub + 1) * DV],
                                    start=(sub == 0), stop=(sub == 1)), (s_t, 1))
                        for sub in range(2):
                            I(te, lambda e, hh=hh, sub=sub, cc=cc: e.matmul(
                                pC_[0:C, 256:256 + C],
                                kn_fm[:, (hh * 2 + sub) * C:(hh * 2 + sub + 1) * C],
                                qfm[:, (hh * 2 + sub) * GRP + cc:(hh * 2 + sub) * GRP + cc + C],
                                start=(sub == 0), stop=(sub == 1)), (s_t, 1))
                        W(ve, s_t)
                        DR(ve)
                        if j > 0:
                            vop_(lambda e, hh=hh: e.scalar_tensor_tensor(
                                u_tm[:, 0:DV], pB[0:C, 0:DV],
                                beta_tm[:, 4 + hh:5 + hh],
                                bvbk[:, hh * 512:hh * 512 + DV], ALU.mult, ALU.add))
                        vop_(lambda e: e.tensor_tensor(
                            attnT[:, 0:C], pC_[0:C, 256:256 + C], mask_upc[:], ALU.mult))
                        DR(ve)
                        W(te, s_v)
                        ui_src = u_tm[:, 0:DV] if j > 0 else bvbk[:, hh * 512:hh * 512 + DV]
                        I(te, lambda e, hh=hh, ui_src=ui_src: e.matmul(
                            pC_[0:C, 0:DV], PT[:, hh * C:(hh + 1) * C],
                            ui_src, start=True, stop=True), (s_t, 1))
                        W(ve, s_t)
                        DR(ve)
                        vop_(lambda e, hh=hh: e.tensor_copy(
                            ui_tm[:, hh * DV:(hh + 1) * DV], pC_[0:C, 0:DV]))
                        DR(ve)
                        W(te, s_v)
                        if j == 0:
                            I(te, lambda e, hh=hh: e.matmul(
                                pD_[0:C, 256:256 + DV], attnT[:, 0:C],
                                ui_tm[:, hh * DV:(hh + 1) * DV], start=True, stop=True), (s_t, 1))
                        else:
                            for sub in range(2):
                                I(te, lambda e, hh=hh, sub=sub, cc=cc: e.matmul(
                                    pD_[0:C, 256:256 + DV],
                                    qfm[:, (hh * 2 + sub) * GRP + cc:(hh * 2 + sub) * GRP + cc + C],
                                    S_sb[:, hh * 512 + sub * DV: hh * 512 + (sub + 1) * DV],
                                    start=(sub == 0), stop=False), (s_t, 1))
                            I(te, lambda e, hh=hh: e.matmul(
                                pD_[0:C, 256:256 + DV], attnT[:, 0:C],
                                ui_tm[:, hh * DV:(hh + 1) * DV], start=False, stop=True), (s_t, 1))
                        for sub in range(2):
                            I(te, lambda e, hh=hh, sub=sub, j=j: e.matmul(
                                pS[hh][:, sub * DV:(sub + 1) * DV],
                                kn_tm[:, hh * DV + sub * 128: hh * DV + (sub + 1) * 128],
                                ui_tm[:, hh * DV:(hh + 1) * DV],
                                start=(j == 0), stop=False,
                                skip_group_check=True), (s_t, 1))
                        W(ve, s_t)
                        WL(ve)
                        vop_(lambda e, hh=hh, jj=jj: e.tensor_scalar(
                            o_tile[:, hh * DV:(hh + 1) * DV], pD_[0:C, 256:256 + DV],
                            rs_all[0:C, jj * 8 + hh: jj * 8 + hh + 1], None, ALU.mult))
                        vop_(lambda e, hh=hh: e.tensor_copy(
                            S_sb[:, hh * 512:hh * 512 + 512], pS[hh][:, 0:512]))

                    if DEBUG and j == 0:
                        W(gp, s_v)
                        W(gp, s_gd)
                        I(gp, lambda e: e.dma_start(
                            out=dbg["uw0"][:, 1024:2048], in_=ui_tm[:]), (s_gd, 16))
                        W(gp, s_gd)
                        I(gp, lambda e: e.dma_start(out=dbg["o0"][:],
                                                    in_=o_tile[:]), (s_gd, 16))
                        W(sy, s_dbg)
                        W(ve, s_dbg)
                        W(sc, s_dbg)
                        W(te, s_dbg)
                        W(gp, s_dbg)
                        W(sy, s_gd)
                        W(ve, s_gd)
                        W(sc, s_gd)
                        W(te, s_gd)
                        W(gp, s_gd)
                    W(sy, s_v)
                    ld(o_loc[j * C:(j + 1) * C, :], o_tile[:])

            me2 = nc.partition_id() % 2
            WL(sy)
            if state["cur"] == sy:
                e = state["eng"]
                e.dma_start(out=o_stage[:, :], in_=o_loc[ts(me2, OWN), :]).then_inc(s_l0, 16)
            state["cnt"][s_l0.name] = state["cnt"].get(s_l0.name, 0) + 16
            for rb in range(8):
                s_dyn = (s_l0, s_l1)[rb % 2]
                W(sy, s_dyn)
                if state["cur"] == sy:
                    e = state["eng"]
                    e.dma_start(out=v_stage[rb * 128:(rb + 1) * 128, :],
                                in_=v_loc[rb * 128:(rb + 1) * 128, bass.ds(me2 * OWN, OWN + 64)]
                                ).then_inc(s_dyn, 16)
                state["cnt"][s_dyn.name] = state["cnt"].get(s_dyn.name, 0) + 16
            WL(sy)

            W(sy, s_t)
            W(sy, s_a)
            W(sy, s_v)
            W(ve, s_t)
            WD(ve)
            load_w(w1_sb, w1T, 8, 2048)
            ld(wstage2[0:32, 0:1024], w1T[1024:1056, 0:1024])
            WL(ve)
            vop_(lambda e: e.tensor_copy(w1_sb[0:32, 8 * 2048:8 * 2048 + 1024], wstage2[0:32, 0:1024]))
            W(sy, s_v)
            ld(wstage2[0:32, 0:1024], w1T[1024:1056, 1024:2048])
            WL(ve)
            vop_(lambda e: e.tensor_copy(w1_sb[0:32, 8 * 2048 + 1024:9 * 2048], wstage2[0:32, 0:1024]))
            W(sy, s_v)
            for kk in range(8):
                ld(wstage2[0:128, 0:1024], woT[kk * 128:(kk + 1) * 128, :])
                WL(ve)
                vop_(lambda e, kk=kk: e.tensor_scalar(
                    wo_sb[:, kk * 1024:(kk + 1) * 1024], wstage2[0:128, 0:1024],
                    onw_sb[:, kk:kk + 1], None, ALU.mult))
                W(sy, s_v)
            W(te, s_v)
            WL(te)
            W(sc, s_v)
            WL(sc)
            WL(gp)

            def xfm_own(r0, n_ti):
                for ti_ in range(n_ti):
                    W(sy, s_v)
                    ld(xt_f[:], x_own[r0 + ti_ * 128: r0 + (ti_ + 1) * 128, :])
                    WL(ve)
                    WD(ve)
                    vop_(lambda e: e.tensor_copy(xt_b[:, 0:D], xt_f[:]))
                    W(sc, s_v)
                    W(sc, s_t)
                    for kk in range(8):
                        dmat(xfm[:, kk * GRP + ti_ * 128: kk * GRP + (ti_ + 1) * 128],
                             xt_b[:, kk * 128:(kk + 1) * 128])
                WD(te)

            for og in range(NOG_LIM):
                ot0 = og * OG
                xfm_own(XHALO + ot0, 2)
                W(sy, s_v)
                for rb in range(8):
                    ld(vow[:, rb * 320:(rb + 1) * 320],
                       v_stage[rb * 128:(rb + 1) * 128, ot0: ot0 + 320])
                for ti in range(2):
                    ld(pth_d[:, ti * D:(ti + 1) * D],
                       o_stage[ot0 + ti * 128: ot0 + (ti + 1) * 128, :])
                WL(ve)
                WD(ve)
                WL(te)
                WD(te)
                WL(sc)

                W(ve, s_t)
                DR(ve)
                for rb in range(8):
                    vb = rb * 320
                    achain = [fir_a[:, 0:OG], fir_a[:, OG:2 * OG]]
                    bchain = [fir_b[:, 0:OG], fir_b[:, OG:2 * OG]]
                    WD(ve)
                    vop_(lambda e, vb=vb, rb=rb: e.tensor_scalar(
                        achain[0], vow[:, vb + 2: vb + 2 + OG],
                        firl_sb[:, rb * 63: rb * 63 + 1], None, ALU.mult))
                    vop_(lambda e, vb=vb, rb=rb: e.tensor_scalar(
                        bchain[0], vow[:, vb + 3: vb + 3 + OG],
                        firl_sb[:, rb * 63 + 1: rb * 63 + 2], None, ALU.mult))
                    ai, bi = 0, 0
                    for t in range(2, 63):
                        WV1()
                        if t % 2 == 0:
                            vop_(lambda e, vb=vb, rb=rb, t=t, ai=ai:
                                 e.scalar_tensor_tensor(achain[1 - ai], vow[:, vb + 2 + t: vb + 2 + t + OG],
                                                        firl_sb[:, rb * 63 + t: rb * 63 + t + 1],
                                                        achain[ai], ALU.mult, ALU.add))
                            ai = 1 - ai
                        else:
                            vop_(lambda e, vb=vb, rb=rb, t=t, bi=bi:
                                 e.scalar_tensor_tensor(bchain[1 - bi], vow[:, vb + 2 + t: vb + 2 + t + OG],
                                                        firl_sb[:, rb * 63 + t: rb * 63 + t + 1],
                                                        bchain[bi], ALU.mult, ALU.add))
                            bi = 1 - bi
                    DR(ve)
                    vop_(lambda e, ai=ai, bi=bi: e.tensor_tensor(
                        fir_o[:, 0:OG], achain[ai], bchain[bi], ALU.add))
                    DR(ve)
                    W(sc, s_v)
                    for ti in range(2):
                        dmat(pth_l[:, ti * D + rb * 128: ti * D + (rb + 1) * 128],
                             fir_o[:, ti * 128:(ti + 1) * 128])
                    vop_(lambda e, vb=vb, rb=rb: e.tensor_scalar(
                        fir_a[:, 0:OG], vow[:, vb + 62: vb + 62 + OG],
                        firs_sb[:, rb * 3: rb * 3 + 1], None, ALU.mult))
                    DR(ve)
                    vop_(lambda e, vb=vb, rb=rb: e.scalar_tensor_tensor(
                        fir_b[:, 0:OG], vow[:, vb + 63: vb + 63 + OG],
                        firs_sb[:, rb * 3 + 1: rb * 3 + 2], fir_a[:, 0:OG], ALU.mult, ALU.add))
                    DR(ve)
                    vop_(lambda e, vb=vb, rb=rb: e.scalar_tensor_tensor(
                        fir_a[:, 0:OG], vow[:, vb + 64: vb + 64 + OG],
                        firs_sb[:, rb * 3 + 2: rb * 3 + 3], fir_b[:, 0:OG], ALU.mult, ALU.add))
                    DR(ve)
                    WD(ve)
                    DR(ve)
                    vop_(lambda e: e.tensor_copy(fir_o[:, 0:OG], fir_a[:, 0:OG]))
                    DR(ve)
                    W(sc, s_v)
                    for ti in range(2):
                        dmat(pth_s[:, ti * D + rb * 128: ti * D + (rb + 1) * 128],
                             fir_o[:, ti * 128:(ti + 1) * 128])
                        dmat(pth_v[:, ti * D + rb * 128: ti * D + (rb + 1) * 128],
                             vow[:, vb + 64 + ti * 128: vb + 64 + (ti + 1) * 128])

                WD(ve)
                for ti in range(2):
                    for pi, pth in enumerate([pth_s, pth_l, pth_d, pth_v]):
                        vop_(lambda e, pth=pth, ti=ti, pi=pi: e.tensor_reduce(
                            stat_tm[:, ti * 32 + 8 * pi: ti * 32 + 8 * pi + 8:2],
                            pth[:, ti * D:(ti + 1) * D].rearrange("p (h dv) -> p h dv", h=4),
                            AX.X, ALU.add))
                        W(sc, s_v)
                        I(sc, lambda e, pth=pth, ti=ti: e.activation(
                            tstage[:, 0:D], pth[:, ti * D:(ti + 1) * D], AF.Square), (s_a, 1))
                        W(ve, s_a)
                        vop_(lambda e, ti=ti, pi=pi: e.tensor_reduce(
                            stat_tm[:, ti * 32 + 8 * pi + 1: ti * 32 + 8 * pi + 8:2],
                            tstage[:, 0:D].rearrange("p (h dv) -> p h dv", h=4),
                            AX.X, ALU.add))
                        DR(ve)
                    vop_(lambda e, ti=ti: e.tensor_scalar(sm_a[:, 0:32], stat_tm[:, ti * 32: ti * 32 + 32],
                                                          1.0 / DV, None, ALU.mult))
                    DR(ve)
                    vop_(lambda e: e.tensor_tensor(sm_b[:, 0:32], sm_a[:, 0:32], sm_a[:, 0:32], ALU.mult))
                    DR(ve)
                    vop_(lambda e, ti=ti: e.tensor_tensor(
                        stat_tm[:, ti * 32 + 1: ti * 32 + 32: 2],
                        sm_a[:, 1:32:2], sm_b[:, 0:32:2], ALU.subtract))
                    vop_(lambda e, ti=ti: e.tensor_copy(
                        stat_tm[:, ti * 32: ti * 32 + 32: 2], sm_a[:, 0:32:2]))
                    DR(ve)
                if DEBUG and og == 0:
                    W(sy, s_v)
                    W(gp, s_v)
                    WD(gp)
                    W(sy, s_dbg)
                    I(sy, lambda e: e.dma_start(out=dbg["stat"], in_=stat_tm[:, 0:64]), (s_dbg, 16))
                    W(gp, s_gd)
                    I(gp, lambda e: e.dma_start(out=dbg["pths"][:, 0:D], in_=pth_s[:, 0:D]), (s_gd, 16))
                    W(gp, s_gd)
                    I(gp, lambda e: e.dma_start(out=dbg["pths"][:, D:2 * D], in_=pth_l[:, 0:D]), (s_gd, 16))
                    W(gp, s_gd)
                    I(gp, lambda e: e.dma_start(out=dbg["pths"][:, 2 * D:3 * D], in_=pth_d[:, 0:D]), (s_gd, 16))
                    W(gp, s_gd)
                    I(gp, lambda e: e.dma_start(out=dbg["pths"][:, 3 * D:4 * D], in_=pth_v[:, 0:D]), (s_gd, 16))
                    W(sy, s_dbg)
                    W(ve, s_dbg)
                    W(sc, s_dbg)
                    W(te, s_dbg)
                    W(gp, s_dbg)
                    W(sy, s_gd)
                    W(ve, s_gd)
                    W(sc, s_gd)
                    W(te, s_gd)
                    W(gp, s_gd)

                W(te, s_v)
                for ti in range(2):
                    I(te, lambda e, ti=ti: e.transpose(
                        pT_f[0:32, 0:128], stat_tm[:, ti * 32:(ti + 1) * 32], ident_f[:]), (s_t, 1))
                    W(ve, s_t)
                    vop_(lambda e, ti=ti: e.tensor_copy(
                        stat_fm[0:32, ti * 128:(ti + 1) * 128], pT_f[0:32, 0:128]))
                    W(te, s_v)

                for m in range(16):
                    pp = pA if (m % 2 == 0) else pB
                    for kk in range(8):
                        I(te, lambda e, pp=pp, m=m, kk=kk: e.matmul(
                            pp[:, 0:OG], w1_sb[:, kk * 2048 + m * 128: kk * 2048 + m * 128 + 128],
                            xfm[:, kk * GRP: kk * GRP + OG],
                            start=(kk == 0), stop=False), (s_t, 1))
                    I(te, lambda e, pp=pp, m=m: e.matmul(
                        pp[:, 0:OG], w1_sb[0:32, 8 * 2048 + m * 128: 8 * 2048 + m * 128 + 128],
                        stat_fm[0:32, 0:OG], start=False, stop=True), (s_t, 1))
                    W(sc, s_t)
                    I(sc, lambda e, pp=pp, m=m: e.activation(
                        hmid[:, (m % 2) * GRP:(m % 2) * GRP + OG], pp[:, 0:OG],
                        AF.Gelu, bias=gb1_sb[:, m:m + 1]), (s_a, 1))
                    W(te, s_a)
                    I(te, lambda e, m=m: e.matmul(
                        pC_[0:16, 0:OG], w2_sb[:, m * 16:(m + 1) * 16],
                        hmid[:, (m % 2) * GRP:(m % 2) * GRP + OG],
                        start=(m == 0), stop=(m == 15)), (s_t, 1))
                W(ve, s_t)
                vop_(lambda e: e.tensor_scalar(logit_f[:, 0:OG], pC_[0:16, 0:OG],
                                               bb_sb[0:16, 0:1], None, ALU.add))
                W(te, s_v)
                for ti in range(2):
                    I(te, lambda e, ti=ti: e.transpose(
                        pT_f[0:128, 0:16], logit_f[:, ti * 128:(ti + 1) * 128],
                        ident_f[0:16, 0:16]), (s_t, 1))
                    W(ve, s_t)
                    vop_(lambda e, ti=ti: e.tensor_copy(logit_t[:, ti * 16:(ti + 1) * 16],
                                                        pT_f[0:128, 0:16]))
                    W(te, s_v)
                DR(ve)
                vop_(lambda e: e.tensor_tensor(sm_a[:, 0:32], logit_t[:, 0:32],
                                               itemp_sb[:, 0:32], ALU.mult))
                DR(ve)
                vop_(lambda e: e.tensor_reduce(sm_b[:, 0:8],
                                               sm_a[:, 0:32].rearrange("p (t q) -> p t q", q=4),
                                               AX.X, ALU.max))
                DR(ve)
                for th in range(8):
                    vop_(lambda e, th=th: e.tensor_scalar(
                        prob_t[:, th * 4:(th + 1) * 4], sm_a[:, th * 4:(th + 1) * 4],
                        sm_b[:, th:th + 1], None, ALU.subtract))
                W(sc, s_v)
                I(sc, lambda e: e.activation(sm_a[:, 32:64], prob_t[:, 0:32], AF.Tanh, scale=0.5), (s_a, 1))
                W(ve, s_a)
                vop_(lambda e: e.tensor_scalar(sm_b[:, 32:64], sm_a[:, 32:64], -1.0, 1.0,
                                               ALU.mult, ALU.add))
                DR(ve)
                vop_(lambda e: e.reciprocal(prob_t[:, 0:32], sm_b[:, 32:64]))
                DR(ve)
                vop_(lambda e: e.tensor_scalar(sm_b[:, 32:64], sm_a[:, 32:64], 1.0, None, ALU.add))
                DR(ve)
                vop_(lambda e: e.tensor_tensor(sm_a[:, 0:32], sm_b[:, 32:64], prob_t[:, 0:32], ALU.mult))
                DR(ve)
                vop_(lambda e: e.tensor_reduce(sm_b[:, 0:8],
                                               sm_a[:, 0:32].rearrange("p (t q) -> p t q", q=4),
                                               AX.X, ALU.add))
                DR(ve)
                vop_(lambda e: e.reciprocal(sm_b[:, 8:16], sm_b[:, 0:8]))
                DR(ve)
                for th in range(8):
                    vop_(lambda e, th=th: e.tensor_scalar(
                        prob_t[:, th * 4:(th + 1) * 4], sm_a[:, th * 4:(th + 1) * 4],
                        sm_b[:, 8 + th:9 + th], None, ALU.mult))
                DR(ve)
                vop_(lambda e: e.tensor_tensor(sm_a[:, 0:32], prob_t[:, 0:32],
                                               floor_sb[:, 0:32], ALU.max))
                DR(ve)
                vop_(lambda e: e.tensor_reduce(sm_b[:, 0:8],
                                               sm_a[:, 0:32].rearrange("p (t q) -> p t q", q=4),
                                               AX.X, ALU.add))
                DR(ve)
                vop_(lambda e: e.reciprocal(sm_b[:, 8:16], sm_b[:, 0:8]))
                DR(ve)
                for th in range(8):
                    vop_(lambda e, th=th: e.tensor_scalar(
                        prob_t[:, th * 4:(th + 1) * 4], sm_a[:, th * 4:(th + 1) * 4],
                        sm_b[:, 8 + th:9 + th], None, ALU.mult))
                DR(ve)
                if DEBUG and og == 0:
                    W(sy, s_v)
                    W(sy, s_dbg)
                    I(sy, lambda e: e.dma_start(out=dbg["probs"], in_=prob_t[:, 0:32]), (s_dbg, 16))
                    W(sy, s_dbg)
                    W(ve, s_dbg)
                    W(sc, s_dbg)
                    W(te, s_dbg)
                    W(gp, s_dbg)
                    W(sy, s_gd)
                    W(ve, s_gd)
                    W(sc, s_gd)
                    W(te, s_gd)
                    W(gp, s_gd)

                for ti in range(2):
                    tcol = ti * D
                    for hh in range(4):
                        vop_(lambda e, hh=hh, tcol=tcol, ti=ti: e.tensor_scalar(
                            comb_a[:, hh * DV:(hh + 1) * DV],
                            pth_s[:, tcol + hh * DV: tcol + (hh + 1) * DV],
                            prob_t[:, ti * 16 + hh * 4: ti * 16 + hh * 4 + 1], None, ALU.mult))
                    DR(ve)
                    for hh in range(4):
                        vop_(lambda e, hh=hh, tcol=tcol, ti=ti: e.scalar_tensor_tensor(
                            comb_b[:, hh * DV:(hh + 1) * DV],
                            pth_l[:, tcol + hh * DV: tcol + (hh + 1) * DV],
                            prob_t[:, ti * 16 + hh * 4 + 1: ti * 16 + hh * 4 + 2],
                            comb_a[:, hh * DV:(hh + 1) * DV], ALU.mult, ALU.add))
                    DR(ve)
                    for hh in range(4):
                        vop_(lambda e, hh=hh, tcol=tcol, ti=ti: e.scalar_tensor_tensor(
                            comb_a[:, hh * DV:(hh + 1) * DV],
                            pth_d[:, tcol + hh * DV: tcol + (hh + 1) * DV],
                            prob_t[:, ti * 16 + hh * 4 + 2: ti * 16 + hh * 4 + 3],
                            comb_b[:, hh * DV:(hh + 1) * DV], ALU.mult, ALU.add))
                    DR(ve)
                    for hh in range(4):
                        vop_(lambda e, hh=hh, tcol=tcol, ti=ti: e.scalar_tensor_tensor(
                            comb_b[:, hh * DV:(hh + 1) * DV],
                            pth_v[:, tcol + hh * DV: tcol + (hh + 1) * DV],
                            prob_t[:, ti * 16 + hh * 4 + 3: ti * 16 + hh * 4 + 4],
                            comb_a[:, hh * DV:(hh + 1) * DV], ALU.mult, ALU.add))
                    DR(ve)
                    vop_(lambda e: e.tensor_tensor(tstage[:, 0:D], comb_b[:, 0:D], comb_b[:, 0:D], ALU.mult))
                    DR(ve)
                    vop_(lambda e: e.tensor_reduce(ln_sc[:, 0:4],
                                                   tstage[:, 0:D].rearrange("p (h dv) -> p h dv", h=4),
                                                   AX.X, ALU.add))
                    DR(ve)
                    vop_(lambda e: e.tensor_scalar(rs_a[:, 0:4], ln_sc[:, 0:4], 1.0 / DV, 1e-5,
                                                   ALU.mult, ALU.add))
                    DR(ve)

                    newton4((0, 4), (0, 4))
                    WD(ve)
                    for hh in range(4):
                        vop_(lambda e, hh=hh: e.tensor_scalar(
                            orms[:, hh * DV:(hh + 1) * DV], comb_b[:, hh * DV:(hh + 1) * DV],
                            rs_b[:, hh:hh + 1], None, ALU.mult))
                    W(sc, s_v)
                    W(sc, s_t)
                    for kk in range(8):
                        dmat(ofm[:, kk * 128:(kk + 1) * 128], orms[:, kk * 128:(kk + 1) * 128])
                    WD(te)
                    for half in range(2):
                        pp = pA if half == 0 else pB
                        for kk in range(8):
                            I(te, lambda e, pp=pp, half=half, kk=kk: e.matmul(
                                pp[:, 0:512], ofm[:, kk * 128:(kk + 1) * 128],
                                wo_sb[:, kk * 1024 + half * 512: kk * 1024 + half * 512 + 512],
                                start=(kk == 0), stop=(kk == 7)), (s_t, 1))
                    W(sy, s_v)
                    ld(xt_f[:], x_own[XHALO + ot0 + ti * 128: XHALO + ot0 + (ti + 1) * 128, :])
                    W(ve, s_t)
                    WL(ve)
                    vop_(lambda e: e.tensor_tensor(y_t[:, 0:512], pA[:, 0:512], xt_f[:, 0:512], ALU.add))
                    vop_(lambda e: e.tensor_tensor(y_t[:, 512:1024], pB[:, 0:512],
                                                   xt_f[:, 512:1024], ALU.add))
                    W(te, s_v)
                    W(ve, s_a)
                    DR(ve)
                    vop_(lambda e: e.tensor_reduce(ln_sc[:, 0:1], y_t[:, 0:D], AX.X, ALU.add))
                    vop_(lambda e: e.tensor_tensor(tstage[:, 0:GRP], y_t[:, 0:GRP], y_t[:, 0:GRP], ALU.mult))
                    vop_(lambda e: e.tensor_tensor(tstage[:, GRP:2 * GRP], y_t[:, GRP:D],
                                                   y_t[:, GRP:D], ALU.mult))
                    DR(ve)
                    vop_(lambda e: e.tensor_reduce(ln_sc[:, 1:2], tstage[:, 0:2 * GRP], AX.X, ALU.add))
                    DR(ve)
                    vop_(lambda e: e.tensor_scalar(ln_sc[:, 2:3], ln_sc[:, 0:1], 1.0 / D, None, ALU.mult))
                    DR(ve)
                    vop_(lambda e: e.tensor_tensor(ln_sc[:, 3:4], ln_sc[:, 2:3], ln_sc[:, 2:3], ALU.mult))
                    vop_(lambda e: e.tensor_scalar(ln_sc[:, 4:5], ln_sc[:, 1:2], 1.0 / D, None, ALU.mult))
                    DR(ve)
                    vop_(lambda e: e.tensor_tensor(ln_sc[:, 5:6], ln_sc[:, 4:5], ln_sc[:, 3:4], ALU.subtract))
                    DR(ve)
                    vop_(lambda e: e.tensor_scalar(rs_a[:, 0:1], ln_sc[:, 5:6], 1.0, 1e-5, ALU.mult, ALU.add))
                    DR(ve)
                    newton4((0, 1), (0, 1), n=1)
                    vop_(lambda e: e.tensor_scalar(ln_sc[:, 10:11], ln_sc[:, 2:3], -1.0, None, ALU.mult))
                    DR(ve)
                    WL(ve)
                    vop_(lambda e: e.tensor_scalar(x1_t[:, 0:D], y_t[:, 0:D],
                                                   ln_sc[:, 10:11], rs_b[:, 0:1], ALU.add, ALU.mult))
                    DR(ve)
                    vop_(lambda e: e.tensor_reduce(ln_sc[:, 12:13], x1_t[:, 0:D], AX.X, ALU.max,
                                                   apply_absolute_value=True))
                    DR(ve)
                    vop_(lambda e: e.tensor_tensor(ln_sc[:, 13:14], amax_run[:, 0:1],
                                                   ln_sc[:, 12:13], ALU.max))
                    DR(ve)
                    vop_(lambda e: e.tensor_copy(amax_run[:, 0:1], ln_sc[:, 13:14]))
                    W(sy, s_v)
                    ld(x1_spill[ot0 + ti * 128: ot0 + (ti + 1) * 128, :], x1_t[:, 0:D])
                    if DEBUG and og == 0 and ti == 0:
                        W(sy, s_dbg)
                        I(sy, lambda e: e.dma_start(out=dbg["x1"], in_=x1_t[:]), (s_dbg, 16))
                        W(sy, s_dbg)
                        W(ve, s_dbg)
                        W(sc, s_dbg)
                        W(te, s_dbg)
                        W(gp, s_dbg)
                        W(sy, s_gd)
                        W(ve, s_gd)
                        W(sc, s_gd)
                        W(te, s_gd)
                        W(gp, s_gd)

            W(te, s_v)
            I(te, lambda e: e.transpose(pT_f[0:1, 0:128], amax_run[:, 0:1], ident_f[:]), (s_t, 1))
            W(ve, s_t)
            vop_(lambda e: e.tensor_reduce(sA[0:1, 40:41], pT_f[0:1, 0:128], AX.X, ALU.max))
            DR(ve)
            vop_(lambda e: e.tensor_scalar(sA[0:1, 43:44], sA[0:1, 40:41], 1e-12, None, ALU.max))
            DR(ve)
            vop_(lambda e: e.reciprocal(sA[0:1, 44:45], sA[0:1, 43:44]))
            DR(ve)
            vop_(lambda e: e.tensor_scalar(sA[0:1, 45:46], sA[0:1, 44:45], 448.0, 448.0,
                                           ALU.mult, ALU.min))
            DR(ve)
            vop_(lambda e: e.reciprocal(sA[0:1, 46:47], sA[0:1, 45:46]))
            DR(ve)
            vop_(lambda e: e.tensor_scalar(sA[0:1, 47:48], sA[0:1, 45:46], 0.5, None, ALU.mult))
            DR(ve)
            vop_(lambda e: e.tensor_tensor(sA[0:1, 48:49], sA[0:1, 46:47], sA[0:1, 7:8], ALU.mult))
            DR(ve)
            vop_(lambda e: e.tensor_scalar(sA[0:1, 49:50], sA[0:1, 48:49], 4.0, None, ALU.mult))
            DR(ve)
            W(sy, s_v)
            ld(loc_sc[0:1, 4:5], sA[0:1, 47:48])
            WL(sy)
            ld(loc_sc[0:1, 6:7], sA[0:1, 49:50])
            WL(sy)
            ld(sc_bc[:, 0:1], loc_sc[0:1, 4:5].broadcast_to((128, 1)))
            ld(sc_bc[:, 1:2], loc_sc[0:1, 6:7].broadcast_to((128, 1)))
            WL(ve)
            if DEBUG:
                W(sy, s_v)
                W(sy, s_dbg)
                I(sy, lambda e: e.dma_start(out=dbg["amax"], in_=amax_run[:]), (s_dbg, 16))

            W(gp, s_v)
            WD(gp)
            W(gp, s_gd)
            I(gp, lambda e: e.dma_start(out=fbias_sb[:],
                                        in_=fbias[None, :].broadcast_to((128, 1024))), (s_gd, 16))
            W(sy, s_t)
            W(sy, s_v)
            for kk in range(8):
                ld(wstage2[:, 0:1024], wfT[kk * 128:(kk + 1) * 128, :])
                WL(ve)
                W(ve, s_t)
                vop_(lambda e, kk=kk: e.tensor_scalar(
                    wf8_sb[:, kk * 1024:(kk + 1) * 1024], wstage2[:, 0:1024],
                    ws2_bc[:, 0:1], None, ALU.mult))
                W(sy, s_v)
            for tt in range(OWN // 128):
                r0 = tt * 128
                W(sy, s_v)
                ld(xt_f[:], x1_spill[r0:r0 + 128, :])
                WL(ve)
                WD(ve)
                vop_(lambda e: e.tensor_scalar(xs_b[:, 0:D], xt_f[:], sc_bc[:, 0:1], None, ALU.mult))
                W(sc, s_v)
                W(sc, s_t)
                for kk in range(8):
                    dmat(xs_fmb[:, kk * 128:(kk + 1) * 128], xs_b[:, kk * 128:(kk + 1) * 128])
                WD(ve)
                vop_(lambda e: e.tensor_copy(xs_fm8[:], xs_fmb[:]))
                W(te, s_v)
                for half in range(2):
                    pp = pA if half == 0 else pB
                    for kk in range(8):
                        I(te, lambda e, pp=pp, half=half, kk=kk: e.matmul(
                            pp[:, 0:512], xs_fm8[:, kk * 128:(kk + 1) * 128],
                            wf8_sb[:, kk * 1024 + half * 512: kk * 1024 + half * 512 + 512],
                            start=(kk == 0), stop=(kk == 7)), (s_t, 1))
                W(ve, s_t)
                W(ve, s_gd)
                vop_(lambda e: e.scalar_tensor_tensor(ffn_t[:, 0:512], pA[:, 0:512],
                                                      sc_bc[:, 1:2], fbias_sb[:, 0:512],
                                                      ALU.mult, ALU.add))
                vop_(lambda e: e.scalar_tensor_tensor(ffn_t[:, 512:1024], pB[:, 0:512],
                                                      sc_bc[:, 1:2], fbias_sb[:, 512:1024],
                                                      ALU.mult, ALU.add))
                W(te, s_v)
                DR(ve)
                vop_(lambda e: e.tensor_tensor(y_t[:, 0:D], xt_f[:], ffn_t[:, 0:D], ALU.add))
                DR(ve)
                vop_(lambda e: e.tensor_reduce(ln_sc[:, 0:1], y_t[:, 0:D], AX.X, ALU.add))
                vop_(lambda e: e.tensor_tensor(tstage[:, 0:GRP], y_t[:, 0:GRP], y_t[:, 0:GRP], ALU.mult))
                vop_(lambda e: e.tensor_tensor(tstage[:, GRP:2 * GRP], y_t[:, GRP:D],
                                               y_t[:, GRP:D], ALU.mult))
                DR(ve)
                vop_(lambda e: e.tensor_reduce(ln_sc[:, 1:2], tstage[:, 0:2 * GRP], AX.X, ALU.add))
                DR(ve)
                vop_(lambda e: e.tensor_scalar(ln_sc[:, 2:3], ln_sc[:, 0:1], 1.0 / D, None, ALU.mult))
                DR(ve)
                vop_(lambda e: e.tensor_tensor(ln_sc[:, 3:4], ln_sc[:, 2:3], ln_sc[:, 2:3], ALU.mult))
                vop_(lambda e: e.tensor_scalar(ln_sc[:, 4:5], ln_sc[:, 1:2], 1.0 / D, None, ALU.mult))
                DR(ve)
                vop_(lambda e: e.tensor_tensor(ln_sc[:, 5:6], ln_sc[:, 4:5], ln_sc[:, 3:4], ALU.subtract))
                DR(ve)
                vop_(lambda e: e.tensor_scalar(rs_a[:, 0:1], ln_sc[:, 5:6], 1.0, 1e-5, ALU.mult, ALU.add))
                DR(ve)
                newton4((0, 1), (0, 1), n=1)
                vop_(lambda e: e.tensor_scalar(ln_sc[:, 10:11], ln_sc[:, 2:3], -1.0, None, ALU.mult))
                DR(ve)
                W(ve, s_st)
                vop_(lambda e: e.tensor_scalar(out_t[:, 0:D], y_t[:, 0:D],
                                               ln_sc[:, 10:11], rs_b[:, 0:1], ALU.add, ALU.mult))
                W(sy, s_v)
                W(sy, s_st)
                I(sy, lambda e, r0=r0: e.dma_start(out=out_d[r0:r0 + 128, :], in_=out_t[:, 0:D]),
                  (s_st, 16))

            W(sy, s_st)
            WL(sy)
            WD(sy)
            W(sy, s_gd)
            W(sy, s_dbg)

        block = ctx.enter_context(nc.Block())

        @block.sync
        def _(sync):
            emit("sync", sync)

        @block.tensor
        def _(tensor):
            emit("tensor", tensor)

        @block.vector
        def _(vector):
            emit("vector", vector)

        @block.scalar
        def _(scalar):
            emit("scalar", scalar)

        @block.gpsimd
        def _(gpsimd):
            emit("gpsimd", gpsimd)

    return nc, out_d


_cache = {}


def kernel(**inputs):
    x = np.ascontiguousarray(inputs["L0_attn_hidden_states"], dtype=np.float32)
    Wq = np.asarray(inputs["L0_attn_Wq"], np.float32)
    Wk = np.asarray(inputs["L0_attn_Wk"], np.float32)
    Wv = np.asarray(inputs["L0_attn_Wv"], np.float32)
    Wb = np.asarray(inputs["L0_attn_Wb"], np.float32)
    qconv = np.asarray(inputs["L0_attn_qconv_w"], np.float32)
    kconv = np.asarray(inputs["L0_attn_kconv_w"], np.float32)
    vconv = np.asarray(inputs["L0_attn_vconv_w"], np.float32)
    fir_s = np.asarray(inputs["L0_attn_fir_short_w"], np.float32).reshape(D, 3)
    fir_l = np.asarray(inputs["L0_attn_fir_long_w"], np.float32).reshape(D, 63)
    g_w1 = np.asarray(inputs["L0_attn_gate_w1"], np.float32)
    g_b1 = np.asarray(inputs["L0_attn_gate_b1"], np.float32)
    g_w2 = np.asarray(inputs["L0_attn_gate_w2"], np.float32)
    log_temp = np.asarray(inputs["L0_attn_log_temp"], np.float32)
    base_bias = np.asarray(inputs["L0_attn_base_bias"], np.float32)
    floor_raw = np.asarray(inputs["L0_attn_floor_raw"], np.float32)
    onorm_w = np.asarray(inputs["L0_attn_onorm_w"], np.float32)
    Wo = np.asarray(inputs["L0_attn_Wo"], np.float32)
    Wf = np.asarray(inputs["L0_ffn_weight"], np.float32)
    fb = np.asarray(inputs["L0_ffn_bias"], np.float32)

    if "nc" not in _cache:
        _cache["nc"] = build_program()
    nc, out_d = _cache["nc"]

    def rbmajor(a):
        rb = a.shape[0] // 128
        return np.ascontiguousarray(a.reshape(rb, 128, -1).transpose(1, 0, 2).reshape(128, -1))

    in_maps = []
    for c in range(8):
        b, me = c // 2, c % 2
        xb_seq = x[b]
        xo = np.zeros((OWN + XHALO, D), np.float32)
        lo = me * OWN - XHALO
        src0 = max(lo, 0)
        xo[src0 - lo:] = xb_seq[src0: me * OWN + OWN]
        in_maps.append({
            "x_seq": xb_seq,
            "x_own": xo,
            "wqT": np.ascontiguousarray(Wq.T),
            "wkT": np.ascontiguousarray(Wk.T),
            "wbT": np.ascontiguousarray(Wb.T),
            "wvT": np.ascontiguousarray(Wv.T),
            "w1T": np.ascontiguousarray(g_w1.T),
            "w2T": np.ascontiguousarray(g_w2.T),
            "woT": np.ascontiguousarray(Wo.T),
            "wfT": np.ascontiguousarray(Wf.T),
            "qcw": rbmajor(qconv),
            "kcw": rbmajor(kconv),
            "vcw": rbmajor(vconv),
            "firs": rbmajor(fir_s),
            "firl": rbmajor(fir_l),
            "gb1": rbmajor(g_b1.reshape(2048, 1)),
            "bb": np.ascontiguousarray(base_bias.reshape(16, 1)),
            "ltemp": log_temp.reshape(1, 4),
            "fraw": floor_raw.reshape(1, 16),
            "onw4": rbmajor(np.tile(onorm_w, H).reshape(D, 1)),
            "fbias": fb,
        })

    res = run_bass_kernel_spmd(nc, in_maps, core_ids=list(range(8)))
    out = np.empty((B, L, D), np.float32)
    for c in range(8):
        b, me = c // 2, c % 2
        out[b, me * OWN:(me + 1) * OWN] = res.results[c]["out"]
    return out
